# version 03
# baseline (speedup 1.0000x reference)
import numpy as np
import concourse.bass as bass
import concourse.mybir as mybir


F32 = mybir.dt.float32
BF16 = mybir.dt.bfloat16
U8 = mybir.dt.uint8
AF = mybir.ActivationFunctionType
OP = mybir.AluOpType
AX = mybir.AxisListType

ENGS = ("pe", "act", "dve", "pool", "sp")


class Buf:
    __slots__ = ("name", "ap_", "writers", "readers", "handle", "chkey")

    def __init__(self, name, handle):
        self.name = name
        self.handle = handle
        self.ap_ = None
        self.writers = []
        self.readers = []
        self.chkey = None

    def ap(self):
        if self.ap_ is None:
            a = getattr(self.handle, "ap", None)
            self.ap_ = self.handle.ap() if callable(a) else self.handle
        return self.ap_

    def __getitem__(self, idx):
        return self.ap()[idx]


class K:
    def __init__(self, nc: bass.Bass):
        self.nc = nc
        self.eng = {
            "pe": nc.tensor, "act": nc.scalar, "dve": nc.vector,
            "pool": nc.gpsimd, "sp": nc.sync,
        }
        self._ctxs = []
        self.sem = {}
        self.cnt = {}
        for e in ENGS:
            for kind in ("C", "D"):
                key = e + kind
                sem = nc.semaphore(f"s_{key}").__enter__()
                self.sem[key] = sem
                self.cnt[key] = 0
        self.idx = {e: 0 for e in ENGS}
        self.accum_last = {}
        self.waited = {e: {} for e in ENGS}
        self.ops = {e: [] for e in ENGS}
        self._cc_sems = []
        self.stats = {"waits": 0, "instrs": 0}

    def sbuf(self, name, shape, dtype):
        g = self.nc.sbuf_tensor(name, list(shape), dtype)
        h = g.__enter__(); self._ctxs.append(g)
        return Buf(name, h)

    def psum(self, name, shape, dtype=F32):
        g = self.nc.psum_tensor(name, list(shape), dtype)
        h = g.__enter__(); self._ctxs.append(g)
        return Buf(name, h)

    def dram_in(self, name, shape, dtype=F32):
        return Buf(name, self.nc.dram_tensor(name, list(shape), dtype, kind="ExternalInput").ap())

    def dram_out(self, name, shape, dtype=F32):
        return Buf(name, self.nc.dram_tensor(name, list(shape), dtype, kind="ExternalOutput").ap())

    def dram_tmp(self, name, shape, dtype=F32):
        return Buf(name, self.nc.dram_tensor(name, list(shape), dtype).ap())

    def alias(self, name, arena, shape, dtype, offset_bytes=0):
        import concourse.mybir as _mb
        h = arena.handle
        esz = _mb.dt.size(dtype)
        assert offset_bytes % esz == 0
        hb = h.bitcast(dtype)
        start = offset_bytes // esz
        n = 1
        for s in shape[1:]:
            n *= s
        apv = hb[:shape[0], start:start + n]
        if len(shape) > 2:
            pat = "p (" + " ".join(f"d{i}" for i in range(1, len(shape))) + ") -> p " +                   " ".join(f"d{i}" for i in range(1, len(shape)))
            kw = {f"d{i}": shape[i] for i in range(1, len(shape))}
            apv = apv.rearrange(pat, **kw)
        b = Buf(name, arena.handle)
        b.ap_ = apv
        b.writers = list(arena.writers)
        b.readers = list(arena.readers)
        return b

    def merge_tokens(self, arena, subbufs):
        ws, rs = list(arena.writers), list(arena.readers)
        for b in subbufs:
            ws.extend(b.writers)
            rs.extend(b.readers)
        arena.writers = ws
        arena.readers = rs

    def new_sem(self, name):
        sem = self.nc.semaphore(name).__enter__()
        self._cc_sems.append(sem)
        return sem

    def _wait(self, e, tok, allow_self=True, cur_dma=False):
        key, val = tok
        if key == e + "C" and e == "pe":
            return
        if key == e + "C" and not allow_self:
            return
        w = self.waited[e]
        if w.get(key, 0) >= val:
            return
        w[key] = val
        sem = self.sem[key]
        self.ops[e].append(lambda eng, sem=sem, val=val: eng.wait_ge(sem, val))
        self.stats["waits"] += 1

    def _channel(self, buf):
        if buf.chkey is None:
            key = "ch_" + buf.name
            sem = self.nc.semaphore(key).__enter__()
            self._cc_sems.append(sem)
            self.sem[key] = sem
            self.cnt[key] = 0
            buf.chkey = key
        return buf.chkey

    def emit(self, e, fn, reads=(), writes=(), inc=1, dma=False, accum=(), relax_reads=False):
        for r in reads:
            al = self.accum_last.get(id(r))
            if al is not None and al[0] == e and al[1] == self.idx[e] - 1:
                raise RuntimeError(f"accum_out hazard: {r.name} read by next same-engine instr on {e}")
        for r in reads:
            for wt in r.writers:
                self._wait(e, wt, allow_self=not relax_reads)
        for wbuf in writes:
            for wt in wbuf.writers:
                self._wait(e, wt)
            for rd in wbuf.readers:
                self._wait(e, rd)
        if dma:
            assert inc, "every DMA must increment its channel semaphore"
            assert len(writes) == 1, "DMA must have exactly one destination buffer"
            key = self._channel(writes[0])
            self.cnt[key] += 16
            tok = (key, self.cnt[key])
            sem = self.sem[key]
            self.ops[e].append(lambda eng, fn=fn, sem=sem: fn(eng).then_inc(sem, 16))
        elif inc:
            key = e + "C"
            self.cnt[key] += 1
            tok = (key, self.cnt[key])
            sem = self.sem[key]
            self.ops[e].append(lambda eng, fn=fn, sem=sem: fn(eng).then_inc(sem, 1))
        else:
            key = e + "C"
            self.ops[e].append(lambda eng, fn=fn: fn(eng))
            tok = (key, self.cnt[key] + 1)
        self.stats["instrs"] += 1
        self.idx[e] += 1
        for r in reads:
            r.readers.append(tok)
        accum_ids = {id(a) for a in accum}
        for wbuf in writes:
            wbuf.writers = [tok]
            wbuf.readers = []
            if id(wbuf) not in accum_ids:
                self.accum_last.pop(id(wbuf), None)
        for abuf in accum:
            self.accum_last[id(abuf)] = (e, self.idx[e] - 1)
        return tok

    def dma(self, e, dst_buf, dst_ap, src_buf, src_ap, inc=1, **kw):
        return self.emit(
            e, lambda eng: eng.dma_start(out=dst_ap, in_=src_ap, **kw),
            reads=[src_buf], writes=[dst_buf], inc=inc, dma=True)

    def raw_wait(self, e, tok):
        self._wait(e, tok)

    def barrier_token(self, e, kind="C"):
        return (e + kind, self.cnt[e + kind])

    def out_token(self, buf):
        assert buf.chkey is not None, f"{buf.name} has no DMA channel"
        return (buf.chkey, self.cnt[buf.chkey])

    def finalize(self, block, final_waits=()):
        for tok in final_waits:
            self._wait("pool", tok)
        blockmap = {"pe": block.tensor, "act": block.scalar, "dve": block.vector,
                    "pool": block.gpsimd, "sp": block.sync}
        for e in ENGS:
            ops = self.ops[e]
            def make(ops=ops):
                def body(eng):
                    for f in ops:
                        f(eng)
                return body
            blockmap[e](make())

    def close(self):
        for h in reversed(self._ctxs):
            h.__exit__(None, None, None)



def bcast_row(k: K, dst, src_row_ap, src_buf, n, ones_stat, psum_bank, dtype_note=""):
    raise NotImplementedError


def ceil_div(a, b):
    return (a + b - 1) // b



D = 4096
NTOK = 1024
NT = NTOK // 128
NE = D // 128
ND = D // 512
HALF = NT // 2


class Env:

    def __init__(self, k: K):
        self.k = k
        self.idn = self._identity("idn", BF16)
        self.idnf = self._identity("idnf", F32)
        self.actT = k.sbuf("actT", [128, NE * NTOK], BF16)
        self.wsb = k.sbuf("wsb", [128, NE * 512], BF16)
        self.xpre = [k.sbuf(f"xpre{i}", [128, D], F32) for i in range(HALF)]
        self.xres = [k.sbuf(f"xres{i}", [128, 512], F32) for i in range(4)]
        self.sums = [k.sbuf(f"sums{i}", [128, ND], F32) for i in range(HALF)]
        self.sqs = [k.sbuf(f"sqs{i}", [128, ND], F32) for i in range(HALF)]
        self.sqjunk = k.sbuf("sqjunk", [128, 512], F32)
        self.mean = k.sbuf("mean", [128, 1], F32)
        self.rstd = k.sbuf("rstd", [128, 1], F32)
        self.lntmp = k.sbuf("lntmp", [128, 1], F32)
        self.junk8 = k.sbuf("junk8", [128, ND], F32)
        self.tstage = k.sbuf("tstage", [128, D], BF16)
        self.ones_col = k.sbuf("ones_col", [128, 128], BF16)
        self.brow = k.sbuf("brow", [128, D], BF16)
        self.arena = k.sbuf("arena", [128, 8 * 1024], BF16)
        self.pmm = [k.psum(f"pmm{i}", [128, 512], F32) for i in range(2)]
        self.ptrf = k.psum("ptrf", [128, 512], F32)
        self.ptrb = k.psum("ptrb", [128, 1024], BF16)
        self.pl = [k.psum(f"pl{i}", [128, 512], F32) for i in range(2)]
        k.emit("pool", lambda e: e.memset(self.ones_col[:], 0.0), writes=[self.ones_col])
        k.emit("pool", lambda e: e.memset(self.ones_col[0:1, :], 1.0), writes=[self.ones_col])

    def _identity(self, name, dtype):
        k = self.k
        idn = k.sbuf(name, [128, 128], dtype)
        k.emit("pool", lambda e: e.memset(idn[:], 0.0), writes=[idn])
        k.emit("pool", lambda e: e.affine_select(out=idn[:], in_=idn[:], compare_op=OP.not_equal,
                                                 fill=1.0, base=0, pattern=[[-1, 128]], channel_multiplier=1),
               reads=[idn], writes=[idn])
        return idn

    def mm_group(self, psum_ap, psum_buf, terms):
        k = self.k
        n = len(terms)
        tok = None
        for i, (stat, mov, rbufs) in enumerate(terms):
            tok = k.emit("pe",
                         lambda e, stat=stat, mov=mov, i=i, n=n: e.matmul(
                             psum_ap, stat, mov, start=(i == 0), stop=(i == n - 1)),
                         reads=list(rbufs), writes=[psum_buf],
                         inc=1 if i == n - 1 else 0)
        return tok

    def ln_stats(self, sums, sqs, dinv, eps):
        k = self.k
        mean, rstd, tmp, junk8 = self.mean, self.rstd, self.lntmp, self.junk8
        k.emit("act", lambda e: e.activation(out=junk8[:], in_=sums[:], func=AF.Identity, accum_out=mean[:]),
               reads=[sums], writes=[junk8, mean], accum=[mean])
        k.emit("dve", lambda e: e.tensor_reduce(out=tmp[:], in_=sqs[:], axis=AX.X, op=OP.add),
               reads=[sqs], writes=[tmp])
        k.emit("pool", lambda e: e.tensor_scalar(out=mean[:], in0=mean[:], scalar1=dinv, scalar2=None,
                                                 op0=OP.mult, op1=OP.bypass), reads=[mean], writes=[mean])
        k.emit("pool", lambda e: e.tensor_scalar(out=tmp[:], in0=tmp[:], scalar1=dinv, scalar2=None,
                                                 op0=OP.mult, op1=OP.bypass), reads=[tmp], writes=[tmp])
        k.emit("pool", lambda e: e.tensor_tensor(out=rstd[:], in0=mean[:], in1=mean[:], op=OP.mult),
               reads=[mean], writes=[rstd])
        k.emit("pool", lambda e: e.tensor_tensor(out=tmp[:], in0=tmp[:], in1=rstd[:], op=OP.subtract),
               reads=[tmp, rstd], writes=[tmp])
        k.emit("pool", lambda e: e.tensor_scalar(out=tmp[:], in0=tmp[:], scalar1=float(eps), scalar2=None,
                                                 op0=OP.add, op1=OP.bypass), reads=[tmp], writes=[tmp])
        k.emit("act", lambda e: e.activation(out=rstd[:], in_=tmp[:], func=AF.Sqrt),
               reads=[tmp], writes=[rstd])
        k.emit("dve", lambda e: e.reciprocal(out=rstd[:], in_=rstd[:]), reads=[rstd], writes=[rstd])

    def load_actT(self, src_hbm):
        k = self.k
        for q in range(4):
            k.dma("sp", self.actT,
                  self.actT.ap().rearrange("p (c t) -> p c t", t=NTOK)[:, q * 8:(q + 1) * 8, :],
                  src_hbm, src_hbm.ap().rearrange("(c p) t -> p c t", p=128)[:, q * 8:(q + 1) * 8, :])

    def transpose_out(self, src_tile_buf, src_ap_f32, gt, xT_out):
        k = self.k
        for eg in range(NE // 4):
            for j in range(4):
                et = eg * 4 + j
                k.emit("pe", lambda e, et=et, j=j: e.transpose(
                    self.ptrf[:, j * 128:(j + 1) * 128], src_ap_f32[:, et * 128:(et + 1) * 128], self.idnf[:]),
                    reads=[src_tile_buf, self.idnf], writes=[self.ptrf], inc=1 if j == 3 else 0)
            k.emit("dve", lambda e, eg=eg: e.tensor_copy(
                out=self.tstage[:, eg * 512:(eg + 1) * 512], in_=self.ptrf[:, :]),
                reads=[self.ptrf], writes=[self.tstage])
        k.dma("sp", xT_out,
              xT_out.ap().rearrange("(c p) t -> p c t", p=128)[:, :, gt * 128:(gt + 1) * 128],
              self.tstage, self.tstage.ap().rearrange("p (c t) -> p c t", t=128))

    def load_weight_chunk(self, W, d):
        k = self.k
        for q in range(4):
            k.dma("pool", self.wsb,
                  self.wsb.ap().rearrange("p (c o) -> p c o", o=512)[:, q * 8:(q + 1) * 8, :],
                  W, W.ap().rearrange("(c p) o -> p c o", p=128)[:, q * 8:(q + 1) * 8, d * 512:(d + 1) * 512])

    def load_bias_row(self, bias_hbm):
        self.k.dma("pool", self.brow, self.brow[0:1, :], bias_hbm, bias_hbm[0:1, :])

    def big_linear(self, W, res_rows, x_out_rows, xT_out, add_bias=False, extra_terms=None,
                   ln_eps=1e-5, epilogue=None):
        k = self.k
        for g in range(2):
            for d in range(ND):
                self.load_weight_chunk(W, d)
                for t in range(HALF):
                    gt = g * HALF + t
                    pm = self.pmm[t % 2]
                    terms = [(self.actT[:, e * NTOK + gt * 128: e * NTOK + gt * 128 + 128],
                              self.wsb[:, e * 512:(e + 1) * 512], [self.actT, self.wsb]) for e in range(NE)]
                    if add_bias:
                        terms.append((self.ones_col[0:1, :], self.brow[0:1, d * 512:(d + 1) * 512],
                                      [self.ones_col, self.brow]))
                    if extra_terms is not None:
                        terms.extend(extra_terms(gt, d))
                    self.mm_group(pm[:, :], pm, terms)
                    xr = self.xres[t]
                    k.dma("sp", xr, xr[:, :], res_rows, res_rows[gt * 128:(gt + 1) * 128, d * 512:(d + 1) * 512])
                    k.emit("dve", lambda e, t=t, d=d, pm=pm, xr=xr: e.scalar_tensor_tensor(
                        out=self.xpre[t][:, d * 512:(d + 1) * 512], in0=pm[:, :], scalar=1.0, in1=xr[:, :],
                        op0=OP.mult, op1=OP.add, accum_out=self.sums[t][:, d:d + 1]),
                        reads=[pm, xr], writes=[self.xpre[t], self.sums[t]], accum=[self.sums[t]])
                    k.emit("act", lambda e, t=t, d=d: e.activation(
                        out=self.sqjunk[:], in_=self.xpre[t][:, d * 512:(d + 1) * 512], func=AF.Square,
                        accum_out=self.sqs[t][:, d:d + 1]),
                        reads=[self.xpre[t]], writes=[self.sqjunk, self.sqs[t]], accum=[self.sqs[t]])
            for t in range(HALF):
                gt = g * HALF + t
                self.ln_stats(self.sums[t], self.sqs[t], 1.0 / D, ln_eps)
                k.emit("dve", lambda e, t=t: e.tensor_scalar(
                    out=self.xpre[t][:, :], in0=self.xpre[t][:, :], scalar1=self.mean[:], scalar2=self.rstd[:],
                    op0=OP.subtract, op1=OP.mult),
                    reads=[self.xpre[t], self.mean, self.rstd], writes=[self.xpre[t]])
                k.dma("sp", x_out_rows, x_out_rows[gt * 128:(gt + 1) * 128, :], self.xpre[t], self.xpre[t][:, :])
                if xT_out is not None:
                    self.transpose_out(self.xpre[t], self.xpre[t].ap(), gt, xT_out)



def emit_stageB(env: Env, io):
    k = env.k
    aoTc, xrows = io["aoTc"], io["xrows"]
    WoT, kernW, fbias = io["WoT"], io["kernW"], io["fbias"]
    LA, LBT, SelAB, AshT, BshT = io["LA"], io["LBT"], io["SelAB"], io["AshT"], io["BshT"]
    x2rows, x2T, x1sc, x1Tsc = io["x2rows"], io["x2T"], io["x1sc"], io["x1Tsc"]

    AeT = k.alias("AeT", env.arena, [128, NE * 32], BF16, 0)
    AshT_sb = k.alias("AshT_sb", env.arena, [128, NE * 32], BF16, 2 * 1024)
    BeT = k.alias("BeT", env.arena, [128, D], BF16, 4 * 1024)
    AxT = k.alias("AxT", env.arena, [128, NTOK], BF16, 12 * 1024)
    sel_sb = k.alias("sel_sb", env.arena, [128, 4 * 32], BF16, 14 * 1024)

    for i in range(4):
        k.dma("pool", env.wsb, env.wsb[:, i * D:(i + 1) * D], LA, LA[i * 128:(i + 1) * 128, :])
    k.dma("pool", sel_sb, sel_sb.ap().rearrange("p (c r) -> p c r", r=32),
          SelAB, SelAB.ap().rearrange("(c p) r -> p c r", p=128))
    k.dma("pool", AshT_sb, AshT_sb.ap().rearrange("p (c r) -> p c r", r=32),
          AshT, AshT.ap().rearrange("(c p) r -> p c r", p=128))
    k.dma("pool", BeT, BeT[32:64, :], BshT, BshT[:, :])
    for et in range(NE):
        terms = [(env.wsb[:, c * D + et * 128: c * D + (et + 1) * 128], sel_sb[:, c * 32:(c + 1) * 32],
                  [env.wsb, sel_sb]) for c in range(4)]
        env.mm_group(env.pl[0][:, 0:32], env.pl[0], terms)
        k.emit("dve", lambda e, et=et: e.tensor_copy(out=AeT[:, et * 32:(et + 1) * 32], in_=env.pl[0][:, 0:32]),
               reads=[env.pl[0]], writes=[AeT])
    for i in range(4):
        k.dma("pool", env.wsb, env.wsb[:, i * D:(i + 1) * D], LBT, LBT[i * 128:(i + 1) * 128, :])
    for oc in range(ND):
        terms = [(sel_sb[:, c * 32:(c + 1) * 32], env.wsb[:, c * D + oc * 512: c * D + (oc + 1) * 512],
                  [env.wsb, sel_sb]) for c in range(4)]
        env.mm_group(env.pl[1][0:32, :], env.pl[1], terms)
        k.emit("dve", lambda e, oc=oc: e.tensor_copy(out=BeT[0:32, oc * 512:(oc + 1) * 512], in_=env.pl[1][0:32, :]),
               reads=[env.pl[1]], writes=[BeT])

    env.load_actT(aoTc)
    env.big_linear(WoT, xrows, x1sc, x1Tsc, add_bias=False, ln_eps=1e-5)

    env.load_actT(x1Tsc)
    for h in range(NTOK // 512):
        terms = [(AeT[:, e * 32:(e + 1) * 32], env.actT[:, e * NTOK + h * 512: e * NTOK + h * 512 + 512],
                  [AeT, env.actT]) for e in range(NE)]
        env.mm_group(env.pl[0][0:32, :], env.pl[0], terms)
        k.emit("dve", lambda e, h=h: e.tensor_copy(out=AxT[0:32, h * 512:(h + 1) * 512], in_=env.pl[0][0:32, :]),
               reads=[env.pl[0]], writes=[AxT])
        terms = [(AshT_sb[:, e * 32:(e + 1) * 32], env.actT[:, e * NTOK + h * 512: e * NTOK + h * 512 + 512],
                  [AshT_sb, env.actT]) for e in range(NE)]
        env.mm_group(env.pl[1][0:32, :], env.pl[1], terms)
        k.emit("dve", lambda e, h=h: e.tensor_copy(out=AxT[32:64, h * 512:(h + 1) * 512], in_=env.pl[1][0:32, :]),
               reads=[env.pl[1]], writes=[AxT])

    env.load_bias_row(fbias)
    def lora_terms(gt, d):
        return [(AxT[0:64, gt * 128:(gt + 1) * 128], BeT[0:64, d * 512:(d + 1) * 512], [AxT, BeT])]
    env.big_linear(kernW, x1sc, x2rows, x2T, add_bias=True, extra_terms=lora_terms, ln_eps=1e-5)
    k.merge_tokens(env.arena, [AeT, AshT_sb, BeT, AxT, sel_sb])
    return [k.out_token(x2rows), k.out_token(x2T)]



def emit_stageD(env: Env, io):
    k = env.k
    attn_rows = io["attn_rows"]
    x2rows, x2T = io["x2rows"], io["x2T"]
    nwcol = io["nwcol"]
    gate_wT, out_wT, W_baseT = io["gate_wT"], io["out_wT"], io["W_baseT"]
    b_base = io["b_base"]
    encT = io["encT"]
    CW0, MW0 = io["CW0"], io["MW0"]
    CW1, MW1 = io["CW1"], io["MW1"]
    Kmat = io["Kmat"]
    outrows = io["outrows"]
    x3sc, x3Tsc, y0Tsc, Hsc = io["x3sc"], io["x3Tsc"], io["y0Tsc"], io["Hsc"]

    Wsum = k.alias("Wsum", env.arena, [128, NE * 40], BF16, 0)
    c0f = k.alias("c0f", env.arena, [128, 40], F32, 4096)
    m0f = k.alias("m0f", env.arena, [128, 40], F32, 4608)
    acc0 = k.alias("acc0", env.arena, [128, 40], F32, 5120)
    Hst = k.alias("Hst", env.arena, [128, 512], BF16, 5632)
    nwsb = k.sbuf("nwsb", [128, 32], F32)
    k.dma("sp", nwsb, nwsb[:, :], nwcol, nwcol[:, :])

    for et in range(NE):
        for eidx in range(3):
            k.dma("pool", c0f, c0f[:, 0:32], CW0, CW0[eidx * D + et * 128: eidx * D + (et + 1) * 128, :])
            k.dma("pool", c0f, c0f[:, 32:40], CW1, CW1[eidx * D + et * 128: eidx * D + (et + 1) * 128, :])
            k.dma("pool", m0f, m0f[:, 0:32], MW0, MW0[eidx * D + et * 128: eidx * D + (et + 1) * 128, :])
            k.dma("pool", m0f, m0f[:, 32:40], MW1, MW1[eidx * D + et * 128: eidx * D + (et + 1) * 128, :])
            if eidx == 0:
                k.emit("dve", lambda e: e.tensor_tensor(out=acc0[:, :], in0=c0f[:, :], in1=m0f[:, :], op=OP.mult),
                       reads=[c0f, m0f], writes=[acc0])
            else:
                k.emit("dve", lambda e: e.tensor_tensor(out=c0f[:, :], in0=c0f[:, :], in1=m0f[:, :], op=OP.mult),
                       reads=[c0f, m0f], writes=[c0f])
                k.emit("dve", lambda e: e.tensor_tensor(out=acc0[:, :], in0=acc0[:, :], in1=c0f[:, :], op=OP.add),
                       reads=[acc0, c0f], writes=[acc0])
        k.emit("dve", lambda e, et=et: e.tensor_copy(out=Wsum[:, et * 40:(et + 1) * 40], in_=acc0[:, :]),
               reads=[acc0], writes=[Wsum])

    for d in range(ND):
        env.load_weight_chunk(Kmat, d)
        terms = [(Wsum[:, e * 40:(e + 1) * 40], env.wsb[:, e * 512:(e + 1) * 512], [Wsum, env.wsb])
                 for e in range(NE)]
        env.mm_group(env.pl[0][0:40, :], env.pl[0], terms)
        k.emit("dve", lambda e: e.tensor_scalar(out=Hst[0:40, :], in0=env.pl[0][0:40, :],
                                                scalar1=0.5, scalar2=None, op0=OP.mult, op1=OP.bypass),
               reads=[env.pl[0]], writes=[Hst])
        k.dma("sp", Hsc, Hsc[:, d * 512:(d + 1) * 512], Hst, Hst[0:40, :])

    k.merge_tokens(env.arena, [Wsum, c0f, m0f, acc0, Hst])
    gateb = [k.alias(f"gateb{i}", env.arena, [128, D], BF16, i * 8 * 1024) for i in range(2)]
    env.load_actT(x2T)
    for g in range(2):
        for tp in range(2):
            for d in range(ND):
                env.load_weight_chunk(gate_wT, d)
                for u in range(2):
                    t = tp * 2 + u
                    gt = g * HALF + t
                    pm = env.pmm[u]
                    terms = [(env.actT[:, e * NTOK + gt * 128: e * NTOK + gt * 128 + 128],
                              env.wsb[:, e * 512:(e + 1) * 512], [env.actT, env.wsb]) for e in range(NE)]
                    env.mm_group(pm[:, :], pm, terms)
                    k.emit("act", lambda e, u=u, d=d, pm=pm: e.activation(
                        out=gateb[u][:, d * 512:(d + 1) * 512], in_=pm[:, :], func=AF.Sigmoid),
                        reads=[pm], writes=[gateb[u]])
            for u in range(2):
                t = tp * 2 + u
                gt = g * HALF + t
                xp = env.xpre[u]
                k.dma("sp", xp, xp[:, :], attn_rows, attn_rows[gt * 128:(gt + 1) * 128, :])
                for dch in range(ND):
                    k.emit("act", lambda e, xp=xp, dch=dch, u=u: e.activation(
                        out=env.sqjunk[:], in_=xp[:, dch * 512:(dch + 1) * 512], func=AF.Square,
                        accum_out=env.sqs[u][:, dch:dch + 1]),
                        reads=[xp], writes=[env.sqjunk, env.sqs[u]], accum=[env.sqs[u]])
                k.emit("dve", lambda e, u=u: e.tensor_reduce(out=env.lntmp[:], in_=env.sqs[u][:], axis=AX.X, op=OP.add),
                       reads=[env.sqs[u]], writes=[env.lntmp])
                k.emit("pool", lambda e: e.tensor_scalar(out=env.lntmp[:], in0=env.lntmp[:], scalar1=1.0 / D,
                                                         scalar2=1e-5, op0=OP.mult, op1=OP.add),
                       reads=[env.lntmp], writes=[env.lntmp])
                k.emit("act", lambda e: e.activation(out=env.rstd[:], in_=env.lntmp[:], func=AF.Sqrt),
                       reads=[env.lntmp], writes=[env.rstd])
                k.emit("dve", lambda e: e.reciprocal(out=env.rstd[:], in_=env.rstd[:]), reads=[env.rstd], writes=[env.rstd])
                k.emit("dve", lambda e, xp=xp: e.tensor_scalar(out=env.tstage[:], in0=xp[:, :], scalar1=env.rstd[:],
                                                               scalar2=None, op0=OP.mult, op1=OP.bypass),
                       reads=[xp, env.rstd], writes=[env.tstage])
                k.emit("dve", lambda e, u=u: e.tensor_tensor(out=env.tstage[:], in0=env.tstage[:], in1=gateb[u][:, :],
                                                             op=OP.mult),
                       reads=[env.tstage, gateb[u]], writes=[env.tstage])
                for eg in range(NE // 8):
                    for j in range(8):
                        et = eg * 8 + j
                        k.emit("pe", lambda e, et=et, j=j: e.transpose(
                            env.ptrb[:, j * 128:(j + 1) * 128], env.tstage[:, et * 128:(et + 1) * 128], env.idn[:]),
                            reads=[env.tstage, env.idn], writes=[env.ptrb], inc=1 if j == 7 else 0)
                    for j in range(8):
                        et = eg * 8 + j
                        k.emit("dve", lambda e, et=et, j=j: e.tensor_scalar(
                            out=env.brow[:, et * 128:(et + 1) * 128], in0=env.ptrb[:, j * 128:(j + 1) * 128],
                            scalar1=nwsb[:, et:et + 1], scalar2=None, op0=OP.mult, op1=OP.bypass),
                            reads=[env.ptrb, nwsb], writes=[env.brow])
                k.dma("sp", y0Tsc,
                      y0Tsc.ap().rearrange("(c p) t -> p c t", p=128)[:, :, gt * 128:(gt + 1) * 128],
                      env.brow, env.brow.ap().rearrange("p (c t) -> p c t", t=128))

    env.load_actT(y0Tsc)
    env.big_linear(out_wT, x2rows, x3sc, x3Tsc, add_bias=False, ln_eps=1e-5)

    env.load_actT(x3Tsc)
    k.merge_tokens(env.arena, [gateb[0], gateb[1]])
    eCT = k.alias("eCT", env.arena, [128, NTOK], BF16, 0)
    encsb = k.alias("encsb", env.arena, [128, NE * 40], BF16, 2048)
    Hre = k.alias("Hre", env.arena, [128, D], BF16, 8 * 1024)
    k.dma("pool", encsb, encsb.ap().rearrange("p (c r) -> p c r", r=40),
          encT, encT.ap().rearrange("(c p) r -> p c r", p=128))
    k.dma("sp", Hre, Hre[0:40, :], Hsc, Hsc[:, :])
    for h in range(NTOK // 512):
        terms = [(encsb[:, e * 40:(e + 1) * 40], env.actT[:, e * NTOK + h * 512: e * NTOK + h * 512 + 512],
                  [encsb, env.actT]) for e in range(NE)]
        env.mm_group(env.pl[0][0:40, :], env.pl[0], terms)
        k.emit("dve", lambda e, h=h: e.tensor_copy(out=eCT[0:40, h * 512:(h + 1) * 512], in_=env.pl[0][0:40, :]),
               reads=[env.pl[0]], writes=[eCT])

    env.load_bias_row(b_base)
    def fourier_terms(gt, d):
        return [(eCT[0:40, gt * 128:(gt + 1) * 128], Hre[0:40, d * 512:(d + 1) * 512], [eCT, Hre])]
    env.big_linear(W_baseT, x3sc, outrows, None, add_bias=True, extra_terms=fourier_terms, ln_eps=1e-5)
    k.merge_tokens(env.arena, [eCT, encsb, Hre])
    return [k.out_token(outrows)]



BS = 256
NBLK = NTOK // BS

def emit_stageC1(env: Env, io, decays):
    k = env.k
    x2T, qkv_wT = io["x2T"], io["qkv_wT"]
    ddtab, qdtab, kdtab = io["ddtab"], io["qdtab"], io["kdtab"]
    attnp, kvloc, cqTs = io["attnp"], io["kvloc"], io["cqTs"]
    block_decay = decays["bd"]

    qkv_sb = k.alias("qkv_sb", env.arena, [128, 8 * 768], BF16, 0)
    scT = k.alias("scT", env.arena, [128, 512], BF16, 12 * 1024)
    ddsb = k.alias("ddsb", env.arena, [128, 512], BF16, 13 * 1024)
    cqT_sb = k.alias("cqT_sb", env.arena, [128, 256], BF16, 14 * 1024)
    ckT_sb = k.alias("ckT_sb", env.arena, [128, 256], BF16, 14 * 1024 + 512)
    ckd_sb = k.alias("ckd_sb", env.arena, [128, 512], BF16, 15 * 1024)
    stf = k.sbuf("stf", [128, 128], F32)
    stbf = k.sbuf("stbf", [128, 128], BF16)
    qdk = k.sbuf("qdk", [128, 4], F32)
    att_ev = k.sbuf("att_ev", [128, 128], F32)
    att_ev2 = k.sbuf("att_ev2", [128, 128], F32)

    env.load_actT(x2T)
    for pg in range(16):
        for dc in range(768 // 256):
            pass
        for ci, (c0, cw) in enumerate([(0, 512), (512, 256)]):
            for q in range(4):
                k.dma("pool", env.wsb,
                      env.wsb.ap().rearrange("p (c o) -> p c o", o=512)[:, q * 8:(q + 1) * 8, 0:cw],
                      qkv_wT, qkv_wT.ap().rearrange("(c p) o -> p c o", p=128)[:, q * 8:(q + 1) * 8,
                                                                              pg * 768 + c0: pg * 768 + c0 + cw])
            for tt in range(NT):
                pm = env.pmm[tt % 2]
                terms = [(env.actT[:, e * NTOK + tt * 128: e * NTOK + tt * 128 + 128],
                          env.wsb.ap().rearrange("p (c o) -> p c o", o=512)[:, e, 0:cw],
                          [env.actT, env.wsb]) for e in range(NE)]
                env.mm_group(pm[:, 0:cw], pm, terms)
                k.emit("act", lambda e, pm=pm, tt=tt, c0=c0, cw=cw: e.activation(
                    out=qkv_sb[:, tt * 768 + c0: tt * 768 + c0 + cw], in_=pm[:, 0:cw], func=AF.Silu),
                    reads=[pm], writes=[qkv_sb])
        for hl in range(2):
            h = 2 * pg + hl
            bd = float(block_decay[h])
            qo = hl * 384
            k.dma("sp", qdk, qdk[:, 0:2], qdtab, qdtab[h * 128:(h + 1) * 128, :])
            k.dma("sp", qdk, qdk[:, 2:4], kdtab, kdtab[h * 128:(h + 1) * 128, :])
            k.dma("pool", ddsb, ddsb.ap().rearrange("p (a b) -> p a b", a=2),
                  ddtab, ddtab.ap().rearrange("(h a p) m -> p (h a) m", a=2, p=128)[:, 2 * h:2 * h + 2, :])
            k.emit("pool", lambda e: e.memset(stf[:], 0.0), writes=[stf])
            k.emit("pool", lambda e: e.memset(stbf[:], 0.0), writes=[stbf])
            for j in range(NBLK):
                rb = 2 * j
                for u in range(2):
                    k.emit("pe", lambda e, u=u, rb=rb, qo=qo: e.transpose(
                        env.ptrb[:, u * 128:(u + 1) * 128],
                        qkv_sb[:, (rb + u) * 768 + qo: (rb + u) * 768 + qo + 128], env.idn[:]),
                        reads=[qkv_sb, env.idn], writes=[env.ptrb], inc=0)
                    k.emit("pe", lambda e, u=u, rb=rb, qo=qo: e.transpose(
                        env.ptrb[:, 256 + u * 128: 256 + (u + 1) * 128],
                        qkv_sb[:, (rb + u) * 768 + qo + 128: (rb + u) * 768 + qo + 256], env.idn[:]),
                        reads=[qkv_sb, env.idn], writes=[env.ptrb], inc=1 if u == 1 else 0)
                k.emit("dve", lambda e: e.tensor_copy(out=cqT_sb[:, :], in_=env.ptrb[:, 0:256]),
                       reads=[env.ptrb], writes=[cqT_sb])
                k.emit("dve", lambda e: e.tensor_copy(out=ckT_sb[:, :], in_=env.ptrb[:, 256:512]),
                       reads=[env.ptrb], writes=[ckT_sb])
                k.dma("sp", cqTs, cqTs[h * 128:(h + 1) * 128, j * 256:(j + 1) * 256], cqT_sb, cqT_sb[:, :])
                for mt in range(2):
                    env.mm_group(env.pl[0][:, mt * 256:(mt + 1) * 256] if False else env.pl[0][:, 0:256], env.pl[0],
                                 [(ckT_sb[:, mt * 128:(mt + 1) * 128], cqT_sb[:, :], [ckT_sb, cqT_sb])])
                    k.emit("dve", lambda e, mt=mt: e.tensor_tensor(
                        out=scT[:, mt * 256:(mt + 1) * 256], in0=env.pl[0][:, 0:256],
                        in1=ddsb[:, mt * 256:(mt + 1) * 256], op=OP.mult),
                        reads=[env.pl[0], ddsb], writes=[scT])
                for ss in range(2):
                    pi = env.pmm[ss]
                    terms = [(scT[:, mt * 256 + ss * 128: mt * 256 + ss * 128 + 128],
                              qkv_sb[:, (rb + mt) * 768 + qo + 256: (rb + mt) * 768 + qo + 384],
                              [scT, qkv_sb]) for mt in range(2)]
                    env.mm_group(pi[:, 0:128], pi, terms)
                    env.mm_group(env.pl[1][:, 0:128], env.pl[1],
                                 [(cqT_sb[:, ss * 128:(ss + 1) * 128], stbf[:, :], [cqT_sb, stbf])])
                    k.emit("act", lambda e, pi=pi: e.copy(out=att_ev[:], in_=pi[:, 0:128]),
                           reads=[pi], writes=[att_ev])
                    k.emit("dve", lambda e, ss=ss: e.scalar_tensor_tensor(
                        out=att_ev2[:], in0=env.pl[1][:, 0:128], scalar=qdk[:, ss:ss + 1], in1=att_ev[:],
                        op0=OP.mult, op1=OP.add),
                        reads=[env.pl[1], qdk, att_ev], writes=[att_ev2])
                    k.dma("sp", attnp, attnp[(j * 2 + ss) * 128:(j * 2 + ss + 1) * 128, h * 128:(h + 1) * 128],
                          att_ev2, att_ev2[:, :])
                for u in range(2):
                    k.emit("dve", lambda e, u=u, rb=rb, qo=qo: e.tensor_scalar(
                        out=ckd_sb[:, u * 128:(u + 1) * 128],
                        in0=qkv_sb[:, (rb + u) * 768 + qo + 128: (rb + u) * 768 + qo + 256],
                        scalar1=qdk[:, 2 + u:3 + u], scalar2=None, op0=OP.mult, op1=OP.bypass),
                        reads=[qkv_sb, qdk], writes=[ckd_sb])
                terms = [(ckd_sb[:, u * 128:(u + 1) * 128],
                          qkv_sb[:, (rb + u) * 768 + qo + 256: (rb + u) * 768 + qo + 384],
                          [ckd_sb, qkv_sb]) for u in range(2)]
                env.mm_group(env.ptrf[:, 0:128], env.ptrf, terms)
                k.emit("dve", lambda e, bd=bd: e.scalar_tensor_tensor(
                    out=stf[:], in0=stf[:], scalar=bd, in1=env.ptrf[:, 0:128], op0=OP.mult, op1=OP.add),
                    reads=[stf, env.ptrf], writes=[stf])
                k.emit("dve", lambda e: e.tensor_copy(out=stbf[:], in_=stf[:]), reads=[stf], writes=[stbf])
            k.dma("sp", kvloc, kvloc[h * 128:(h + 1) * 128, :], stf, stf[:, :])
    k.merge_tokens(env.arena, [qkv_sb, scT, ddsb, cqT_sb, ckT_sb, ckd_sb])
    return [k.out_token(attnp), k.out_token(kvloc), k.out_token(cqTs)]


def emit_stageC2(env: Env, io, decays):
    k = env.k
    kvall, attnp, cqTs, qdbtab, Wkvb = io["kvall"], io["attnp"], io["cqTs"], io["qdbtab"], io["Wkvb"]
    attn_rows = io["attn_rows"]
    stf = k.alias("stf2", env.arena, [128, 128], F32, 0)
    stbf = k.alias("stbf2", env.arena, [128, 128], BF16, 512)
    dl = k.alias("dl", env.arena, [128, 128], F32, 768)
    wv = k.alias("wv", env.arena, [128, 8], F32, 1280)
    qdb = k.alias("qdb", env.arena, [128, 2 * NBLK], F32, 1312)
    cqT_sb = k.alias("cqT_sb2", env.arena, [128, NTOK], BF16, 1344)
    ap_in = k.alias("ap_in", env.arena, [128, 128], F32, 3392)
    ap_out = k.alias("ap_out", env.arena, [128, 128], F32, 3904)
    for h in range(32):
        k.dma("sp", wv, wv[:, :], Wkvb, Wkvb[h * 128:(h + 1) * 128, :])
        k.dma("sp", qdb, qdb[:, :], qdbtab, qdbtab[h * 128:(h + 1) * 128, :])
        k.dma("sp", cqT_sb, cqT_sb[:, :], cqTs, cqTs[h * 128:(h + 1) * 128, :])
        k.emit("pool", lambda e: e.memset(stf[:], 0.0), writes=[stf])
        for r in range(8):
            k.dma("pool", dl, dl[:, :], kvall, kvall[r * D + h * 128: r * D + (h + 1) * 128, :])
            k.emit("dve", lambda e, r=r: e.scalar_tensor_tensor(
                out=stf[:], in0=dl[:], scalar=wv[:, r:r + 1], in1=stf[:], op0=OP.mult, op1=OP.add),
                reads=[dl, wv, stf], writes=[stf])
        k.emit("dve", lambda e: e.tensor_copy(out=stbf[:], in_=stf[:]), reads=[stf], writes=[stbf])
        for j in range(NBLK):
            for ss in range(2):
                env.mm_group(env.pl[0][:, 0:128], env.pl[0],
                             [(cqT_sb[:, j * 256 + ss * 128: j * 256 + ss * 128 + 128], stbf[:, :],
                               [cqT_sb, stbf])])
                row = (j * 2 + ss) * 128
                k.dma("sp", ap_in, ap_in[:, :], attnp, attnp[row:row + 128, h * 128:(h + 1) * 128])
                k.emit("dve", lambda e, j=j, ss=ss: e.scalar_tensor_tensor(
                    out=ap_out[:], in0=env.pl[0][:, 0:128], scalar=qdb[:, j * 2 + ss: j * 2 + ss + 1],
                    in1=ap_in[:], op0=OP.mult, op1=OP.add),
                    reads=[env.pl[0], qdb, ap_in], writes=[ap_out])
                k.dma("sp", attn_rows, attn_rows[row:row + 128, h * 128:(h + 1) * 128], ap_out, ap_out[:, :])
    k.merge_tokens(env.arena, [stf, stbf, dl, wv, qdb, cqT_sb, ap_in, ap_out])
    return [k.out_token(attn_rows)]



def build_B():
    nc = bass.Bass("TRN2", target_bir_lowering=False)
    k = K(nc)
    io = dict(
        aoTc=k.dram_in("aoTc", [D, NTOK], BF16),
        xrows=k.dram_in("xrows", [NTOK, D], F32),
        WoT=k.dram_in("WoT", [D, D], F32),
        kernW=k.dram_in("kernW", [D, D], F32),
        fbias=k.dram_in("fbias", [1, D], F32),
        LA=k.dram_in("LA", [512, D], F32),
        LBT=k.dram_in("LBT", [512, D], F32),
        SelAB=k.dram_in("SelAB", [512, 32], F32),
        AshT=k.dram_in("AshT", [D, 32], F32),
        BshT=k.dram_in("BshT", [32, D], F32),
        x2rows=k.dram_out("x2rows", [NTOK, D], F32),
        x2T=k.dram_out("x2T", [D, NTOK], BF16),
        x1sc=k.dram_tmp("x1sc", [NTOK, D], F32),
        x1Tsc=k.dram_tmp("x1Tsc", [D, NTOK], BF16),
    )
    with nc.Block() as blk:
        env = Env(k)
        fw = emit_stageB(env, io)
        k.finalize(blk, final_waits=fw)
    k.close()
    return k


def build_D():
    nc = bass.Bass("TRN2", target_bir_lowering=False)
    k = K(nc)
    io = dict(
        attn_rows=k.dram_in("attn_rows", [NTOK, D], F32),
        x2rows=k.dram_in("x2rows", [NTOK, D], F32),
        x2T=k.dram_in("x2T", [D, NTOK], BF16),
        nwcol=k.dram_in("nwcol", [128, 32], F32),
        gate_wT=k.dram_in("gate_wT", [D, D], F32),
        out_wT=k.dram_in("out_wT", [D, D], F32),
        W_baseT=k.dram_in("W_baseT", [D, D], F32),
        b_base=k.dram_in("b_base", [1, D], F32),
        encT=k.dram_in("encT", [D, 40], F32),
        CW0=k.dram_in("CW0", [3 * D, 32], F32),
        MW0=k.dram_in("MW0", [3 * D, 32], U8),
        CW1=k.dram_in("CW1", [3 * D, 8], F32),
        MW1=k.dram_in("MW1", [3 * D, 8], U8),
        Kmat=k.dram_in("Kmat", [D, D], F32),
        outrows=k.dram_out("outrows", [NTOK, D], F32),
        x3sc=k.dram_tmp("x3sc", [NTOK, D], F32),
        x3Tsc=k.dram_tmp("x3Tsc", [D, NTOK], BF16),
        y0Tsc=k.dram_tmp("y0Tsc", [D, NTOK], BF16),
        Hsc=k.dram_tmp("Hsc", [40, D], BF16),
    )
    with nc.Block() as blk:
        env = Env(k)
        fw = emit_stageD(env, io)
        k.finalize(blk, final_waits=fw)
    k.close()
    return k


def build_C1():
    nc = bass.Bass("TRN2", target_bir_lowering=False)
    k = K(nc)
    io = dict(
        x2T=k.dram_in("x2T", [D, NTOK], BF16),
        qkv_wT=k.dram_in("qkv_wT", [D, 3 * D], F32),
        ddtab=k.dram_in("ddtab", [32 * 256, 256], F32),
        qdtab=k.dram_in("qdtab", [32 * 128, 2], F32),
        kdtab=k.dram_in("kdtab", [32 * 128, 2], F32),
        attnp=k.dram_out("attnp", [NTOK, D], F32),
        kvloc=k.dram_out("kvloc", [D, 128], F32),
        cqTs=k.dram_out("cqTs", [D, NTOK], BF16),
    )
    decays = dict(bd=io.pop("_bd", None))
    import math
    HEADS = 32
    base = 1.0 / 2.0 ** (8.0 / HEADS)
    factor = 1.0 - 1 / (32 - 1 + 1e-05) + 1e-05
    bd = [math.exp(-(base ** (h + 1) * factor) * BS) for h in range(HEADS)]
    decays = dict(bd=bd)
    with nc.Block() as blk:
        env = Env(k)
        fw = emit_stageC1(env, io, decays)
        k.finalize(blk, final_waits=fw)
    k.close()
    return k


def build_C2():
    nc = bass.Bass("TRN2", target_bir_lowering=False)
    k = K(nc)
    io = dict(
        kvall=k.dram_in("kvall", [8 * D, 128], F32),
        attnp=k.dram_in("attnp", [NTOK, D], F32),
        cqTs=k.dram_in("cqTs", [D, NTOK], BF16),
        qdbtab=k.dram_in("qdbtab", [32 * 128, 2 * NBLK], F32),
        Wkvb=k.dram_in("Wkvb", [32 * 128, 8], F32),
        attn_rows=k.dram_out("attn_rows", [NTOK, D], F32),
    )
    with nc.Block() as blk:
        env = Env(k)
        fw = emit_stageC2(env, io, dict())
        k.finalize(blk, final_waits=fw)
    k.close()
    return k



S = 4096
NJ = S // 128
KSEL = 820

def emit_stageA2(env, io):
    k = env.k
    qhTh, qrTh, khTh, krTh, vh, aoT = io["qhT"], io["qrT"], io["khT"], io["krT"], io["v"], io["aoT"]

    qhT = k.alias("a_qhT", env.actT, [128, S], BF16, 0)
    qrT = k.alias("a_qrT", env.actT, [128, S], BF16, 8 * 1024)
    khT = k.alias("a_khT", env.actT, [128, S], BF16, 16 * 1024)
    krT = k.alias("a_krT", env.actT, [128, S], BF16, 24 * 1024)
    vsb = k.alias("a_v", env.actT, [128, S], BF16, 32 * 1024)
    outT = k.alias("a_outT", env.actT, [128, S], BF16, 40 * 1024)
    sim = [k.alias("b_sim0", env.xpre[0], [128, S], BF16, 0),
           k.alias("b_sim1", env.xpre[0], [128, S], BF16, 8 * 1024),
           k.alias("b_sim2", env.xpre[1], [128, S], BF16, 0),
           k.alias("b_sim3", env.xpre[1], [128, S], BF16, 8 * 1024)]
    scores = k.alias("b_scores", env.xpre[2], [128, S], F32, 0)
    attn = k.alias("b_attn", env.xpre[3], [128, S], BF16, 0)
    attnT = k.alias("b_attnT", env.xpre[3], [128, S], BF16, 8 * 1024)
    E = k.alias("a_E", env.arena, [128, S], BF16, 0)
    msel = k.alias("a_m", env.arena, [128, S], BF16, 8 * 1024)
    P = k.alias("a_P", env.wsb, [128, S], F32, 0)
    PG = k.alias("a_PG", env.wsb, [128, S], F32, 16 * 1024)
    blkmask = k.alias("a_blkmask", env.brow, [128, S], BF16, 0)
    b2 = k.alias("a_b2", env.tstage, [128, S], BF16, 0)
    ones32 = k.sbuf("a_ones32", [128, NJ], F32)
    bs = k.sbuf("a_bs", [128, NJ], F32)
    bo = k.sbuf("a_bo", [128, NJ], F32)
    rhs = k.sbuf("a_rhs", [128, NJ], F32)
    thr = k.sbuf("a_thr", [128, 4], F32)
    vc = k.sbuf("a_vc", [128, 4], F32)
    cnt = k.sbuf("a_cnt", [128, 4], F32)
    pred = k.sbuf("a_pred", [128, 4], F32)
    quota = k.sbuf("a_quota", [128, 4], F32)
    rowsum = k.sbuf("a_rowsum", [128, 1], F32)
    rinv = k.sbuf("a_rinv", [128, 1], F32)
    biasex = k.sbuf("a_biasex", [128, 1], F32)
    causf = k.sbuf("a_causf", [128, 128], F32)
    causb = k.sbuf("a_causb", [128, 128], BF16)

    SCALE = 1.0 / float(np.sqrt(128.0))
    BIG = 1000.0
    BIAS = -BIG * SCALE

    k.emit("pool", lambda e: e.memset(causf[:], 0.0), writes=[causf])
    k.emit("pool", lambda e: e.affine_select(out=causf[:], in_=causf[:], compare_op=OP.is_ge,
                                             fill=-3000.0, base=0, pattern=[[-1, 128]], channel_multiplier=1),
           reads=[causf], writes=[causf])
    k.emit("pool", lambda e: e.memset(causb[:], 0.0), writes=[causb])
    k.emit("pool", lambda e: e.affine_select(out=causb[:], in_=causb[:], compare_op=OP.is_ge,
                                             fill=-1000.0, base=0, pattern=[[-1, 128]], channel_multiplier=1),
           reads=[causb], writes=[causb])
    k.emit("pool", lambda e: e.memset(biasex[:], BIAS), writes=[biasex])
    k.emit("pool", lambda e: e.memset(ones32[:], 1.0), writes=[ones32])
    k.emit("pool", lambda e: e.memset(blkmask[:], 1.0), writes=[blkmask])
    for bidx in range(NJ):
        k.emit("pool", lambda e, bidx=bidx: e.memset(blkmask[:, bidx * 128:bidx * 128 + 1], 0.0), writes=[blkmask])

    FIRST_SEARCH = 6

    for h in range(8):
        kv = h // 4
        for q in range(4):
            k.dma("sp", qhT, qhT.ap().rearrange("p (a t) -> p a t", a=4)[:, q, :],
                  qhTh, qhTh[h * 128:(h + 1) * 128, q * 1024:(q + 1) * 1024])
            k.dma("sp", qrT, qrT.ap().rearrange("p (a t) -> p a t", a=4)[:, q, :],
                  qrTh, qrTh[h * 128:(h + 1) * 128, q * 1024:(q + 1) * 1024])
            k.dma("sp", khT, khT.ap().rearrange("p (a t) -> p a t", a=4)[:, q, :],
                  khTh, khTh[h * 128:(h + 1) * 128, q * 1024:(q + 1) * 1024])
            k.dma("sp", krT, krT.ap().rearrange("p (a t) -> p a t", a=4)[:, q, :],
                  krTh, krTh[kv * 128:(kv + 1) * 128, q * 1024:(q + 1) * 1024])
        k.dma("sp", vsb, vsb.ap().rearrange("p (c d) -> p c d", d=128),
              vh, vh.ap().rearrange("(kv c p) d -> p (kv c) d", p=128, c=NJ)[:, kv * NJ:(kv + 1) * NJ, :])

        groups = [list(range(g, min(g + 4, NJ))) for g in range(FIRST_SEARCH, NJ, 4)]
        def do_sim(j, slot):
            T = (j + 1) * 128
            nch = (T + 511) // 512
            for ch in range(nch):
                w = min(512, T - ch * 512)
                pm = env.pl[ch % 2]
                env.mm_group(pm[:, 0:w], pm,
                             [(qhT[:, j * 128:(j + 1) * 128], khT[:, ch * 512:ch * 512 + w], [qhT, khT])])
                k.emit("dve", lambda e, pm=pm, ch=ch, w=w, slot=slot: e.tensor_copy(
                    out=sim[slot][:, ch * 512:ch * 512 + w], in_=pm[:, 0:w]),
                    reads=[pm], writes=[sim[slot]])
            k.emit("dve", lambda e, j=j, slot=slot: e.tensor_tensor(
                out=sim[slot][:, j * 128:(j + 1) * 128], in0=sim[slot][:, j * 128:(j + 1) * 128],
                in1=causb[:, :], op=OP.add),
                reads=[sim[slot], causb], writes=[sim[slot]])

        def do_scores(j):
            T = (j + 1) * 128
            nch = (T + 511) // 512
            for ch in range(nch):
                w = min(512, T - ch * 512)
                pm = env.pmm[ch % 2]
                env.mm_group(pm[:, 0:w], pm,
                             [(qrT[:, j * 128:(j + 1) * 128], krT[:, ch * 512:ch * 512 + w], [qrT, krT])])
                k.emit("act", lambda e, pm=pm, ch=ch, w=w: e.copy(
                    out=scores[:, ch * 512:ch * 512 + w], in_=pm[:, 0:w]),
                    reads=[pm], writes=[scores])
            k.emit("dve", lambda e, j=j: e.tensor_tensor(
                out=scores[:, j * 128:(j + 1) * 128], in0=scores[:, j * 128:(j + 1) * 128],
                in1=causf[:, :], op=OP.add),
                reads=[scores, causf], writes=[scores])

        def finish_tile(j, masked_buf, masked_ap):
            T = (j + 1) * 128
            k.emit("act", lambda e, T=T, masked_ap=masked_ap: e.activation(
                out=attn[:, 0:T], in_=masked_ap, func=AF.Exp, scale=SCALE, bias=biasex[:],
                accum_out=rowsum[:]),
                reads=[masked_buf, biasex], writes=[attn, rowsum], accum=[rowsum])
            k.emit("dve", lambda e: e.reciprocal(out=rinv[:], in_=rowsum[:]), reads=[rowsum], writes=[rinv])
            k.emit("dve", lambda e, T=T: e.tensor_scalar(out=attn[:, 0:T], in0=attn[:, 0:T], scalar1=rinv[:],
                                                         scalar2=None, op0=OP.mult, op1=OP.bypass),
                   reads=[attn, rinv], writes=[attn])
            nt = j + 1
            for tg in range((nt + 7) // 8):
                n8 = min(8, nt - tg * 8)
                for u in range(n8):
                    tc = tg * 8 + u
                    k.emit("pe", lambda e, tc=tc, u=u: e.transpose(
                        env.ptrb[:, u * 128:(u + 1) * 128], attn[:, tc * 128:(tc + 1) * 128], env.idn[:]),
                        reads=[attn, env.idn], writes=[env.ptrb], inc=1 if u == n8 - 1 else 0)
                k.emit("dve", lambda e, tg=tg, n8=n8: e.tensor_copy(
                    out=attnT[:, tg * 1024:tg * 1024 + n8 * 128], in_=env.ptrb[:, 0:n8 * 128]),
                    reads=[env.ptrb], writes=[attnT])
            terms = [(vsb[:, tc * 128:(tc + 1) * 128], attnT[:, tc * 128:(tc + 1) * 128], [vsb, attnT])
                     for tc in range(nt)]
            env.mm_group(env.ptrf[:, 0:128], env.ptrf, terms)
            k.emit("dve", lambda e, j=j: e.tensor_copy(out=outT[:, j * 128:(j + 1) * 128], in_=env.ptrf[:, 0:128]),
                   reads=[env.ptrf], writes=[outT])

        for j in range(FIRST_SEARCH):
            do_scores(j)
            k.emit("dve", lambda e, j=j: e.tensor_scalar(
                out=scores[:, 0:(j + 1) * 128], in0=scores[:, 0:(j + 1) * 128], scalar1=BIG, scalar2=None,
                op0=OP.add, op1=OP.bypass), reads=[scores], writes=[scores])
            finish_tile(j, scores, scores[:, 0:(j + 1) * 128])

        for grp in groups:
            for slot, j in enumerate(grp):
                do_sim(j, slot)
            ns = len(grp)
            k.emit("pool", lambda e, ns=ns: e.memset(thr[:, 0:ns], -130.0), writes=[thr])
            for step in (128, 64, 32, 16, 8, 4, 2, 1):
                sv = float(2 * step)
                k.emit("pool", lambda e, sv=sv, ns=ns: e.tensor_scalar(
                    out=vc[:, 0:ns], in0=thr[:, 0:ns], scalar1=sv, scalar2=None, op0=OP.add, op1=OP.bypass),
                    reads=[thr], writes=[vc])
                for slot, j in enumerate(grp):
                    T = (j + 1) * 128
                    k.emit("dve", lambda e, slot=slot, j=j, T=T: e.tensor_scalar(
                        out=msel[:, 0:T], in0=sim[slot][:, 0:T], scalar1=vc[:, slot:slot + 1], scalar2=None,
                        op0=OP.is_ge, op1=OP.add, accum_out=cnt[:, slot:slot + 1]),
                        reads=[sim[slot], vc], writes=[msel, cnt], accum=[cnt])
                k.emit("pool", lambda e, ns=ns: e.tensor_scalar(
                    out=pred[:, 0:ns], in0=cnt[:, 0:ns], scalar1=float(KSEL), scalar2=None,
                    op0=OP.is_ge, op1=OP.bypass), reads=[cnt], writes=[pred])
                k.emit("pool", lambda e, sv=sv, ns=ns: e.tensor_scalar(
                    out=pred[:, 0:ns], in0=pred[:, 0:ns], scalar1=sv, scalar2=None, op0=OP.mult, op1=OP.bypass),
                    reads=[pred], writes=[pred])
                k.emit("pool", lambda e, ns=ns: e.tensor_tensor(
                    out=thr[:, 0:ns], in0=thr[:, 0:ns], in1=pred[:, 0:ns], op=OP.add),
                    reads=[thr, pred], writes=[thr])
            k.emit("pool", lambda e, ns=ns: e.tensor_scalar(
                out=vc[:, 0:ns], in0=thr[:, 0:ns], scalar1=2.0, scalar2=None, op0=OP.add, op1=OP.bypass),
                reads=[thr], writes=[vc])
            for slot, j in enumerate(grp):
                T = (j + 1) * 128
                k.emit("dve", lambda e, slot=slot, T=T: e.tensor_scalar(
                    out=msel[:, 0:T], in0=sim[slot][:, 0:T], scalar1=vc[:, slot:slot + 1], scalar2=None,
                    op0=OP.is_ge, op1=OP.add, accum_out=cnt[:, slot:slot + 1]),
                    reads=[sim[slot], vc], writes=[msel, cnt], accum=[cnt])
            k.emit("pool", lambda e, ns=ns: e.tensor_scalar(
                out=quota[:, 0:ns], in0=cnt[:, 0:ns], scalar1=-1.0, scalar2=float(KSEL),
                op0=OP.mult, op1=OP.add), reads=[cnt], writes=[quota])
            for slot, j in enumerate(grp):
                T = (j + 1) * 128
                NB = (j + 1)
                k.emit("dve", lambda e, slot=slot, T=T: e.tensor_scalar(
                    out=E[:, 0:T], in0=sim[slot][:, 0:T], scalar1=thr[:, slot:slot + 1], scalar2=None,
                    op0=OP.is_equal, op1=OP.bypass), reads=[sim[slot], thr], writes=[E])
                k.emit("dve", lambda e, slot=slot, T=T: e.tensor_scalar(
                    out=b2[:, 0:T], in0=sim[slot][:, 0:T], scalar1=vc[:, slot:slot + 1], scalar2=None,
                    op0=OP.is_ge, op1=OP.bypass), reads=[sim[slot], vc], writes=[b2])
                k.emit("dve", lambda e, T=T: e.tensor_tensor_scan(
                    out=P[:, 0:T], data0=blkmask[:, 0:T], data1=E[:, 0:T], initial=0.0,
                    op0=OP.mult, op1=OP.add), reads=[blkmask, E], writes=[P])
                k.emit("dve", lambda e, NB=NB: e.tensor_copy(out=bs[:, 0:NB], in_=P[:, 127:NB * 128:128]),
                       reads=[P], writes=[bs])
                k.emit("dve", lambda e, NB=NB: e.tensor_tensor_scan(
                    out=bo[:, 0:NB], data0=ones32[:, 0:NB], data1=bs[:, 0:NB], initial=0.0,
                    op0=OP.mult, op1=OP.add), reads=[ones32, bs], writes=[bo])
                k.emit("dve", lambda e, NB=NB: e.tensor_tensor(
                    out=rhs[:, 0:NB], in0=bo[:, 0:NB], in1=bs[:, 0:NB], op=OP.subtract),
                    reads=[bo, bs], writes=[rhs])
                k.emit("dve", lambda e, NB=NB, T=T: e.tensor_tensor(
                    out=PG.ap()[:, 0:T].rearrange("p (b t) -> p b t", t=128),
                    in0=P.ap()[:, 0:T].rearrange("p (b t) -> p b t", t=128),
                    in1=rhs.ap()[:, 0:NB].unsqueeze(2).broadcast_to([128, NB, 128]),
                    op=OP.add), reads=[P, rhs], writes=[PG])
                k.emit("dve", lambda e, slot=slot, T=T: e.scalar_tensor_tensor(
                    out=msel[:, 0:T], in0=PG[:, 0:T], scalar=quota[:, slot:slot + 1], in1=E[:, 0:T],
                    op0=OP.is_le, op1=OP.mult), reads=[PG, quota, E], writes=[msel])
                do_scores(j)
                k.emit("dve", lambda e, T=T: e.scalar_tensor_tensor(
                    out=scores[:, 0:T], in0=b2[:, 0:T], scalar=BIG, in1=scores[:, 0:T],
                    op0=OP.mult, op1=OP.add), reads=[b2, scores], writes=[scores])
                k.emit("dve", lambda e, T=T: e.scalar_tensor_tensor(
                    out=scores[:, 0:T], in0=msel[:, 0:T], scalar=BIG, in1=scores[:, 0:T],
                    op0=OP.mult, op1=OP.add), reads=[msel, scores], writes=[scores])
                finish_tile(j, scores, scores[:, 0:T])
        k.dma("sp", aoT, aoT[h * 128:(h + 1) * 128, :], outT, outT[:, :])
    k.merge_tokens(env.actT, [qhT, qrT, khT, krT, vsb, outT])
    for xb, subs in [(env.xpre[0], [sim[0], sim[1]]), (env.xpre[1], [sim[2], sim[3]]),
                     (env.xpre[2], [scores]), (env.xpre[3], [attn, attnT])]:
        k.merge_tokens(xb, subs)
    k.merge_tokens(env.arena, [E, msel])
    k.merge_tokens(env.wsb, [P, PG])
    k.merge_tokens(env.brow, [blkmask])
    k.merge_tokens(env.tstage, [b2])
    return [k.out_token(aoT)]


def build_A2():
    nc = bass.Bass("TRN2", target_bir_lowering=False)
    k = K(nc)
    io = dict(
        qhT=k.dram_in("qhT", [1024, S], BF16),
        qrT=k.dram_in("qrT", [1024, S], BF16),
        khT=k.dram_in("khT", [1024, S], BF16),
        krT=k.dram_in("krT", [256, S], BF16),
        v=k.dram_in("v", [2 * S, 128], BF16),
        aoT=k.dram_out("aoT", [1024, S], BF16),
    )
    with nc.Block() as blk:
        env = Env(k)
        fw = emit_stageA2(env, io)
        k.finalize(blk, final_waits=fw)
    k.close()
    return k



def emit_stageA1(env, io):
    k = env.k
    xA, WqkvT = io["xA"], io["WqkvT"]
    hp1, hp2, hb1 = io["hp1"], io["hp2"], io["hb1"]
    cosf, sinm = io["cosf"], io["sinm"]
    qrTo, qhTo, krTo, khTo, vo = io["qrT"], io["qhT"], io["krT"], io["khT"], io["v"]
    WhT, WlT = io["WhT"], io["WlT"]

    xhT = k.alias("p_xhT", env.actT, [128, NE * 512], BF16, 0)
    xlT = k.alias("p_xlT", env.actT, [128, NE * 512], BF16, NE * 1024)
    whsb = k.alias("p_whsb", env.wsb, [128, NE * 256], BF16, 0)
    wlsb = k.alias("p_wlsb", env.wsb, [128, NE * 256], BF16, NE * 512)
    proj = [k.alias(f"p_proj{t}", env.xpre[t], [128, 1536], F32, 0) for t in range(4)]
    xf = [k.alias(f"p_xf{t}", env.xpre[t], [128, 512], F32, 6144) for t in range(4)]
    cosg = k.alias("p_cosg", env.arena, [128, 512], F32, 0)
    sing = k.alias("p_sing", env.arena, [128, 512], F32, 2048)
    hp1s = k.alias("p_hp1", env.arena, [128, 8 * 128], F32, 4096)
    hp2s = k.alias("p_hp2", env.arena, [128, 8 * 128], F32, 8192)
    ropeA = k.alias("p_ropeA", env.arena, [128, 128], F32, 12288)
    ropeB = k.alias("p_ropeB", env.arena, [128, 128], F32, 12800)
    qrTf = k.alias("p_qrTf", env.arena, [128, 128], F32, 13312)
    hh1 = k.alias("p_hh1", env.arena, [128, 128], F32, 13824)
    hh2 = k.alias("p_hh2", env.arena, [128, 128], F32, 14336)
    qbf = k.alias("p_qbf", env.arena, [128, 128], BF16, 14848)
    vstg = k.alias("p_vstg", env.arena, [128, 256], BF16, 15104)
    xbf = k.alias("p_xbf", env.tstage, [128, D], BF16, 0)
    xlb = k.alias("p_xlb", env.brow, [128, D], BF16, 0)
    hbs = k.sbuf("p_hbs", [128, 32], F32)
    k.dma("sp", hp1s, hp1s.ap().rearrange("p (h e) -> p h e", e=128),
          hp1, hp1.ap().rearrange("(h p) e -> p h e", p=128))
    k.dma("sp", hp2s, hp2s.ap().rearrange("p (h e) -> p h e", e=128),
          hp2, hp2.ap().rearrange("(h p) e -> p h e", p=128))
    k.dma("sp", hbs, hbs[:, 0:16], hb1, hb1[:, :])

    wstg = [k.alias(f"p_wstg{i}", env.xpre[i], [128, 1536], F32, 6144 + 2048) if False else None
            for i in range(1)]
    wf = k.alias("p_wf", env.xpre[0], [128, 1536], F32, 8192)
    wh_t = k.alias("p_wht", env.tstage, [128, 1536], BF16, 0)
    wl_t = k.alias("p_wlt", env.brow, [128, 1536], BF16, 0)
    wres = k.alias("p_wres", env.xpre[1], [128, 1536], F32, 8192)
    for r in range(NE):
        k.dma("sp", wf, wf[:, :], WqkvT, WqkvT[r * 128:(r + 1) * 128, :])
        k.emit("dve", lambda e: e.tensor_copy(out=wh_t[:, :], in_=wf[:, :]), reads=[wf], writes=[wh_t])
        k.emit("dve", lambda e: e.tensor_tensor(out=wres[:, :], in0=wf[:, :], in1=wh_t[:, :], op=OP.subtract),
               reads=[wf, wh_t], writes=[wres])
        k.emit("dve", lambda e: e.tensor_copy(out=wl_t[:, :], in_=wres[:, :]), reads=[wres], writes=[wl_t])
        k.dma("sp", WhT, WhT[r * 128:(r + 1) * 128, :], wh_t, wh_t[:, :])
        k.dma("sp", WlT, WlT[r * 128:(r + 1) * 128, :], wl_t, wl_t[:, :])
    k.merge_tokens(env.tstage, [wh_t])
    k.merge_tokens(env.brow, [wl_t])
    k.merge_tokens(env.xpre[0], [wf])
    k.merge_tokens(env.xpre[1], [wres])

    def rope(src_ap, src_buf, ttok):
        co = cosg[:, 0:128] if False else cosg[:, ttok * 128:(ttok + 1) * 128]
        si = sing[:, ttok * 128:(ttok + 1) * 128]
        k.emit("dve", lambda e: e.tensor_tensor(out=ropeA[:, :], in0=src_ap, in1=co, op=OP.mult),
               reads=[src_buf, cosg], writes=[ropeA])
        k.emit("dve", lambda e: e.tensor_tensor(out=ropeB[:, 0:64], in0=src_ap[:, 64:128],
                                                in1=si[:, 0:64], op=OP.mult),
               reads=[src_buf, sing], writes=[ropeB])
        k.emit("dve", lambda e: e.tensor_tensor(out=ropeB[:, 64:128], in0=src_ap[:, 0:64],
                                                in1=si[:, 64:128], op=OP.mult),
               reads=[src_buf, sing], writes=[ropeB])
        k.emit("dve", lambda e: e.tensor_tensor(out=ropeA[:, :], in0=ropeA[:, :], in1=ropeB[:, :], op=OP.add),
               reads=[ropeA, ropeB], writes=[ropeA])
        k.emit("pe", lambda e: e.transpose(env.ptrf[:, 0:128], ropeA[:, :], env.idnf[:]),
               reads=[ropeA, env.idnf], writes=[env.ptrf])
        k.emit("dve", lambda e: e.tensor_copy(out=qrTf[:, :], in_=env.ptrf[:, 0:128]),
               reads=[env.ptrf], writes=[qrTf])
        k.emit("dve", lambda e: e.tensor_copy(out=qbf[:, :], in_=qrTf[:, :]), reads=[qrTf], writes=[qbf])

    def hash_chain(h, out_hbm, r0, c0):
        env.mm_group(env.pl[0][:, 0:128], env.pl[0],
                     [(hp1s[:, h * 128:(h + 1) * 128], qrTf[:, :], [hp1s, qrTf])])
        k.emit("act", lambda e, h=h: e.activation(out=hh1[:], in_=env.pl[0][:, 0:128], func=AF.Silu,
                                                  bias=hbs[:, h:h + 1], scale=1.0),
               reads=[env.pl[0], hbs], writes=[hh1])
        k.emit("dve", lambda e: e.tensor_tensor(out=hh1[:], in0=hh1[:], in1=qrTf[:, :], op=OP.add),
               reads=[hh1, qrTf], writes=[hh1])
        env.mm_group(env.pl[1][:, 0:128], env.pl[1],
                     [(hp2s[:, h * 128:(h + 1) * 128], hh1[:], [hp2s, hh1])])
        k.emit("act", lambda e, h=h: e.activation(out=hh2[:], in_=env.pl[1][:, 0:128], func=AF.Identity,
                                                  bias=hbs[:, 8 + h:9 + h], scale=1.0),
               reads=[env.pl[1], hbs], writes=[hh2])
        k.emit("dve", lambda e: e.tensor_tensor(out=hh2[:], in0=hh2[:], in1=hh1[:], op=OP.add),
               reads=[hh2, hh1], writes=[hh2])
        k.emit("act", lambda e: e.activation(out=qbf[:, :], in_=hh2[:], func=AF.Sign),
               reads=[hh2], writes=[qbf])
        k.dma("sp", out_hbm, out_hbm[r0:r0 + 128, c0:c0 + 128], qbf, qbf[:, :])

    for g in range(8):
        k.dma("sp", cosg, cosg.ap().rearrange("p (a d) -> p a d", d=128),
              cosf, cosf.ap().rearrange("(a p) d -> p a d", p=128)[:, g * 4:(g + 1) * 4, :])
        k.dma("sp", sing, sing.ap().rearrange("p (a d) -> p a d", d=128),
              sinm, sinm.ap().rearrange("(a p) d -> p a d", p=128)[:, g * 4:(g + 1) * 4, :])
        for t in range(4):
            gt = g * 4 + t
            k.dma("sp", xf[t], xf[t][:, 0:512], xA, xA[gt * 128:(gt + 1) * 128, 0:512])
        for t in range(4):
            gt = g * 4 + t
            for ch in range(8):
                k.dma("sp", xf[t], xf[t][:, 0:512], xA, xA[gt * 128:(gt + 1) * 128, ch * 512:(ch + 1) * 512])
                k.emit("dve", lambda e, t=t, ch=ch: e.tensor_copy(out=xbf[:, ch * 512:(ch + 1) * 512],
                                                                  in_=xf[t][:, 0:512]),
                       reads=[xf[t]], writes=[xbf])
                k.emit("dve", lambda e, t=t, ch=ch: e.tensor_tensor(
                    out=xf[t][:, 0:512], in0=xf[t][:, 0:512], in1=xbf[:, ch * 512:(ch + 1) * 512],
                    op=OP.subtract), reads=[xf[t], xbf], writes=[xf[t]])
                k.emit("dve", lambda e, t=t, ch=ch: e.tensor_copy(out=xlb[:, ch * 512:(ch + 1) * 512],
                                                                  in_=xf[t][:, 0:512]),
                       reads=[xf[t]], writes=[xlb])
            for src, dstT in ((xbf, xhT), (xlb, xlT)):
                for eg in range(4):
                    for j in range(8):
                        et = eg * 8 + j
                        k.emit("pe", lambda e, et=et, j=j, src=src: e.transpose(
                            env.ptrb[:, j * 128:(j + 1) * 128], src[:, et * 128:(et + 1) * 128], env.idn[:]),
                            reads=[src, env.idn], writes=[env.ptrb], inc=1 if j == 7 else 0)
                    k.emit("dve", lambda e, eg=eg, t=t, dstT=dstT: e.tensor_copy(
                        out=dstT.ap().rearrange("p (c t2) -> p c t2", t2=512)[:, eg * 8:(eg + 1) * 8,
                                                                             t * 128:(t + 1) * 128],
                        in_=env.ptrb.ap().rearrange("p (c t2) -> p c t2", t2=128)),
                        reads=[env.ptrb], writes=[dstT])
        for dc in range(6):
            for q in range(4):
                k.dma("sp", whsb, whsb.ap().rearrange("p (c o) -> p c o", o=256)[:, q * 8:(q + 1) * 8, :],
                      WhT, WhT.ap().rearrange("(c p) o -> p c o", p=128)[:, q * 8:(q + 1) * 8,
                                                                        dc * 256:(dc + 1) * 256])
            split = dc < 5
            if split:
                for q in range(4):
                    k.dma("sp", wlsb, wlsb.ap().rearrange("p (c o) -> p c o", o=256)[:, q * 8:(q + 1) * 8, :],
                          WlT, WlT.ap().rearrange("(c p) o -> p c o", p=128)[:, q * 8:(q + 1) * 8,
                                                                            dc * 256:(dc + 1) * 256])
            for t in range(4):
                pm = env.pmm[t % 2]
                terms = []
                for e_ in range(NE):
                    terms.append((xhT[:, e_ * 512 + t * 128: e_ * 512 + t * 128 + 128],
                                  whsb[:, e_ * 256:(e_ + 1) * 256], [xhT, whsb]))
                if split:
                    for e_ in range(NE):
                        terms.append((xhT[:, e_ * 512 + t * 128: e_ * 512 + t * 128 + 128],
                                      wlsb[:, e_ * 256:(e_ + 1) * 256], [xhT, wlsb]))
                    for e_ in range(NE):
                        terms.append((xlT[:, e_ * 512 + t * 128: e_ * 512 + t * 128 + 128],
                                      whsb[:, e_ * 256:(e_ + 1) * 256], [xlT, whsb]))
                env.mm_group(pm[:, 0:256], pm, terms)
                k.emit("dve", lambda e, t=t, dc=dc, pm=pm: e.tensor_copy(
                    out=proj[t][:, dc * 256:(dc + 1) * 256], in_=pm[:, 0:256]),
                    reads=[pm], writes=[proj[t]])
        for t in range(4):
            gt = g * 4 + t
            for h in range(8):
                rope(proj[t].ap()[:, h * 128:(h + 1) * 128], proj[t], t)
                k.dma("sp", qrTo, qrTo[h * 128:(h + 1) * 128, gt * 128:(gt + 1) * 128], qbf, qbf[:, :])
                hash_chain(h, qhTo, h * 128, gt * 128)
            for kvl in range(2):
                rope(proj[t].ap()[:, 1024 + kvl * 128: 1024 + (kvl + 1) * 128], proj[t], t)
                k.dma("sp", krTo, krTo[kvl * 128:(kvl + 1) * 128, gt * 128:(gt + 1) * 128], qbf, qbf[:, :])
                for hq in range(4):
                    hash_chain(kvl * 4 + hq, khTo, (kvl * 4 + hq) * 128, gt * 128)
            k.emit("dve", lambda e, t=t: e.tensor_copy(out=vstg[:, :], in_=proj[t][:, 1280:1536]),
                   reads=[proj[t]], writes=[vstg])
            for kvl in range(2):
                k.dma("sp", vo, vo[kvl * S + gt * 128: kvl * S + (gt + 1) * 128, :],
                      vstg, vstg[:, kvl * 128:(kvl + 1) * 128])
    k.merge_tokens(env.arena, [cosg, sing, hp1s, hp2s, ropeA, ropeB, qrTf, hh1, hh2, qbf, vstg])
    for i in range(4):
        k.merge_tokens(env.xpre[i], [proj[i], xf[i]])
    k.merge_tokens(env.actT, [xhT, xlT])
    k.merge_tokens(env.wsb, [whsb, wlsb])
    k.merge_tokens(env.tstage, [xbf])
    k.merge_tokens(env.brow, [xlb])
    return [k.out_token(qrTo), k.out_token(qhTo), k.out_token(krTo), k.out_token(khTo), k.out_token(vo)]


def build_A1():
    nc = bass.Bass("TRN2", target_bir_lowering=False)
    k = K(nc)
    io = dict(
        xA=k.dram_in("xA", [S, D], F32),
        WqkvT=k.dram_in("WqkvT", [D, 1536], F32),
        hp1=k.dram_in("hp1", [1024, 128], F32),
        hp2=k.dram_in("hp2", [1024, 128], F32),
        hb1=k.dram_in("hb1", [128, 16], F32),
        cosf=k.dram_in("cosf", [S, 128], F32),
        sinm=k.dram_in("sinm", [S, 128], F32),
        qrT=k.dram_out("qrT", [1024, S], BF16),
        qhT=k.dram_out("qhT", [1024, S], BF16),
        krT=k.dram_out("krT", [256, S], BF16),
        khT=k.dram_out("khT", [1024, S], BF16),
        v=k.dram_out("v", [2 * S, 128], BF16),
        WhT=k.dram_tmp("WhT", [D, 1536], BF16),
        WlT=k.dram_tmp("WlT", [D, 1536], BF16),
    )
    with nc.Block() as blk:
        env = Env(k)
        fw = emit_stageA1(env, io)
        k.finalize(blk, final_waits=fw)
    k.close()
    return k



def build_L1():
    nc = bass.Bass("TRN2", target_bir_lowering=False)
    k = K(nc)
    io = dict(
        xA=k.dram_in("xA", [S, D], F32),
        WqkvT=k.dram_in("WqkvT", [D, 1536], F32),
        hp1=k.dram_in("hp1", [1024, 128], F32),
        hp2=k.dram_in("hp2", [1024, 128], F32),
        hb1=k.dram_in("hb1", [128, 16], F32),
        cosf=k.dram_in("cosf", [S, 128], F32),
        sinm=k.dram_in("sinm", [S, 128], F32),
        qrT=k.dram_tmp("qrT", [1024, S], BF16),
        qhT=k.dram_tmp("qhT", [1024, S], BF16),
        krT=k.dram_tmp("krT", [256, S], BF16),
        khT=k.dram_tmp("khT", [1024, S], BF16),
        v=k.dram_tmp("v", [2 * S, 128], BF16),
        WhT=k.dram_tmp("WhT", [D, 1536], BF16),
        WlT=k.dram_tmp("WlT", [D, 1536], BF16),
        aoT=k.dram_out("aoT", [1024, S], BF16),
    )
    with nc.Block() as blk:
        env = Env(k)
        emit_stageA1(env, io)
        fw = emit_stageA2(env, io)
        k.finalize(blk, final_waits=fw)
    k.close()
    return k


def build_L2():
    nc = bass.Bass("TRN2", target_bir_lowering=False)
    k = K(nc)
    io = dict(
        aoTc=k.dram_in("aoTc", [D, NTOK], BF16),
        xrows=k.dram_in("xrows", [NTOK, D], F32),
        WoT=k.dram_in("WoT", [D, D], F32),
        kernW=k.dram_in("kernW", [D, D], F32),
        fbias=k.dram_in("fbias", [1, D], F32),
        LA=k.dram_in("LA", [512, D], F32),
        LBT=k.dram_in("LBT", [512, D], F32),
        SelAB=k.dram_in("SelAB", [512, 32], F32),
        AshT=k.dram_in("AshT", [D, 32], F32),
        BshT=k.dram_in("BshT", [32, D], F32),
        x2rows=k.dram_out("x2rows", [NTOK, D], F32),
        x2T=k.dram_out("x2T", [D, NTOK], BF16),
        x1sc=k.dram_tmp("x1sc", [NTOK, D], F32),
        x1Tsc=k.dram_tmp("x1Tsc", [D, NTOK], BF16),
        qkv_wT=k.dram_in("qkv_wT", [D, 3 * D], F32),
        ddtab=k.dram_in("ddtab", [32 * 256, 256], F32),
        qdtab=k.dram_in("qdtab", [32 * 128, 2], F32),
        kdtab=k.dram_in("kdtab", [32 * 128, 2], F32),
        attnp=k.dram_out("attnp", [NTOK, D], F32),
        kvloc=k.dram_out("kvloc", [D, 128], F32),
        cqTs=k.dram_out("cqTs", [D, NTOK], BF16),
    )
    import math
    HEADS = 32
    base = 1.0 / 2.0 ** (8.0 / HEADS)
    factor = 1.0 - 1 / (32 - 1 + 1e-05) + 1e-05
    bd = [math.exp(-(base ** (h + 1) * factor) * BS) for h in range(HEADS)]
    with nc.Block() as blk:
        env = Env(k)
        fwB = emit_stageB(env, io)
        fwC = emit_stageC1(env, io, dict(bd=bd))
        k.finalize(blk, final_waits=fwB + fwC)
    k.close()
    return k


def build_L3():
    nc = bass.Bass("TRN2", target_bir_lowering=False)
    k = K(nc)
    io = dict(
        kvall=k.dram_in("kvall", [8 * D, 128], F32),
        attnp=k.dram_in("attnp", [NTOK, D], F32),
        cqTs=k.dram_in("cqTs", [D, NTOK], BF16),
        qdbtab=k.dram_in("qdbtab", [32 * 128, 2 * NBLK], F32),
        Wkvb=k.dram_in("Wkvb", [32 * 128, 8], F32),
        attn_rows=k.dram_tmp("attn_rows", [NTOK, D], F32),
        x2rows=k.dram_in("x2rows", [NTOK, D], F32),
        x2T=k.dram_in("x2T", [D, NTOK], BF16),
        nwcol=k.dram_in("nwcol", [128, 32], F32),
        gate_wT=k.dram_in("gate_wT", [D, D], F32),
        out_wT=k.dram_in("out_wT", [D, D], F32),
        W_baseT=k.dram_in("W_baseT", [D, D], F32),
        b_base=k.dram_in("b_base", [1, D], F32),
        encT=k.dram_in("encT", [D, 40], F32),
        CW0=k.dram_in("CW0", [3 * D, 32], F32),
        MW0=k.dram_in("MW0", [3 * D, 32], U8),
        CW1=k.dram_in("CW1", [3 * D, 8], F32),
        MW1=k.dram_in("MW1", [3 * D, 8], U8),
        Kmat=k.dram_in("Kmat", [D, D], F32),
        outrows=k.dram_out("outrows", [NTOK, D], F32),
        x3sc=k.dram_tmp("x3sc", [NTOK, D], F32),
        x3Tsc=k.dram_tmp("x3Tsc", [D, NTOK], BF16),
        y0Tsc=k.dram_tmp("y0Tsc", [D, NTOK], BF16),
        Hsc=k.dram_tmp("Hsc", [40, D], BF16),
    )
    with nc.Block() as blk:
        env = Env(k)
        emit_stageC2(env, io, dict())
        fw = emit_stageD(env, io)
        k.finalize(blk, final_waits=fw)
    k.close()
    return k



def emit_stageW(env, io):
    k = env.k
    aoT, WoTloc, ypart = io["aoT"], io["WoTloc"], io["ypart"]
    whloc = k.alias("w_whloc", env.wsb, [128, 8 * 512], BF16, 0)
    wstg = env.xpre[0]
    for q in range(4):
        k.dma("sp", env.actT, env.actT.ap().rearrange("p (c t) -> p c t", t=S)[:, q * 2:(q + 1) * 2, :],
              aoT, aoT.ap().rearrange("(c p) t -> p c t", p=128)[:, q * 2:(q + 1) * 2, :])
    for oc in range(8):
        k.dma("pool", wstg, wstg.ap().rearrange("p (c o) -> p c o", o=512), WoTloc,
              WoTloc.ap().rearrange("(c p) o -> p c o", p=128)[:, :, oc * 512:(oc + 1) * 512])
        for e_ in range(8):
            k.emit("dve", lambda e, e_=e_: e.tensor_copy(out=whloc[:, e_ * 512:(e_ + 1) * 512],
                                                         in_=wstg[:, e_ * 512:(e_ + 1) * 512]),
                   reads=[wstg], writes=[whloc])
        for tt in range(32):
            pm = env.pmm[tt % 2]
            terms = [(env.actT[:, e_ * S + tt * 128: e_ * S + tt * 128 + 128],
                      whloc[:, e_ * 512:(e_ + 1) * 512], [env.actT, whloc]) for e_ in range(8)]
            env.mm_group(pm[:, :], pm, terms)
            ys = env.xres[tt % 4]
            k.emit("act", lambda e, pm=pm, ys=ys: e.copy(out=ys[:, :], in_=pm[:, :]),
                   reads=[pm], writes=[ys])
            k.dma("sp", ypart, ypart[tt * 128:(tt + 1) * 128, oc * 512:(oc + 1) * 512], ys, ys[:, :])
    k.merge_tokens(env.wsb, [whloc])
    return [k.out_token(ypart)]


def emit_lnrows(env, io):
    k = env.k
    xrows, ymy, x1sc, x1Tsc = io["xrows"], io["ymy"], io["x1sc"], io["x1Tsc"]
    for g in range(2):
        for t in range(HALF):
            gt = g * HALF + t
            xp = env.xpre[t]
            k.dma("sp", xp, xp[:, :], ymy, ymy[gt * 128:(gt + 1) * 128, :])
            for d in range(ND):
                xr = env.xres[t]
                k.dma("sp", xr, xr[:, :], xrows, xrows[gt * 128:(gt + 1) * 128, d * 512:(d + 1) * 512])
                k.emit("dve", lambda e, t=t, d=d, xp=xp, xr=xr: e.scalar_tensor_tensor(
                    out=xp[:, d * 512:(d + 1) * 512], in0=xp[:, d * 512:(d + 1) * 512], scalar=1.0,
                    in1=xr[:, :], op0=OP.mult, op1=OP.add, accum_out=env.sums[t][:, d:d + 1]),
                    reads=[xp, xr], writes=[xp, env.sums[t]], accum=[env.sums[t]])
                k.emit("act", lambda e, t=t, d=d, xp=xp: e.activation(
                    out=env.sqjunk[:], in_=xp[:, d * 512:(d + 1) * 512], func=AF.Square,
                    accum_out=env.sqs[t][:, d:d + 1]),
                    reads=[xp], writes=[env.sqjunk, env.sqs[t]], accum=[env.sqs[t]])
        for t in range(HALF):
            gt = g * HALF + t
            env.ln_stats(env.sums[t], env.sqs[t], 1.0 / D, 1e-5)
            k.emit("dve", lambda e, t=t: e.tensor_scalar(
                out=env.xpre[t][:, :], in0=env.xpre[t][:, :], scalar1=env.mean[:], scalar2=env.rstd[:],
                op0=OP.subtract, op1=OP.mult),
                reads=[env.xpre[t], env.mean, env.rstd], writes=[env.xpre[t]])
            k.dma("sp", x1sc, x1sc[gt * 128:(gt + 1) * 128, :], env.xpre[t], env.xpre[t][:, :])
            env.transpose_out(env.xpre[t], env.xpre[t].ap(), gt, x1Tsc)


def emit_stageB_fused(env, io):
    k = env.k
    kernW, fbias = io["kernW"], io["fbias"]
    LA, LBT, SelAB, AshT, BshT = io["LA"], io["LBT"], io["SelAB"], io["AshT"], io["BshT"]
    x2rows, x2T, x1sc, x1Tsc = io["x2rows"], io["x2T"], io["x1sc"], io["x1Tsc"]

    AeT = k.alias("AeT", env.arena, [128, NE * 32], BF16, 0)
    AshT_sb = k.alias("AshT_sb", env.arena, [128, NE * 32], BF16, 2 * 1024)
    BeT = k.alias("BeT", env.arena, [128, D], BF16, 4 * 1024)
    AxT = k.alias("AxT", env.arena, [128, NTOK], BF16, 12 * 1024)
    sel_sb = k.alias("sel_sb", env.arena, [128, 4 * 32], BF16, 14 * 1024)

    for i in range(4):
        k.dma("pool", env.wsb, env.wsb[:, i * D:(i + 1) * D], LA, LA[i * 128:(i + 1) * 128, :])
    k.dma("pool", sel_sb, sel_sb.ap().rearrange("p (c r) -> p c r", r=32),
          SelAB, SelAB.ap().rearrange("(c p) r -> p c r", p=128))
    k.dma("pool", AshT_sb, AshT_sb.ap().rearrange("p (c r) -> p c r", r=32),
          AshT, AshT.ap().rearrange("(c p) r -> p c r", p=128))
    k.dma("pool", BeT, BeT[32:64, :], BshT, BshT[:, :])
    for et in range(NE):
        terms = [(env.wsb[:, c * D + et * 128: c * D + (et + 1) * 128], sel_sb[:, c * 32:(c + 1) * 32],
                  [env.wsb, sel_sb]) for c in range(4)]
        env.mm_group(env.pl[0][:, 0:32], env.pl[0], terms)
        k.emit("dve", lambda e, et=et: e.tensor_copy(out=AeT[:, et * 32:(et + 1) * 32], in_=env.pl[0][:, 0:32]),
               reads=[env.pl[0]], writes=[AeT])
    for i in range(4):
        k.dma("pool", env.wsb, env.wsb[:, i * D:(i + 1) * D], LBT, LBT[i * 128:(i + 1) * 128, :])
    for oc in range(ND):
        terms = [(sel_sb[:, c * 32:(c + 1) * 32], env.wsb[:, c * D + oc * 512: c * D + (oc + 1) * 512],
                  [env.wsb, sel_sb]) for c in range(4)]
        env.mm_group(env.pl[1][0:32, :], env.pl[1], terms)
        k.emit("dve", lambda e, oc=oc: e.tensor_copy(out=BeT[0:32, oc * 512:(oc + 1) * 512], in_=env.pl[1][0:32, :]),
               reads=[env.pl[1]], writes=[BeT])

    env.load_actT(x1Tsc)
    for h in range(NTOK // 512):
        terms = [(AeT[:, e * 32:(e + 1) * 32], env.actT[:, e * NTOK + h * 512: e * NTOK + h * 512 + 512],
                  [AeT, env.actT]) for e in range(NE)]
        env.mm_group(env.pl[0][0:32, :], env.pl[0], terms)
        k.emit("dve", lambda e, h=h: e.tensor_copy(out=AxT[0:32, h * 512:(h + 1) * 512], in_=env.pl[0][0:32, :]),
               reads=[env.pl[0]], writes=[AxT])
        terms = [(AshT_sb[:, e * 32:(e + 1) * 32], env.actT[:, e * NTOK + h * 512: e * NTOK + h * 512 + 512],
                  [AshT_sb, env.actT]) for e in range(NE)]
        env.mm_group(env.pl[1][0:32, :], env.pl[1], terms)
        k.emit("dve", lambda e, h=h: e.tensor_copy(out=AxT[32:64, h * 512:(h + 1) * 512], in_=env.pl[1][0:32, :]),
               reads=[env.pl[1]], writes=[AxT])

    env.load_bias_row(fbias)
    def lora_terms(gt, d):
        return [(AxT[0:64, gt * 128:(gt + 1) * 128], BeT[0:64, d * 512:(d + 1) * 512], [AxT, BeT])]
    env.big_linear(kernW, x1sc, x2rows, x2T, add_bias=True, extra_terms=lora_terms, ln_eps=1e-5)
    k.merge_tokens(env.arena, [AeT, AshT_sb, BeT, AxT, sel_sb])
    return [k.out_token(x2rows), k.out_token(x2T)]


def build_FUSED():
    nc = bass.Bass("TRN2", target_bir_lowering=False)
    k = K(nc)
    io = dict(
        xA=k.dram_in("xA", [S, D], F32),
        WqkvT=k.dram_in("WqkvT", [D, 1536], F32),
        hp1=k.dram_in("hp1", [1024, 128], F32),
        hp2=k.dram_in("hp2", [1024, 128], F32),
        hb1=k.dram_in("hb1", [128, 16], F32),
        cosf=k.dram_in("cosf", [S, 128], F32),
        sinm=k.dram_in("sinm", [S, 128], F32),
        qrT=k.dram_tmp("qrT", [1024, S], BF16),
        qhT=k.dram_tmp("qhT", [1024, S], BF16),
        krT=k.dram_tmp("krT", [256, S], BF16),
        khT=k.dram_tmp("khT", [1024, S], BF16),
        v=k.dram_tmp("v", [2 * S, 128], BF16),
        WhT=k.dram_tmp("WhT", [D, 1536], BF16),
        WlT=k.dram_tmp("WlT", [D, 1536], BF16),
        aoT=k.dram_tmp("aoT", [1024, S], BF16),
        WoTloc=k.dram_in("WoTloc", [1024, D], F32),
        ypart=k.dram_tmp("ypart", [S, D], F32),
        ymy=k.dram_tmp("ymy", [NTOK, D], F32),
        xrows=k.dram_in("xrows", [NTOK, D], F32),
        kernW=k.dram_in("kernW", [D, D], F32),
        fbias=k.dram_in("fbias", [1, D], F32),
        LA=k.dram_in("LA", [512, D], F32),
        LBT=k.dram_in("LBT", [512, D], F32),
        SelAB=k.dram_in("SelAB", [512, 32], F32),
        AshT=k.dram_in("AshT", [D, 32], F32),
        BshT=k.dram_in("BshT", [32, D], F32),
        x1sc=k.dram_tmp("x1sc", [NTOK, D], F32),
        x1Tsc=k.dram_tmp("x1Tsc", [D, NTOK], BF16),
        x2rows=k.dram_tmp("x2rows", [NTOK, D], F32),
        x2T=k.dram_tmp("x2T", [D, NTOK], BF16),
        qkv_wT=k.dram_in("qkv_wT", [D, 3 * D], F32),
        ddtab=k.dram_in("ddtab", [32 * 256, 256], F32),
        qdtab=k.dram_in("qdtab", [32 * 128, 2], F32),
        kdtab=k.dram_in("kdtab", [32 * 128, 2], F32),
        attnp=k.dram_tmp("attnp", [NTOK, D], F32),
        kvloc=k.dram_tmp("kvloc", [D, 128], F32),
        cqTs=k.dram_tmp("cqTs", [D, NTOK], BF16),
        kvall=k.dram_tmp("kvall", [8 * D, 128], F32),
        qdbtab=k.dram_in("qdbtab", [32 * 128, 2 * NBLK], F32),
        Wkvb=k.dram_in("Wkvb", [32 * 128, 8], F32),
        attn_rows=k.dram_tmp("attn_rows", [NTOK, D], F32),
        nwcol=k.dram_in("nwcol", [128, 32], F32),
        gate_wT=k.dram_in("gate_wT", [D, D], F32),
        out_wT=k.dram_in("out_wT", [D, D], F32),
        W_baseT=k.dram_in("W_baseT", [D, D], F32),
        b_base=k.dram_in("b_base", [1, D], F32),
        encT=k.dram_in("encT", [D, 40], F32),
        CW0=k.dram_in("CW0", [3 * D, 32], F32),
        MW0=k.dram_in("MW0", [3 * D, 32], U8),
        CW1=k.dram_in("CW1", [3 * D, 8], F32),
        MW1=k.dram_in("MW1", [3 * D, 8], U8),
        Kmat=k.dram_in("Kmat", [D, D], F32),
        outrows=k.dram_out("outrows", [NTOK, D], F32),
        x3sc=k.dram_tmp("x3sc", [NTOK, D], F32),
        x3Tsc=k.dram_tmp("x3Tsc", [D, NTOK], BF16),
        y0Tsc=k.dram_tmp("y0Tsc", [D, NTOK], BF16),
        Hsc=k.dram_tmp("Hsc", [40, D], BF16),
    )
    import math
    HEADS = 32
    base = 1.0 / 2.0 ** (8.0 / HEADS)
    factor = 1.0 - 1 / (32 - 1 + 1e-05) + 1e-05
    bd = [math.exp(-(base ** (h + 1) * factor) * BS) for h in range(HEADS)]
    with nc.Block() as blk:
        env = Env(k)
        emit_stageA1(env, io)
        emit_stageA2(env, io)
        emit_stageW(env, io)
        k.emit("pool", lambda e: e.collective_compute(
            "ReduceScatter", OP.add, replica_groups=[[0, 1, 2, 3], [4, 5, 6, 7]],
            ins=[io["ypart"].ap().opt()], outs=[io["ymy"].ap().opt()]),
            reads=[io["ypart"]], writes=[io["ymy"]], dma=True)
        emit_lnrows(env, io)
        emit_stageB_fused(env, io)
        emit_stageC1(env, io, dict(bd=bd))
        k.emit("pool", lambda e: e.collective_compute(
            "AllGather", OP.bypass, replica_groups=[[0, 1, 2, 3, 4, 5, 6, 7]],
            ins=[io["kvloc"].ap().opt()], outs=[io["kvall"].ap().opt()]),
            reads=[io["kvloc"]], writes=[io["kvall"]], dma=True)
        emit_stageC2(env, io, dict())
        fw = emit_stageD(env, io)
        k.finalize(blk, final_waits=fw)
    k.close()
    return k



from concourse.bass_utils import run_bass_kernel_spmd
import math as _math


def _rope_tables():
    Dh = 128
    inv_freq = 1.0 / (10000.0 ** (np.arange(0, Dh, 2, dtype=np.float32) / Dh))
    t = np.arange(S, dtype=np.float32)
    freqs = np.outer(t, inv_freq)
    emb = np.concatenate([freqs, freqs], -1)
    cosf = np.cos(emb).astype(np.float32)
    sinf = np.sin(emb).astype(np.float32)
    sinm = sinf.copy()
    sinm[:, :64] *= -1.0
    return cosf, sinm


def _decay_tables():
    HEADS, BLK = 32, 256
    base = 1.0 / 2.0 ** (8.0 / HEADS)
    factor = 1.0 - 1 / (32 - 1 + 1e-05) + 1e-05
    slope = np.array([base ** (h + 1) * factor for h in range(HEADS)], np.float64)
    r = np.arange(BLK, dtype=np.float64) + 1.0
    qd = np.exp(-slope[:, None] * r[None, :]).astype(np.float32)
    kd = np.exp(-slope[:, None] * (BLK - r)[None, :]).astype(np.float32)
    diff = r[:, None] - r[None, :]
    dd = np.where(diff >= 0, np.exp(-slope[:, None, None] * diff[None, :, :]), 0.0).astype(np.float32)
    bd = np.exp(-slope * BLK)
    qdtab = np.zeros((32 * 128, 2), np.float32)
    kdtab = np.zeros((32 * 128, 2), np.float32)
    qdbtab = np.zeros((32 * 128, 8), np.float32)
    for h in range(32):
        for ss in range(2):
            qdtab[h * 128:(h + 1) * 128, ss] = qd[h, ss * 128:(ss + 1) * 128]
            kdtab[h * 128:(h + 1) * 128, ss] = kd[h, ss * 128:(ss + 1) * 128]
            for j in range(4):
                qdbtab[h * 128:(h + 1) * 128, j * 2 + ss] = qd[h, ss * 128:(ss + 1) * 128] * (bd[h] ** j)
    ddtab = np.zeros((32 * 256, 256), np.float32)
    for h in range(32):
        ddtab[h * 256:(h + 1) * 256, :] = dd[h].T
    return qdtab, kdtab, qdbtab, ddtab, bd


def _kmat():
    n = D
    Sm = np.zeros((n, n // 2 + 1), np.complex64)
    Sm[0, 0] = np.sqrt(n)
    ps = np.sqrt(n / 2.0)
    mf = (n - 1) // 2
    idx = np.arange(mf)
    Sm[2 * idx + 1, idx + 1] = ps
    Sm[2 * idx + 2, idx + 1] = -1j * ps
    Sm[n - 1, n // 2] = np.sqrt(n)
    return np.fft.irfft(Sm, n=n, axis=-1).astype(np.float32)


def kernel(**inp):
    C = np.ascontiguousarray
    x0 = np.asarray(inp["L0_attn_hidden_states"], np.float32)
    x0f = x0.reshape(2 * S, D)
    cosf, sinm = _rope_tables()
    qdtab, kdtab, qdbtab, ddtab, bd = _decay_tables()
    Kmat = _kmat()

    Wq, Wk, Wv, Wo = (np.asarray(inp[f"L0_attn_W{n}"], np.float32) for n in "qkvo")
    p1 = np.asarray(inp["L0_attn_hash_proj1"], np.float32)
    p2 = np.asarray(inp["L0_attn_hash_proj2"], np.float32)
    b1 = np.asarray(inp["L0_attn_hash_bias1"], np.float32)
    b2 = np.asarray(inp["L0_attn_hash_bias2"], np.float32)
    WqT, WkT, WvT = Wq.T, Wk.T, Wv.T
    WoT = C(Wo.T)

    gate = np.asarray(inp["L0_ffn_gate"], np.float32)
    lA = np.asarray(inp["L0_ffn_lora_A"], np.float32)
    lB = np.asarray(inp["L0_ffn_lora_B"], np.float32)
    lAs = np.asarray(inp["L0_ffn_lora_A_shared"], np.float32)
    lBs = np.asarray(inp["L0_ffn_lora_B_shared"], np.float32)
    kernW = C(np.asarray(inp["L0_ffn_kernel"], np.float32))
    fbias = np.asarray(inp["L0_ffn_bias"], np.float32).reshape(1, D)
    LA = C(lA.reshape(512, D))
    LBT = C(lB.transpose(0, 2, 1).reshape(512, D))
    AshT = C(lAs.T)
    BshT = C(lBs.T)

    qkv_wT = C(np.asarray(inp["L1_attn_qkv_w"], np.float32).T)
    nwcol = C(np.asarray(inp["L1_attn_norm_w"], np.float32).reshape(32, 128).T)
    gate_wT = C(np.asarray(inp["L1_attn_gate_w"], np.float32).T)
    out_wT = C(np.asarray(inp["L1_attn_out_w"], np.float32).T)
    W_baseT = C(np.asarray(inp["L1_ffn_W_base"], np.float32).T)
    b_base = np.asarray(inp["L1_ffn_b_base"], np.float32).reshape(1, D)
    encT = np.zeros((D, 40), np.float32)
    encT[:, 0:32] = np.asarray(inp["L1_ffn_enc0"], np.float32).T
    encT[:, 32:40] = np.asarray(inp["L1_ffn_enc1"], np.float32).T
    CW0 = C(np.asarray(inp["L1_ffn_coefs0"], np.float32).reshape(3 * D, 32))
    CW1 = C(np.asarray(inp["L1_ffn_coefs1"], np.float32).reshape(3 * D, 8))
    MW0 = C(np.asarray(inp["L1_ffn_mask0"]).reshape(3 * D, 32)).view(np.uint8)
    MW1 = C(np.asarray(inp["L1_ffn_mask1"]).reshape(3 * D, 8)).view(np.uint8)

    maps = []
    for c in range(8):
        b = c // 4
        pos = c % 4
        loc = pos * 1024
        hs = [8 * pos + i for i in range(8)]
        kvs = [2 * pos, 2 * pos + 1]
        WqkvT = C(np.concatenate(
            [WqT[:, h * 128:(h + 1) * 128] for h in hs]
            + [WkT[:, kv * 128:(kv + 1) * 128] for kv in kvs]
            + [WvT[:, kv * 128:(kv + 1) * 128] for kv in kvs], 1))
        hbc = np.zeros((128, 16), np.float32)
        for i, h in enumerate(hs):
            hbc[:, i] = b1[h, 0, :]
            hbc[:, 8 + i] = b2[h, 0, :]
        SelAB = np.zeros((512, 32), np.float32)
        for n_ in range(16):
            for r_ in range(32):
                SelAB[n_ * 32 + r_, r_] = gate[c, n_]
        Wkvb = np.zeros((32 * 128, 8), np.float32)
        for h in range(32):
            for kp in range(pos):
                Wkvb[h * 128:(h + 1) * 128, 4 * b + kp] = (bd[h] ** 4) ** (pos - 1 - kp)
        maps.append(dict(
            m1=dict(
                xA=C(x0[b]), WqkvT=WqkvT,
                hp1=C(np.concatenate([p1[h] for h in hs], 0)),
                hp2=C(np.concatenate([p2[h] for h in hs], 0)),
                hb1=hbc, cosf=cosf, sinm=sinm),
            m2=dict(
                xrows=C(x0f[c * 1024:(c + 1) * 1024]),
                WoT=WoT, kernW=kernW, fbias=fbias, LA=LA, LBT=LBT,
                SelAB=SelAB, AshT=AshT, BshT=BshT,
                qkv_wT=qkv_wT, ddtab=ddtab, qdtab=qdtab, kdtab=kdtab),
            m3=dict(
                qdbtab=qdbtab, Wkvb=Wkvb,
                nwcol=nwcol, gate_wT=gate_wT, out_wT=out_wT, W_baseT=W_baseT,
                b_base=b_base, encT=encT, CW0=CW0, MW0=MW0, CW1=CW1, MW1=MW1,
                Kmat=Kmat)))

    k1 = build_L1()
    r1 = run_bass_kernel_spmd(k1.nc, [m["m1"] for m in maps], core_ids=list(range(8))).results
    aoT_b = [np.concatenate([r1[4 * b + i]["aoT"] for i in range(4)], 0) for b in range(2)]

    maps2 = []
    for c in range(8):
        b, loc = c // 4, (c % 4) * 1024
        m = dict(maps[c]["m2"])
        m["aoTc"] = C(aoT_b[b][:, loc:loc + 1024])
        maps2.append(m)
    k2 = build_L2()
    r2 = run_bass_kernel_spmd(k2.nc, maps2, core_ids=list(range(8))).results

    kvall = np.concatenate([r2[c]["kvloc"] for c in range(8)], 0)
    maps3 = []
    for c in range(8):
        m = dict(maps[c]["m3"])
        m.update(kvall=kvall, attnp=r2[c]["attnp"], cqTs=r2[c]["cqTs"],
                 x2rows=r2[c]["x2rows"], x2T=r2[c]["x2T"])
        maps3.append(m)
    k3 = build_L3()
    r3 = run_bass_kernel_spmd(k3.nc, maps3, core_ids=list(range(8))).results

    out = np.concatenate([r3[c]["outrows"] for c in range(8)], 0)
    return out.reshape(2, S, D).astype(np.float32)


# version 04 (final)
# speedup vs baseline: 1.0197x; 1.0197x over previous
import numpy as np
import concourse.bass as bass
import concourse.mybir as mybir


F32 = mybir.dt.float32
BF16 = mybir.dt.bfloat16
U8 = mybir.dt.uint8
AF = mybir.ActivationFunctionType
OP = mybir.AluOpType
AX = mybir.AxisListType

ENGS = ("pe", "act", "dve", "pool", "sp")


class Buf:
    __slots__ = ("name", "ap_", "writers", "readers", "handle", "chkey")

    def __init__(self, name, handle):
        self.name = name
        self.handle = handle
        self.ap_ = None
        self.writers = []
        self.readers = []
        self.chkey = None

    def ap(self):
        if self.ap_ is None:
            a = getattr(self.handle, "ap", None)
            self.ap_ = self.handle.ap() if callable(a) else self.handle
        return self.ap_

    def __getitem__(self, idx):
        return self.ap()[idx]


class K:
    def __init__(self, nc: bass.Bass):
        self.nc = nc
        self.eng = {
            "pe": nc.tensor, "act": nc.scalar, "dve": nc.vector,
            "pool": nc.gpsimd, "sp": nc.sync,
        }
        self._ctxs = []
        self.sem = {}
        self.cnt = {}
        for e in ENGS:
            for kind in ("C", "D"):
                key = e + kind
                sem = nc.semaphore(f"s_{key}").__enter__()
                self.sem[key] = sem
                self.cnt[key] = 0
        self.idx = {e: 0 for e in ENGS}
        self.accum_last = {}
        self.waited = {e: {} for e in ENGS}
        self.ops = {e: [] for e in ENGS}
        self._cc_sems = []
        self.stats = {"waits": 0, "instrs": 0}

    def sbuf(self, name, shape, dtype):
        g = self.nc.sbuf_tensor(name, list(shape), dtype)
        h = g.__enter__(); self._ctxs.append(g)
        return Buf(name, h)

    def psum(self, name, shape, dtype=F32):
        g = self.nc.psum_tensor(name, list(shape), dtype)
        h = g.__enter__(); self._ctxs.append(g)
        return Buf(name, h)

    def dram_in(self, name, shape, dtype=F32):
        return Buf(name, self.nc.dram_tensor(name, list(shape), dtype, kind="ExternalInput").ap())

    def dram_out(self, name, shape, dtype=F32):
        return Buf(name, self.nc.dram_tensor(name, list(shape), dtype, kind="ExternalOutput").ap())

    def dram_tmp(self, name, shape, dtype=F32):
        return Buf(name, self.nc.dram_tensor(name, list(shape), dtype).ap())

    def alias(self, name, arena, shape, dtype, offset_bytes=0):
        import concourse.mybir as _mb
        h = arena.handle
        esz = _mb.dt.size(dtype)
        assert offset_bytes % esz == 0
        hb = h.bitcast(dtype)
        start = offset_bytes // esz
        n = 1
        for s in shape[1:]:
            n *= s
        apv = hb[:shape[0], start:start + n]
        if len(shape) > 2:
            pat = "p (" + " ".join(f"d{i}" for i in range(1, len(shape))) + ") -> p " +                   " ".join(f"d{i}" for i in range(1, len(shape)))
            kw = {f"d{i}": shape[i] for i in range(1, len(shape))}
            apv = apv.rearrange(pat, **kw)
        b = Buf(name, arena.handle)
        b.ap_ = apv
        b.writers = list(arena.writers)
        b.readers = list(arena.readers)
        return b

    def merge_tokens(self, arena, subbufs):
        ws, rs = list(arena.writers), list(arena.readers)
        for b in subbufs:
            ws.extend(b.writers)
            rs.extend(b.readers)
        arena.writers = ws
        arena.readers = rs

    def new_sem(self, name):
        sem = self.nc.semaphore(name).__enter__()
        self._cc_sems.append(sem)
        return sem

    def _wait(self, e, tok, allow_self=True, cur_dma=False):
        key, val = tok
        if key == e + "C" and e == "pe":
            return
        if key == e + "C" and not allow_self:
            return
        w = self.waited[e]
        if w.get(key, 0) >= val:
            return
        w[key] = val
        sem = self.sem[key]
        self.ops[e].append(lambda eng, sem=sem, val=val: eng.wait_ge(sem, val))
        self.stats["waits"] += 1

    def _channel(self, buf):
        if buf.chkey is None:
            key = "ch_" + buf.name
            sem = self.nc.semaphore(key).__enter__()
            self._cc_sems.append(sem)
            self.sem[key] = sem
            self.cnt[key] = 0
            buf.chkey = key
        return buf.chkey

    def emit(self, e, fn, reads=(), writes=(), inc=1, dma=False, accum=(), relax_reads=False):
        for r in reads:
            al = self.accum_last.get(id(r))
            if al is not None and al[0] == e and al[1] == self.idx[e] - 1:
                raise RuntimeError(f"accum_out hazard: {r.name} read by next same-engine instr on {e}")
        for r in reads:
            for wt in r.writers:
                self._wait(e, wt, allow_self=not relax_reads)
        for wbuf in writes:
            for wt in wbuf.writers:
                self._wait(e, wt)
            for rd in wbuf.readers:
                self._wait(e, rd)
        if dma:
            assert inc, "every DMA must increment its channel semaphore"
            assert len(writes) == 1, "DMA must have exactly one destination buffer"
            key = self._channel(writes[0])
            self.cnt[key] += 16
            tok = (key, self.cnt[key])
            sem = self.sem[key]
            self.ops[e].append(lambda eng, fn=fn, sem=sem: fn(eng).then_inc(sem, 16))
        elif inc:
            key = e + "C"
            self.cnt[key] += 1
            tok = (key, self.cnt[key])
            sem = self.sem[key]
            self.ops[e].append(lambda eng, fn=fn, sem=sem: fn(eng).then_inc(sem, 1))
        else:
            key = e + "C"
            self.ops[e].append(lambda eng, fn=fn: fn(eng))
            tok = (key, self.cnt[key] + 1)
        self.stats["instrs"] += 1
        self.idx[e] += 1
        for r in reads:
            r.readers.append(tok)
        accum_ids = {id(a) for a in accum}
        for wbuf in writes:
            wbuf.writers = [tok]
            wbuf.readers = []
            if id(wbuf) not in accum_ids:
                self.accum_last.pop(id(wbuf), None)
        for abuf in accum:
            self.accum_last[id(abuf)] = (e, self.idx[e] - 1)
        return tok

    def dma(self, e, dst_buf, dst_ap, src_buf, src_ap, inc=1, **kw):
        return self.emit(
            e, lambda eng: eng.dma_start(out=dst_ap, in_=src_ap, **kw),
            reads=[src_buf], writes=[dst_buf], inc=inc, dma=True)

    def raw_wait(self, e, tok):
        self._wait(e, tok)

    def barrier_token(self, e, kind="C"):
        return (e + kind, self.cnt[e + kind])

    def out_token(self, buf):
        assert buf.chkey is not None, f"{buf.name} has no DMA channel"
        return (buf.chkey, self.cnt[buf.chkey])

    def finalize(self, block, final_waits=()):
        for tok in final_waits:
            self._wait("pool", tok)
        blockmap = {"pe": block.tensor, "act": block.scalar, "dve": block.vector,
                    "pool": block.gpsimd, "sp": block.sync}
        for e in ENGS:
            ops = self.ops[e]
            def make(ops=ops):
                def body(eng):
                    for f in ops:
                        f(eng)
                return body
            blockmap[e](make())

    def close(self):
        for h in reversed(self._ctxs):
            h.__exit__(None, None, None)



def bcast_row(k: K, dst, src_row_ap, src_buf, n, ones_stat, psum_bank, dtype_note=""):
    raise NotImplementedError


def ceil_div(a, b):
    return (a + b - 1) // b



D = 4096
NTOK = 1024
NT = NTOK // 128
NE = D // 128
ND = D // 512
HALF = NT // 2


class Env:

    def __init__(self, k: K):
        self.k = k
        self.idn = self._identity("idn", BF16)
        self.idnf = self._identity("idnf", F32)
        self.actT = k.sbuf("actT", [128, NE * NTOK], BF16)
        self.wsb = k.sbuf("wsb", [128, NE * 512], BF16)
        self.xpre = [k.sbuf(f"xpre{i}", [128, D], F32) for i in range(HALF)]
        self.xres = [k.sbuf(f"xres{i}", [128, 512], F32) for i in range(4)]
        self.sums = [k.sbuf(f"sums{i}", [128, ND], F32) for i in range(HALF)]
        self.sqs = [k.sbuf(f"sqs{i}", [128, ND], F32) for i in range(HALF)]
        self.sqjunk = k.sbuf("sqjunk", [128, 512], F32)
        self.mean = k.sbuf("mean", [128, 1], F32)
        self.rstd = k.sbuf("rstd", [128, 1], F32)
        self.lntmp = k.sbuf("lntmp", [128, 1], F32)
        self.junk8 = k.sbuf("junk8", [128, ND], F32)
        self.tstage = k.sbuf("tstage", [128, D], BF16)
        self.ones_col = k.sbuf("ones_col", [128, 128], BF16)
        self.brow = k.sbuf("brow", [128, D], BF16)
        self.arena = k.sbuf("arena", [128, 8 * 1024], BF16)
        self.pmm = [k.psum(f"pmm{i}", [128, 512], F32) for i in range(2)]
        self.ptrf = k.psum("ptrf", [128, 512], F32)
        self.ptrb = k.psum("ptrb", [128, 1024], BF16)
        self.pl = [k.psum(f"pl{i}", [128, 512], F32) for i in range(2)]
        k.emit("pool", lambda e: e.memset(self.ones_col[:], 0.0), writes=[self.ones_col])
        k.emit("pool", lambda e: e.memset(self.ones_col[0:1, :], 1.0), writes=[self.ones_col])

    def _identity(self, name, dtype):
        k = self.k
        idn = k.sbuf(name, [128, 128], dtype)
        k.emit("pool", lambda e: e.memset(idn[:], 0.0), writes=[idn])
        k.emit("pool", lambda e: e.affine_select(out=idn[:], in_=idn[:], compare_op=OP.not_equal,
                                                 fill=1.0, base=0, pattern=[[-1, 128]], channel_multiplier=1),
               reads=[idn], writes=[idn])
        return idn

    def mm_group(self, psum_ap, psum_buf, terms):
        k = self.k
        n = len(terms)
        tok = None
        for i, (stat, mov, rbufs) in enumerate(terms):
            tok = k.emit("pe",
                         lambda e, stat=stat, mov=mov, i=i, n=n: e.matmul(
                             psum_ap, stat, mov, start=(i == 0), stop=(i == n - 1)),
                         reads=list(rbufs), writes=[psum_buf],
                         inc=1 if i == n - 1 else 0)
        return tok

    def ln_stats(self, sums, sqs, dinv, eps):
        k = self.k
        mean, rstd, tmp, junk8 = self.mean, self.rstd, self.lntmp, self.junk8
        k.emit("act", lambda e: e.activation(out=junk8[:], in_=sums[:], func=AF.Identity, accum_out=mean[:]),
               reads=[sums], writes=[junk8, mean], accum=[mean])
        k.emit("dve", lambda e: e.tensor_reduce(out=tmp[:], in_=sqs[:], axis=AX.X, op=OP.add),
               reads=[sqs], writes=[tmp])
        k.emit("pool", lambda e: e.tensor_scalar(out=mean[:], in0=mean[:], scalar1=dinv, scalar2=None,
                                                 op0=OP.mult, op1=OP.bypass), reads=[mean], writes=[mean])
        k.emit("pool", lambda e: e.tensor_scalar(out=tmp[:], in0=tmp[:], scalar1=dinv, scalar2=None,
                                                 op0=OP.mult, op1=OP.bypass), reads=[tmp], writes=[tmp])
        k.emit("pool", lambda e: e.tensor_tensor(out=rstd[:], in0=mean[:], in1=mean[:], op=OP.mult),
               reads=[mean], writes=[rstd])
        k.emit("pool", lambda e: e.tensor_tensor(out=tmp[:], in0=tmp[:], in1=rstd[:], op=OP.subtract),
               reads=[tmp, rstd], writes=[tmp])
        k.emit("pool", lambda e: e.tensor_scalar(out=tmp[:], in0=tmp[:], scalar1=float(eps), scalar2=None,
                                                 op0=OP.add, op1=OP.bypass), reads=[tmp], writes=[tmp])
        k.emit("act", lambda e: e.activation(out=rstd[:], in_=tmp[:], func=AF.Sqrt),
               reads=[tmp], writes=[rstd])
        k.emit("dve", lambda e: e.reciprocal(out=rstd[:], in_=rstd[:]), reads=[rstd], writes=[rstd])

    def load_actT(self, src_hbm):
        k = self.k
        for q in range(4):
            k.dma("sp", self.actT,
                  self.actT.ap().rearrange("p (c t) -> p c t", t=NTOK)[:, q * 8:(q + 1) * 8, :],
                  src_hbm, src_hbm.ap().rearrange("(c p) t -> p c t", p=128)[:, q * 8:(q + 1) * 8, :])

    def transpose_out(self, src_tile_buf, src_ap_f32, gt, xT_out):
        k = self.k
        for eg in range(NE // 4):
            for j in range(4):
                et = eg * 4 + j
                k.emit("pe", lambda e, et=et, j=j: e.transpose(
                    self.ptrf[:, j * 128:(j + 1) * 128], src_ap_f32[:, et * 128:(et + 1) * 128], self.idnf[:]),
                    reads=[src_tile_buf, self.idnf], writes=[self.ptrf], inc=1 if j == 3 else 0)
            k.emit("dve", lambda e, eg=eg: e.tensor_copy(
                out=self.tstage[:, eg * 512:(eg + 1) * 512], in_=self.ptrf[:, :]),
                reads=[self.ptrf], writes=[self.tstage])
        k.dma("sp", xT_out,
              xT_out.ap().rearrange("(c p) t -> p c t", p=128)[:, :, gt * 128:(gt + 1) * 128],
              self.tstage, self.tstage.ap().rearrange("p (c t) -> p c t", t=128))

    def load_weight_chunk(self, W, d):
        k = self.k
        for q in range(4):
            k.dma("pool", self.wsb,
                  self.wsb.ap().rearrange("p (c o) -> p c o", o=512)[:, q * 8:(q + 1) * 8, :],
                  W, W.ap().rearrange("(c p) o -> p c o", p=128)[:, q * 8:(q + 1) * 8, d * 512:(d + 1) * 512])

    def load_bias_row(self, bias_hbm):
        self.k.dma("pool", self.brow, self.brow[0:1, :], bias_hbm, bias_hbm[0:1, :])

    def big_linear(self, W, res_rows, x_out_rows, xT_out, add_bias=False, extra_terms=None,
                   ln_eps=1e-5, epilogue=None):
        k = self.k
        for g in range(2):
            for d in range(ND):
                self.load_weight_chunk(W, d)
                for t in range(HALF):
                    gt = g * HALF + t
                    pm = self.pmm[t % 2]
                    terms = [(self.actT[:, e * NTOK + gt * 128: e * NTOK + gt * 128 + 128],
                              self.wsb[:, e * 512:(e + 1) * 512], [self.actT, self.wsb]) for e in range(NE)]
                    if add_bias:
                        terms.append((self.ones_col[0:1, :], self.brow[0:1, d * 512:(d + 1) * 512],
                                      [self.ones_col, self.brow]))
                    if extra_terms is not None:
                        terms.extend(extra_terms(gt, d))
                    self.mm_group(pm[:, :], pm, terms)
                    xr = self.xres[t]
                    k.dma("sp", xr, xr[:, :], res_rows, res_rows[gt * 128:(gt + 1) * 128, d * 512:(d + 1) * 512])
                    k.emit("dve", lambda e, t=t, d=d, pm=pm, xr=xr: e.scalar_tensor_tensor(
                        out=self.xpre[t][:, d * 512:(d + 1) * 512], in0=pm[:, :], scalar=1.0, in1=xr[:, :],
                        op0=OP.mult, op1=OP.add, accum_out=self.sums[t][:, d:d + 1]),
                        reads=[pm, xr], writes=[self.xpre[t], self.sums[t]], accum=[self.sums[t]])
                    k.emit("act", lambda e, t=t, d=d: e.activation(
                        out=self.sqjunk[:], in_=self.xpre[t][:, d * 512:(d + 1) * 512], func=AF.Square,
                        accum_out=self.sqs[t][:, d:d + 1]),
                        reads=[self.xpre[t]], writes=[self.sqjunk, self.sqs[t]], accum=[self.sqs[t]])
            for t in range(HALF):
                gt = g * HALF + t
                self.ln_stats(self.sums[t], self.sqs[t], 1.0 / D, ln_eps)
                k.emit("dve", lambda e, t=t: e.tensor_scalar(
                    out=self.xpre[t][:, :], in0=self.xpre[t][:, :], scalar1=self.mean[:], scalar2=self.rstd[:],
                    op0=OP.subtract, op1=OP.mult),
                    reads=[self.xpre[t], self.mean, self.rstd], writes=[self.xpre[t]])
                k.dma("sp", x_out_rows, x_out_rows[gt * 128:(gt + 1) * 128, :], self.xpre[t], self.xpre[t][:, :])
                if xT_out is not None:
                    self.transpose_out(self.xpre[t], self.xpre[t].ap(), gt, xT_out)



def emit_stageB(env: Env, io):
    k = env.k
    aoTc, xrows = io["aoTc"], io["xrows"]
    WoT, kernW, fbias = io["WoT"], io["kernW"], io["fbias"]
    LA, LBT, SelAB, AshT, BshT = io["LA"], io["LBT"], io["SelAB"], io["AshT"], io["BshT"]
    x2rows, x2T, x1sc, x1Tsc = io["x2rows"], io["x2T"], io["x1sc"], io["x1Tsc"]

    AeT = k.alias("AeT", env.arena, [128, NE * 32], BF16, 0)
    AshT_sb = k.alias("AshT_sb", env.arena, [128, NE * 32], BF16, 2 * 1024)
    BeT = k.alias("BeT", env.arena, [128, D], BF16, 4 * 1024)
    AxT = k.alias("AxT", env.arena, [128, NTOK], BF16, 12 * 1024)
    sel_sb = k.alias("sel_sb", env.arena, [128, 4 * 32], BF16, 14 * 1024)

    for i in range(4):
        k.dma("pool", env.wsb, env.wsb[:, i * D:(i + 1) * D], LA, LA[i * 128:(i + 1) * 128, :])
    k.dma("pool", sel_sb, sel_sb.ap().rearrange("p (c r) -> p c r", r=32),
          SelAB, SelAB.ap().rearrange("(c p) r -> p c r", p=128))
    k.dma("pool", AshT_sb, AshT_sb.ap().rearrange("p (c r) -> p c r", r=32),
          AshT, AshT.ap().rearrange("(c p) r -> p c r", p=128))
    k.dma("pool", BeT, BeT[32:64, :], BshT, BshT[:, :])
    for et in range(NE):
        terms = [(env.wsb[:, c * D + et * 128: c * D + (et + 1) * 128], sel_sb[:, c * 32:(c + 1) * 32],
                  [env.wsb, sel_sb]) for c in range(4)]
        env.mm_group(env.pl[0][:, 0:32], env.pl[0], terms)
        k.emit("dve", lambda e, et=et: e.tensor_copy(out=AeT[:, et * 32:(et + 1) * 32], in_=env.pl[0][:, 0:32]),
               reads=[env.pl[0]], writes=[AeT])
    for i in range(4):
        k.dma("pool", env.wsb, env.wsb[:, i * D:(i + 1) * D], LBT, LBT[i * 128:(i + 1) * 128, :])
    for oc in range(ND):
        terms = [(sel_sb[:, c * 32:(c + 1) * 32], env.wsb[:, c * D + oc * 512: c * D + (oc + 1) * 512],
                  [env.wsb, sel_sb]) for c in range(4)]
        env.mm_group(env.pl[1][0:32, :], env.pl[1], terms)
        k.emit("dve", lambda e, oc=oc: e.tensor_copy(out=BeT[0:32, oc * 512:(oc + 1) * 512], in_=env.pl[1][0:32, :]),
               reads=[env.pl[1]], writes=[BeT])

    env.load_actT(aoTc)
    env.big_linear(WoT, xrows, x1sc, x1Tsc, add_bias=False, ln_eps=1e-5)

    env.load_actT(x1Tsc)
    for h in range(NTOK // 512):
        terms = [(AeT[:, e * 32:(e + 1) * 32], env.actT[:, e * NTOK + h * 512: e * NTOK + h * 512 + 512],
                  [AeT, env.actT]) for e in range(NE)]
        env.mm_group(env.pl[0][0:32, :], env.pl[0], terms)
        k.emit("dve", lambda e, h=h: e.tensor_copy(out=AxT[0:32, h * 512:(h + 1) * 512], in_=env.pl[0][0:32, :]),
               reads=[env.pl[0]], writes=[AxT])
        terms = [(AshT_sb[:, e * 32:(e + 1) * 32], env.actT[:, e * NTOK + h * 512: e * NTOK + h * 512 + 512],
                  [AshT_sb, env.actT]) for e in range(NE)]
        env.mm_group(env.pl[1][0:32, :], env.pl[1], terms)
        k.emit("dve", lambda e, h=h: e.tensor_copy(out=AxT[32:64, h * 512:(h + 1) * 512], in_=env.pl[1][0:32, :]),
               reads=[env.pl[1]], writes=[AxT])

    env.load_bias_row(fbias)
    def lora_terms(gt, d):
        return [(AxT[0:64, gt * 128:(gt + 1) * 128], BeT[0:64, d * 512:(d + 1) * 512], [AxT, BeT])]
    env.big_linear(kernW, x1sc, x2rows, x2T, add_bias=True, extra_terms=lora_terms, ln_eps=1e-5)
    k.merge_tokens(env.arena, [AeT, AshT_sb, BeT, AxT, sel_sb])
    return [k.out_token(x2rows), k.out_token(x2T)]



def emit_stageD(env: Env, io):
    k = env.k
    attn_rows = io["attn_rows"]
    x2rows, x2T = io["x2rows"], io["x2T"]
    nwcol = io["nwcol"]
    gate_wT, out_wT, W_baseT = io["gate_wT"], io["out_wT"], io["W_baseT"]
    b_base = io["b_base"]
    encT = io["encT"]
    CW0, MW0 = io["CW0"], io["MW0"]
    CW1, MW1 = io["CW1"], io["MW1"]
    Kmat = io["Kmat"]
    outrows = io["outrows"]
    x3sc, x3Tsc, y0Tsc, Hsc = io["x3sc"], io["x3Tsc"], io["y0Tsc"], io["Hsc"]

    Wsum = k.alias("Wsum", env.arena, [128, NE * 40], BF16, 0)
    c0f = k.alias("c0f", env.arena, [128, 40], F32, 4096)
    m0f = k.alias("m0f", env.arena, [128, 40], F32, 4608)
    acc0 = k.alias("acc0", env.arena, [128, 40], F32, 5120)
    Hst = k.alias("Hst", env.arena, [128, 512], BF16, 5632)
    nwsb = k.sbuf("nwsb", [128, 32], F32)
    k.dma("sp", nwsb, nwsb[:, :], nwcol, nwcol[:, :])

    for et in range(NE):
        for eidx in range(3):
            k.dma("pool", c0f, c0f[:, 0:32], CW0, CW0[eidx * D + et * 128: eidx * D + (et + 1) * 128, :])
            k.dma("pool", c0f, c0f[:, 32:40], CW1, CW1[eidx * D + et * 128: eidx * D + (et + 1) * 128, :])
            k.dma("pool", m0f, m0f[:, 0:32], MW0, MW0[eidx * D + et * 128: eidx * D + (et + 1) * 128, :])
            k.dma("pool", m0f, m0f[:, 32:40], MW1, MW1[eidx * D + et * 128: eidx * D + (et + 1) * 128, :])
            if eidx == 0:
                k.emit("dve", lambda e: e.tensor_tensor(out=acc0[:, :], in0=c0f[:, :], in1=m0f[:, :], op=OP.mult),
                       reads=[c0f, m0f], writes=[acc0])
            else:
                k.emit("dve", lambda e: e.tensor_tensor(out=c0f[:, :], in0=c0f[:, :], in1=m0f[:, :], op=OP.mult),
                       reads=[c0f, m0f], writes=[c0f])
                k.emit("dve", lambda e: e.tensor_tensor(out=acc0[:, :], in0=acc0[:, :], in1=c0f[:, :], op=OP.add),
                       reads=[acc0, c0f], writes=[acc0])
        k.emit("dve", lambda e, et=et: e.tensor_copy(out=Wsum[:, et * 40:(et + 1) * 40], in_=acc0[:, :]),
               reads=[acc0], writes=[Wsum])

    for d in range(ND):
        env.load_weight_chunk(Kmat, d)
        terms = [(Wsum[:, e * 40:(e + 1) * 40], env.wsb[:, e * 512:(e + 1) * 512], [Wsum, env.wsb])
                 for e in range(NE)]
        env.mm_group(env.pl[0][0:40, :], env.pl[0], terms)
        k.emit("dve", lambda e: e.tensor_scalar(out=Hst[0:40, :], in0=env.pl[0][0:40, :],
                                                scalar1=0.5, scalar2=None, op0=OP.mult, op1=OP.bypass),
               reads=[env.pl[0]], writes=[Hst])
        k.dma("sp", Hsc, Hsc[:, d * 512:(d + 1) * 512], Hst, Hst[0:40, :])

    k.merge_tokens(env.arena, [Wsum, c0f, m0f, acc0, Hst])
    gateb = [k.alias(f"gateb{i}", env.arena, [128, D], BF16, i * 8 * 1024) for i in range(2)]
    env.load_actT(x2T)
    for g in range(2):
        for tp in range(2):
            for d in range(ND):
                env.load_weight_chunk(gate_wT, d)
                for u in range(2):
                    t = tp * 2 + u
                    gt = g * HALF + t
                    pm = env.pmm[u]
                    terms = [(env.actT[:, e * NTOK + gt * 128: e * NTOK + gt * 128 + 128],
                              env.wsb[:, e * 512:(e + 1) * 512], [env.actT, env.wsb]) for e in range(NE)]
                    env.mm_group(pm[:, :], pm, terms)
                    k.emit("act", lambda e, u=u, d=d, pm=pm: e.activation(
                        out=gateb[u][:, d * 512:(d + 1) * 512], in_=pm[:, :], func=AF.Sigmoid),
                        reads=[pm], writes=[gateb[u]])
            for u in range(2):
                t = tp * 2 + u
                gt = g * HALF + t
                xp = env.xpre[u]
                k.dma("sp", xp, xp[:, :], attn_rows, attn_rows[gt * 128:(gt + 1) * 128, :])
                for dch in range(ND):
                    k.emit("act", lambda e, xp=xp, dch=dch, u=u: e.activation(
                        out=env.sqjunk[:], in_=xp[:, dch * 512:(dch + 1) * 512], func=AF.Square,
                        accum_out=env.sqs[u][:, dch:dch + 1]),
                        reads=[xp], writes=[env.sqjunk, env.sqs[u]], accum=[env.sqs[u]])
                k.emit("dve", lambda e, u=u: e.tensor_reduce(out=env.lntmp[:], in_=env.sqs[u][:], axis=AX.X, op=OP.add),
                       reads=[env.sqs[u]], writes=[env.lntmp])
                k.emit("pool", lambda e: e.tensor_scalar(out=env.lntmp[:], in0=env.lntmp[:], scalar1=1.0 / D,
                                                         scalar2=1e-5, op0=OP.mult, op1=OP.add),
                       reads=[env.lntmp], writes=[env.lntmp])
                k.emit("act", lambda e: e.activation(out=env.rstd[:], in_=env.lntmp[:], func=AF.Sqrt),
                       reads=[env.lntmp], writes=[env.rstd])
                k.emit("dve", lambda e: e.reciprocal(out=env.rstd[:], in_=env.rstd[:]), reads=[env.rstd], writes=[env.rstd])
                k.emit("dve", lambda e, xp=xp: e.tensor_scalar(out=env.tstage[:], in0=xp[:, :], scalar1=env.rstd[:],
                                                               scalar2=None, op0=OP.mult, op1=OP.bypass),
                       reads=[xp, env.rstd], writes=[env.tstage])
                k.emit("dve", lambda e, u=u: e.tensor_tensor(out=env.tstage[:], in0=env.tstage[:], in1=gateb[u][:, :],
                                                             op=OP.mult),
                       reads=[env.tstage, gateb[u]], writes=[env.tstage])
                for eg in range(NE // 8):
                    for j in range(8):
                        et = eg * 8 + j
                        k.emit("pe", lambda e, et=et, j=j: e.transpose(
                            env.ptrb[:, j * 128:(j + 1) * 128], env.tstage[:, et * 128:(et + 1) * 128], env.idn[:]),
                            reads=[env.tstage, env.idn], writes=[env.ptrb], inc=1 if j == 7 else 0)
                    for j in range(8):
                        et = eg * 8 + j
                        k.emit("dve", lambda e, et=et, j=j: e.tensor_scalar(
                            out=env.brow[:, et * 128:(et + 1) * 128], in0=env.ptrb[:, j * 128:(j + 1) * 128],
                            scalar1=nwsb[:, et:et + 1], scalar2=None, op0=OP.mult, op1=OP.bypass),
                            reads=[env.ptrb, nwsb], writes=[env.brow])
                k.dma("sp", y0Tsc,
                      y0Tsc.ap().rearrange("(c p) t -> p c t", p=128)[:, :, gt * 128:(gt + 1) * 128],
                      env.brow, env.brow.ap().rearrange("p (c t) -> p c t", t=128))

    env.load_actT(y0Tsc)
    env.big_linear(out_wT, x2rows, x3sc, x3Tsc, add_bias=False, ln_eps=1e-5)

    env.load_actT(x3Tsc)
    k.merge_tokens(env.arena, [gateb[0], gateb[1]])
    eCT = k.alias("eCT", env.arena, [128, NTOK], BF16, 0)
    encsb = k.alias("encsb", env.arena, [128, NE * 40], BF16, 2048)
    Hre = k.alias("Hre", env.arena, [128, D], BF16, 8 * 1024)
    k.dma("pool", encsb, encsb.ap().rearrange("p (c r) -> p c r", r=40),
          encT, encT.ap().rearrange("(c p) r -> p c r", p=128))
    k.dma("sp", Hre, Hre[0:40, :], Hsc, Hsc[:, :])
    for h in range(NTOK // 512):
        terms = [(encsb[:, e * 40:(e + 1) * 40], env.actT[:, e * NTOK + h * 512: e * NTOK + h * 512 + 512],
                  [encsb, env.actT]) for e in range(NE)]
        env.mm_group(env.pl[0][0:40, :], env.pl[0], terms)
        k.emit("dve", lambda e, h=h: e.tensor_copy(out=eCT[0:40, h * 512:(h + 1) * 512], in_=env.pl[0][0:40, :]),
               reads=[env.pl[0]], writes=[eCT])

    env.load_bias_row(b_base)
    def fourier_terms(gt, d):
        return [(eCT[0:40, gt * 128:(gt + 1) * 128], Hre[0:40, d * 512:(d + 1) * 512], [eCT, Hre])]
    env.big_linear(W_baseT, x3sc, outrows, None, add_bias=True, extra_terms=fourier_terms, ln_eps=1e-5)
    k.merge_tokens(env.arena, [eCT, encsb, Hre])
    return [k.out_token(outrows)]



BS = 256
NBLK = NTOK // BS

def emit_stageC1(env: Env, io, decays):
    k = env.k
    x2T, qkv_wT = io["x2T"], io["qkv_wT"]
    ddtab, qdtab, kdtab = io["ddtab"], io["qdtab"], io["kdtab"]
    attnp, kvloc, cqTs = io["attnp"], io["kvloc"], io["cqTs"]
    block_decay = decays["bd"]

    qkv_sb = k.alias("qkv_sb", env.arena, [128, 8 * 768], BF16, 0)
    scT = k.alias("scT", env.arena, [128, 512], BF16, 12 * 1024)
    ddsb = k.alias("ddsb", env.arena, [128, 512], BF16, 13 * 1024)
    cqT_sb = k.alias("cqT_sb", env.arena, [128, 256], BF16, 14 * 1024)
    ckT_sb = k.alias("ckT_sb", env.arena, [128, 256], BF16, 14 * 1024 + 512)
    ckd_sb = k.alias("ckd_sb", env.arena, [128, 512], BF16, 15 * 1024)
    stf = k.sbuf("stf", [128, 128], F32)
    stbf = k.sbuf("stbf", [128, 128], BF16)
    qdk = k.sbuf("qdk", [128, 4], F32)
    att_ev = k.sbuf("att_ev", [128, 128], F32)
    att_ev2 = k.sbuf("att_ev2", [128, 128], F32)

    env.load_actT(x2T)
    for pg in range(16):
        for dc in range(768 // 256):
            pass
        for ci, (c0, cw) in enumerate([(0, 512), (512, 256)]):
            for q in range(4):
                k.dma("pool", env.wsb,
                      env.wsb.ap().rearrange("p (c o) -> p c o", o=512)[:, q * 8:(q + 1) * 8, 0:cw],
                      qkv_wT, qkv_wT.ap().rearrange("(c p) o -> p c o", p=128)[:, q * 8:(q + 1) * 8,
                                                                              pg * 768 + c0: pg * 768 + c0 + cw])
            for tt in range(NT):
                pm = env.pmm[tt % 2]
                terms = [(env.actT[:, e * NTOK + tt * 128: e * NTOK + tt * 128 + 128],
                          env.wsb.ap().rearrange("p (c o) -> p c o", o=512)[:, e, 0:cw],
                          [env.actT, env.wsb]) for e in range(NE)]
                env.mm_group(pm[:, 0:cw], pm, terms)
                k.emit("act", lambda e, pm=pm, tt=tt, c0=c0, cw=cw: e.activation(
                    out=qkv_sb[:, tt * 768 + c0: tt * 768 + c0 + cw], in_=pm[:, 0:cw], func=AF.Silu),
                    reads=[pm], writes=[qkv_sb])
        for hl in range(2):
            h = 2 * pg + hl
            bd = float(block_decay[h])
            qo = hl * 384
            k.dma("sp", qdk, qdk[:, 0:2], qdtab, qdtab[h * 128:(h + 1) * 128, :])
            k.dma("sp", qdk, qdk[:, 2:4], kdtab, kdtab[h * 128:(h + 1) * 128, :])
            k.dma("pool", ddsb, ddsb.ap().rearrange("p (a b) -> p a b", a=2),
                  ddtab, ddtab.ap().rearrange("(h a p) m -> p (h a) m", a=2, p=128)[:, 2 * h:2 * h + 2, :])
            k.emit("pool", lambda e: e.memset(stf[:], 0.0), writes=[stf])
            k.emit("pool", lambda e: e.memset(stbf[:], 0.0), writes=[stbf])
            for j in range(NBLK):
                rb = 2 * j
                for u in range(2):
                    k.emit("pe", lambda e, u=u, rb=rb, qo=qo: e.transpose(
                        env.ptrb[:, u * 128:(u + 1) * 128],
                        qkv_sb[:, (rb + u) * 768 + qo: (rb + u) * 768 + qo + 128], env.idn[:]),
                        reads=[qkv_sb, env.idn], writes=[env.ptrb], inc=0)
                    k.emit("pe", lambda e, u=u, rb=rb, qo=qo: e.transpose(
                        env.ptrb[:, 256 + u * 128: 256 + (u + 1) * 128],
                        qkv_sb[:, (rb + u) * 768 + qo + 128: (rb + u) * 768 + qo + 256], env.idn[:]),
                        reads=[qkv_sb, env.idn], writes=[env.ptrb], inc=1 if u == 1 else 0)
                k.emit("dve", lambda e: e.tensor_copy(out=cqT_sb[:, :], in_=env.ptrb[:, 0:256]),
                       reads=[env.ptrb], writes=[cqT_sb])
                k.emit("dve", lambda e: e.tensor_copy(out=ckT_sb[:, :], in_=env.ptrb[:, 256:512]),
                       reads=[env.ptrb], writes=[ckT_sb])
                k.dma("sp", cqTs, cqTs[h * 128:(h + 1) * 128, j * 256:(j + 1) * 256], cqT_sb, cqT_sb[:, :])
                for mt in range(2):
                    env.mm_group(env.pl[0][:, mt * 256:(mt + 1) * 256] if False else env.pl[0][:, 0:256], env.pl[0],
                                 [(ckT_sb[:, mt * 128:(mt + 1) * 128], cqT_sb[:, :], [ckT_sb, cqT_sb])])
                    k.emit("dve", lambda e, mt=mt: e.tensor_tensor(
                        out=scT[:, mt * 256:(mt + 1) * 256], in0=env.pl[0][:, 0:256],
                        in1=ddsb[:, mt * 256:(mt + 1) * 256], op=OP.mult),
                        reads=[env.pl[0], ddsb], writes=[scT])
                for ss in range(2):
                    pi = env.pmm[ss]
                    terms = [(scT[:, mt * 256 + ss * 128: mt * 256 + ss * 128 + 128],
                              qkv_sb[:, (rb + mt) * 768 + qo + 256: (rb + mt) * 768 + qo + 384],
                              [scT, qkv_sb]) for mt in range(2)]
                    env.mm_group(pi[:, 0:128], pi, terms)
                    env.mm_group(env.pl[1][:, 0:128], env.pl[1],
                                 [(cqT_sb[:, ss * 128:(ss + 1) * 128], stbf[:, :], [cqT_sb, stbf])])
                    k.emit("act", lambda e, pi=pi: e.copy(out=att_ev[:], in_=pi[:, 0:128]),
                           reads=[pi], writes=[att_ev])
                    k.emit("dve", lambda e, ss=ss: e.scalar_tensor_tensor(
                        out=att_ev2[:], in0=env.pl[1][:, 0:128], scalar=qdk[:, ss:ss + 1], in1=att_ev[:],
                        op0=OP.mult, op1=OP.add),
                        reads=[env.pl[1], qdk, att_ev], writes=[att_ev2])
                    k.dma("sp", attnp, attnp[(j * 2 + ss) * 128:(j * 2 + ss + 1) * 128, h * 128:(h + 1) * 128],
                          att_ev2, att_ev2[:, :])
                for u in range(2):
                    k.emit("dve", lambda e, u=u, rb=rb, qo=qo: e.tensor_scalar(
                        out=ckd_sb[:, u * 128:(u + 1) * 128],
                        in0=qkv_sb[:, (rb + u) * 768 + qo + 128: (rb + u) * 768 + qo + 256],
                        scalar1=qdk[:, 2 + u:3 + u], scalar2=None, op0=OP.mult, op1=OP.bypass),
                        reads=[qkv_sb, qdk], writes=[ckd_sb])
                terms = [(ckd_sb[:, u * 128:(u + 1) * 128],
                          qkv_sb[:, (rb + u) * 768 + qo + 256: (rb + u) * 768 + qo + 384],
                          [ckd_sb, qkv_sb]) for u in range(2)]
                env.mm_group(env.ptrf[:, 0:128], env.ptrf, terms)
                k.emit("dve", lambda e, bd=bd: e.scalar_tensor_tensor(
                    out=stf[:], in0=stf[:], scalar=bd, in1=env.ptrf[:, 0:128], op0=OP.mult, op1=OP.add),
                    reads=[stf, env.ptrf], writes=[stf])
                k.emit("dve", lambda e: e.tensor_copy(out=stbf[:], in_=stf[:]), reads=[stf], writes=[stbf])
            k.dma("sp", kvloc, kvloc[h * 128:(h + 1) * 128, :], stf, stf[:, :])
    k.merge_tokens(env.arena, [qkv_sb, scT, ddsb, cqT_sb, ckT_sb, ckd_sb])
    return [k.out_token(attnp), k.out_token(kvloc), k.out_token(cqTs)]


def emit_stageC2(env: Env, io, decays):
    k = env.k
    kvall, attnp, cqTs, qdbtab, Wkvb = io["kvall"], io["attnp"], io["cqTs"], io["qdbtab"], io["Wkvb"]
    attn_rows = io["attn_rows"]
    stf = k.alias("stf2", env.arena, [128, 128], F32, 0)
    stbf = k.alias("stbf2", env.arena, [128, 128], BF16, 512)
    dl = k.alias("dl", env.arena, [128, 128], F32, 768)
    wv = k.alias("wv", env.arena, [128, 8], F32, 1280)
    qdb = k.alias("qdb", env.arena, [128, 2 * NBLK], F32, 1312)
    cqT_sb = k.alias("cqT_sb2", env.arena, [128, NTOK], BF16, 1344)
    ap_in = k.alias("ap_in", env.arena, [128, 128], F32, 3392)
    ap_out = k.alias("ap_out", env.arena, [128, 128], F32, 3904)
    for h in range(32):
        k.dma("sp", wv, wv[:, :], Wkvb, Wkvb[h * 128:(h + 1) * 128, :])
        k.dma("sp", qdb, qdb[:, :], qdbtab, qdbtab[h * 128:(h + 1) * 128, :])
        k.dma("sp", cqT_sb, cqT_sb[:, :], cqTs, cqTs[h * 128:(h + 1) * 128, :])
        k.emit("pool", lambda e: e.memset(stf[:], 0.0), writes=[stf])
        for r in range(8):
            k.dma("pool", dl, dl[:, :], kvall, kvall[r * D + h * 128: r * D + (h + 1) * 128, :])
            k.emit("dve", lambda e, r=r: e.scalar_tensor_tensor(
                out=stf[:], in0=dl[:], scalar=wv[:, r:r + 1], in1=stf[:], op0=OP.mult, op1=OP.add),
                reads=[dl, wv, stf], writes=[stf])
        k.emit("dve", lambda e: e.tensor_copy(out=stbf[:], in_=stf[:]), reads=[stf], writes=[stbf])
        for j in range(NBLK):
            for ss in range(2):
                env.mm_group(env.pl[0][:, 0:128], env.pl[0],
                             [(cqT_sb[:, j * 256 + ss * 128: j * 256 + ss * 128 + 128], stbf[:, :],
                               [cqT_sb, stbf])])
                row = (j * 2 + ss) * 128
                k.dma("sp", ap_in, ap_in[:, :], attnp, attnp[row:row + 128, h * 128:(h + 1) * 128])
                k.emit("dve", lambda e, j=j, ss=ss: e.scalar_tensor_tensor(
                    out=ap_out[:], in0=env.pl[0][:, 0:128], scalar=qdb[:, j * 2 + ss: j * 2 + ss + 1],
                    in1=ap_in[:], op0=OP.mult, op1=OP.add),
                    reads=[env.pl[0], qdb, ap_in], writes=[ap_out])
                k.dma("sp", attn_rows, attn_rows[row:row + 128, h * 128:(h + 1) * 128], ap_out, ap_out[:, :])
    k.merge_tokens(env.arena, [stf, stbf, dl, wv, qdb, cqT_sb, ap_in, ap_out])
    return [k.out_token(attn_rows)]



def build_B():
    nc = bass.Bass("TRN2", target_bir_lowering=False)
    k = K(nc)
    io = dict(
        aoTc=k.dram_in("aoTc", [D, NTOK], BF16),
        xrows=k.dram_in("xrows", [NTOK, D], F32),
        WoT=k.dram_in("WoT", [D, D], F32),
        kernW=k.dram_in("kernW", [D, D], F32),
        fbias=k.dram_in("fbias", [1, D], F32),
        LA=k.dram_in("LA", [512, D], F32),
        LBT=k.dram_in("LBT", [512, D], F32),
        SelAB=k.dram_in("SelAB", [512, 32], F32),
        AshT=k.dram_in("AshT", [D, 32], F32),
        BshT=k.dram_in("BshT", [32, D], F32),
        x2rows=k.dram_out("x2rows", [NTOK, D], F32),
        x2T=k.dram_out("x2T", [D, NTOK], BF16),
        x1sc=k.dram_tmp("x1sc", [NTOK, D], F32),
        x1Tsc=k.dram_tmp("x1Tsc", [D, NTOK], BF16),
    )
    with nc.Block() as blk:
        env = Env(k)
        fw = emit_stageB(env, io)
        k.finalize(blk, final_waits=fw)
    k.close()
    return k


def build_D():
    nc = bass.Bass("TRN2", target_bir_lowering=False)
    k = K(nc)
    io = dict(
        attn_rows=k.dram_in("attn_rows", [NTOK, D], F32),
        x2rows=k.dram_in("x2rows", [NTOK, D], F32),
        x2T=k.dram_in("x2T", [D, NTOK], BF16),
        nwcol=k.dram_in("nwcol", [128, 32], F32),
        gate_wT=k.dram_in("gate_wT", [D, D], F32),
        out_wT=k.dram_in("out_wT", [D, D], F32),
        W_baseT=k.dram_in("W_baseT", [D, D], F32),
        b_base=k.dram_in("b_base", [1, D], F32),
        encT=k.dram_in("encT", [D, 40], F32),
        CW0=k.dram_in("CW0", [3 * D, 32], F32),
        MW0=k.dram_in("MW0", [3 * D, 32], U8),
        CW1=k.dram_in("CW1", [3 * D, 8], F32),
        MW1=k.dram_in("MW1", [3 * D, 8], U8),
        Kmat=k.dram_in("Kmat", [D, D], F32),
        outrows=k.dram_out("outrows", [NTOK, D], F32),
        x3sc=k.dram_tmp("x3sc", [NTOK, D], F32),
        x3Tsc=k.dram_tmp("x3Tsc", [D, NTOK], BF16),
        y0Tsc=k.dram_tmp("y0Tsc", [D, NTOK], BF16),
        Hsc=k.dram_tmp("Hsc", [40, D], BF16),
    )
    with nc.Block() as blk:
        env = Env(k)
        fw = emit_stageD(env, io)
        k.finalize(blk, final_waits=fw)
    k.close()
    return k


def build_C1():
    nc = bass.Bass("TRN2", target_bir_lowering=False)
    k = K(nc)
    io = dict(
        x2T=k.dram_in("x2T", [D, NTOK], BF16),
        qkv_wT=k.dram_in("qkv_wT", [D, 3 * D], F32),
        ddtab=k.dram_in("ddtab", [32 * 256, 256], F32),
        qdtab=k.dram_in("qdtab", [32 * 128, 2], F32),
        kdtab=k.dram_in("kdtab", [32 * 128, 2], F32),
        attnp=k.dram_out("attnp", [NTOK, D], F32),
        kvloc=k.dram_out("kvloc", [D, 128], F32),
        cqTs=k.dram_out("cqTs", [D, NTOK], BF16),
    )
    decays = dict(bd=io.pop("_bd", None))
    import math
    HEADS = 32
    base = 1.0 / 2.0 ** (8.0 / HEADS)
    factor = 1.0 - 1 / (32 - 1 + 1e-05) + 1e-05
    bd = [math.exp(-(base ** (h + 1) * factor) * BS) for h in range(HEADS)]
    decays = dict(bd=bd)
    with nc.Block() as blk:
        env = Env(k)
        fw = emit_stageC1(env, io, decays)
        k.finalize(blk, final_waits=fw)
    k.close()
    return k


def build_C2():
    nc = bass.Bass("TRN2", target_bir_lowering=False)
    k = K(nc)
    io = dict(
        kvall=k.dram_in("kvall", [8 * D, 128], F32),
        attnp=k.dram_in("attnp", [NTOK, D], F32),
        cqTs=k.dram_in("cqTs", [D, NTOK], BF16),
        qdbtab=k.dram_in("qdbtab", [32 * 128, 2 * NBLK], F32),
        Wkvb=k.dram_in("Wkvb", [32 * 128, 8], F32),
        attn_rows=k.dram_out("attn_rows", [NTOK, D], F32),
    )
    with nc.Block() as blk:
        env = Env(k)
        fw = emit_stageC2(env, io, dict())
        k.finalize(blk, final_waits=fw)
    k.close()
    return k



S = 4096
NJ = S // 128
KSEL = 820

def emit_stageA2(env, io):
    k = env.k
    qhTh, qrTh, khTh, krTh, vh, aoT = io["qhT"], io["qrT"], io["khT"], io["krT"], io["v"], io["aoT"]

    qhT = k.alias("a_qhT", env.actT, [128, S], BF16, 0)
    qrT = k.alias("a_qrT", env.actT, [128, S], BF16, 8 * 1024)
    khT = k.alias("a_khT", env.actT, [128, S], BF16, 16 * 1024)
    krT = k.alias("a_krT", env.actT, [128, S], BF16, 24 * 1024)
    vsb = k.alias("a_v", env.actT, [128, S], BF16, 32 * 1024)
    outT = k.alias("a_outT", env.actT, [128, S], BF16, 40 * 1024)
    sim = [k.alias("b_sim0", env.xpre[0], [128, S], BF16, 0),
           k.alias("b_sim1", env.xpre[0], [128, S], BF16, 8 * 1024),
           k.alias("b_sim2", env.xpre[1], [128, S], BF16, 0),
           k.alias("b_sim3", env.xpre[1], [128, S], BF16, 8 * 1024)]
    scores = k.alias("b_scores", env.xpre[2], [128, S], F32, 0)
    attn = k.alias("b_attn", env.xpre[3], [128, S], BF16, 0)
    attnT = k.alias("b_attnT", env.xpre[3], [128, S], BF16, 8 * 1024)
    E = k.alias("a_E", env.arena, [128, S], BF16, 0)
    msel = k.alias("a_m", env.arena, [128, S], BF16, 8 * 1024)
    P = k.alias("a_P", env.wsb, [128, S], F32, 0)
    PG = k.alias("a_PG", env.wsb, [128, S], F32, 16 * 1024)
    blkmask = k.alias("a_blkmask", env.brow, [128, S], BF16, 0)
    b2 = k.alias("a_b2", env.tstage, [128, S], BF16, 0)
    ones32 = k.sbuf("a_ones32", [128, NJ], F32)
    bs = k.sbuf("a_bs", [128, NJ], F32)
    bo = k.sbuf("a_bo", [128, NJ], F32)
    rhs = k.sbuf("a_rhs", [128, NJ], F32)
    thr = k.sbuf("a_thr", [128, 4], F32)
    vc = k.sbuf("a_vc", [128, 4], F32)
    cnt = k.sbuf("a_cnt", [128, 4], F32)
    pred = k.sbuf("a_pred", [128, 4], F32)
    quota = k.sbuf("a_quota", [128, 4], F32)
    rowsum = k.sbuf("a_rowsum", [128, 1], F32)
    rinv = k.sbuf("a_rinv", [128, 1], F32)
    biasex = k.sbuf("a_biasex", [128, 1], F32)
    causf = k.sbuf("a_causf", [128, 128], F32)
    causb = k.sbuf("a_causb", [128, 128], BF16)

    SCALE = 1.0 / float(np.sqrt(128.0))
    BIG = 1000.0
    BIAS = -BIG * SCALE

    k.emit("pool", lambda e: e.memset(causf[:], 0.0), writes=[causf])
    k.emit("pool", lambda e: e.affine_select(out=causf[:], in_=causf[:], compare_op=OP.is_ge,
                                             fill=-3000.0, base=0, pattern=[[-1, 128]], channel_multiplier=1),
           reads=[causf], writes=[causf])
    k.emit("pool", lambda e: e.memset(causb[:], 0.0), writes=[causb])
    k.emit("pool", lambda e: e.affine_select(out=causb[:], in_=causb[:], compare_op=OP.is_ge,
                                             fill=-1000.0, base=0, pattern=[[-1, 128]], channel_multiplier=1),
           reads=[causb], writes=[causb])
    k.emit("pool", lambda e: e.memset(biasex[:], BIAS), writes=[biasex])
    k.emit("pool", lambda e: e.memset(ones32[:], 1.0), writes=[ones32])
    k.emit("pool", lambda e: e.memset(blkmask[:], 1.0), writes=[blkmask])
    for bidx in range(NJ):
        k.emit("pool", lambda e, bidx=bidx: e.memset(blkmask[:, bidx * 128:bidx * 128 + 1], 0.0), writes=[blkmask])

    FIRST_SEARCH = 6

    for h in range(8):
        kv = h // 4
        for q in range(4):
            k.dma("sp", qhT, qhT.ap().rearrange("p (a t) -> p a t", a=4)[:, q, :],
                  qhTh, qhTh[h * 128:(h + 1) * 128, q * 1024:(q + 1) * 1024])
            k.dma("sp", qrT, qrT.ap().rearrange("p (a t) -> p a t", a=4)[:, q, :],
                  qrTh, qrTh[h * 128:(h + 1) * 128, q * 1024:(q + 1) * 1024])
            k.dma("sp", khT, khT.ap().rearrange("p (a t) -> p a t", a=4)[:, q, :],
                  khTh, khTh[h * 128:(h + 1) * 128, q * 1024:(q + 1) * 1024])
            k.dma("sp", krT, krT.ap().rearrange("p (a t) -> p a t", a=4)[:, q, :],
                  krTh, krTh[kv * 128:(kv + 1) * 128, q * 1024:(q + 1) * 1024])
        k.dma("sp", vsb, vsb.ap().rearrange("p (c d) -> p c d", d=128),
              vh, vh.ap().rearrange("(kv c p) d -> p (kv c) d", p=128, c=NJ)[:, kv * NJ:(kv + 1) * NJ, :])

        groups = [list(range(g, min(g + 4, NJ))) for g in range(FIRST_SEARCH, NJ, 4)]
        def do_sim(j, slot):
            T = (j + 1) * 128
            nch = (T + 511) // 512
            for ch in range(nch):
                w = min(512, T - ch * 512)
                pm = env.pl[ch % 2]
                env.mm_group(pm[:, 0:w], pm,
                             [(qhT[:, j * 128:(j + 1) * 128], khT[:, ch * 512:ch * 512 + w], [qhT, khT])])
                k.emit("act", lambda e, pm=pm, ch=ch, w=w, slot=slot: e.copy(
                    out=sim[slot][:, ch * 512:ch * 512 + w], in_=pm[:, 0:w]),
                    reads=[pm], writes=[sim[slot]])
            k.emit("dve", lambda e, j=j, slot=slot: e.tensor_tensor(
                out=sim[slot][:, j * 128:(j + 1) * 128], in0=sim[slot][:, j * 128:(j + 1) * 128],
                in1=causb[:, :], op=OP.add),
                reads=[sim[slot], causb], writes=[sim[slot]])

        def do_scores(j):
            T = (j + 1) * 128
            nch = (T + 511) // 512
            for ch in range(nch):
                w = min(512, T - ch * 512)
                pm = env.pmm[ch % 2]
                env.mm_group(pm[:, 0:w], pm,
                             [(qrT[:, j * 128:(j + 1) * 128], krT[:, ch * 512:ch * 512 + w], [qrT, krT])])
                k.emit("act", lambda e, pm=pm, ch=ch, w=w: e.copy(
                    out=scores[:, ch * 512:ch * 512 + w], in_=pm[:, 0:w]),
                    reads=[pm], writes=[scores])
            k.emit("dve", lambda e, j=j: e.tensor_tensor(
                out=scores[:, j * 128:(j + 1) * 128], in0=scores[:, j * 128:(j + 1) * 128],
                in1=causf[:, :], op=OP.add),
                reads=[scores, causf], writes=[scores])

        def finish_tile(j, masked_buf, masked_ap):
            T = (j + 1) * 128
            k.emit("act", lambda e, T=T, masked_ap=masked_ap: e.activation(
                out=attn[:, 0:T], in_=masked_ap, func=AF.Exp, scale=SCALE, bias=biasex[:],
                accum_out=rowsum[:]),
                reads=[masked_buf, biasex], writes=[attn, rowsum], accum=[rowsum])
            k.emit("dve", lambda e: e.reciprocal(out=rinv[:], in_=rowsum[:]), reads=[rowsum], writes=[rinv])
            k.emit("dve", lambda e, T=T: e.tensor_scalar(out=attn[:, 0:T], in0=attn[:, 0:T], scalar1=rinv[:],
                                                         scalar2=None, op0=OP.mult, op1=OP.bypass),
                   reads=[attn, rinv], writes=[attn])
            nt = j + 1
            for tg in range((nt + 7) // 8):
                n8 = min(8, nt - tg * 8)
                for u in range(n8):
                    tc = tg * 8 + u
                    k.emit("pe", lambda e, tc=tc, u=u: e.transpose(
                        env.ptrb[:, u * 128:(u + 1) * 128], attn[:, tc * 128:(tc + 1) * 128], env.idn[:]),
                        reads=[attn, env.idn], writes=[env.ptrb], inc=1 if u == n8 - 1 else 0)
                k.emit("act", lambda e, tg=tg, n8=n8: e.copy(
                    out=attnT[:, tg * 1024:tg * 1024 + n8 * 128], in_=env.ptrb[:, 0:n8 * 128]),
                    reads=[env.ptrb], writes=[attnT])
            terms = [(vsb[:, tc * 128:(tc + 1) * 128], attnT[:, tc * 128:(tc + 1) * 128], [vsb, attnT])
                     for tc in range(nt)]
            env.mm_group(env.ptrf[:, 0:128], env.ptrf, terms)
            k.emit("dve", lambda e, j=j: e.tensor_copy(out=outT[:, j * 128:(j + 1) * 128], in_=env.ptrf[:, 0:128]),
                   reads=[env.ptrf], writes=[outT])

        for j in range(FIRST_SEARCH):
            do_scores(j)
            k.emit("dve", lambda e, j=j: e.tensor_scalar(
                out=scores[:, 0:(j + 1) * 128], in0=scores[:, 0:(j + 1) * 128], scalar1=BIG, scalar2=None,
                op0=OP.add, op1=OP.bypass), reads=[scores], writes=[scores])
            finish_tile(j, scores, scores[:, 0:(j + 1) * 128])

        for grp in groups:
            for slot, j in enumerate(grp):
                do_sim(j, slot)
            ns = len(grp)
            k.emit("pool", lambda e, ns=ns: e.memset(thr[:, 0:ns], -130.0), writes=[thr])
            for step in (128, 64, 32, 16, 8, 4, 2, 1):
                sv = float(2 * step)
                k.emit("dve", lambda e, sv=sv, ns=ns: e.tensor_scalar(
                    out=vc[:, 0:ns], in0=thr[:, 0:ns], scalar1=sv, scalar2=None, op0=OP.add, op1=OP.bypass),
                    reads=[thr], writes=[vc])
                for slot, j in enumerate(grp):
                    T = (j + 1) * 128
                    k.emit("dve", lambda e, slot=slot, j=j, T=T: e.tensor_scalar(
                        out=msel[:, 0:T], in0=sim[slot][:, 0:T], scalar1=vc[:, slot:slot + 1], scalar2=None,
                        op0=OP.is_ge, op1=OP.add, accum_out=cnt[:, slot:slot + 1]),
                        reads=[sim[slot], vc], writes=[msel, cnt], accum=[cnt])
                k.emit("dve", lambda e, ns=ns: e.tensor_copy(out=pred[:, 0:ns], in_=thr[:, 0:ns]),
                       reads=[thr], writes=[pred])
                k.emit("dve", lambda e, sv=sv, ns=ns: e.tensor_scalar(
                    out=pred[:, 0:ns], in0=cnt[:, 0:ns], scalar1=float(KSEL), scalar2=sv,
                    op0=OP.is_ge, op1=OP.mult), reads=[cnt], writes=[pred])
                k.emit("dve", lambda e, ns=ns: e.tensor_tensor(
                    out=thr[:, 0:ns], in0=thr[:, 0:ns], in1=pred[:, 0:ns], op=OP.add),
                    reads=[thr, pred], writes=[thr])
            k.emit("dve", lambda e, ns=ns: e.tensor_scalar(
                out=vc[:, 0:ns], in0=thr[:, 0:ns], scalar1=2.0, scalar2=None, op0=OP.add, op1=OP.bypass),
                reads=[thr], writes=[vc])
            for slot, j in enumerate(grp):
                T = (j + 1) * 128
                k.emit("dve", lambda e, slot=slot, T=T: e.tensor_scalar(
                    out=msel[:, 0:T], in0=sim[slot][:, 0:T], scalar1=vc[:, slot:slot + 1], scalar2=None,
                    op0=OP.is_ge, op1=OP.add, accum_out=cnt[:, slot:slot + 1]),
                    reads=[sim[slot], vc], writes=[msel, cnt], accum=[cnt])
            k.emit("dve", lambda e, ns=ns: e.tensor_copy(out=pred[:, 0:ns], in_=thr[:, 0:ns]),
                   reads=[thr], writes=[pred])
            k.emit("dve", lambda e, ns=ns: e.tensor_scalar(
                out=quota[:, 0:ns], in0=cnt[:, 0:ns], scalar1=-1.0, scalar2=float(KSEL),
                op0=OP.mult, op1=OP.add), reads=[cnt], writes=[quota])
            for slot, j in enumerate(grp):
                T = (j + 1) * 128
                NB = (j + 1)
                k.emit("dve", lambda e, slot=slot, T=T: e.tensor_scalar(
                    out=E[:, 0:T], in0=sim[slot][:, 0:T], scalar1=thr[:, slot:slot + 1], scalar2=None,
                    op0=OP.is_equal, op1=OP.bypass), reads=[sim[slot], thr], writes=[E])
                k.emit("dve", lambda e, slot=slot, T=T: e.tensor_scalar(
                    out=b2[:, 0:T], in0=sim[slot][:, 0:T], scalar1=vc[:, slot:slot + 1], scalar2=None,
                    op0=OP.is_ge, op1=OP.bypass), reads=[sim[slot], vc], writes=[b2])
                k.emit("dve", lambda e, T=T: e.tensor_tensor_scan(
                    out=P[:, 0:T], data0=blkmask[:, 0:T], data1=E[:, 0:T], initial=0.0,
                    op0=OP.mult, op1=OP.add), reads=[blkmask, E], writes=[P])
                k.emit("dve", lambda e, NB=NB: e.tensor_copy(out=bs[:, 0:NB], in_=P[:, 127:NB * 128:128]),
                       reads=[P], writes=[bs])
                k.emit("dve", lambda e, NB=NB: e.tensor_tensor_scan(
                    out=bo[:, 0:NB], data0=ones32[:, 0:NB], data1=bs[:, 0:NB], initial=0.0,
                    op0=OP.mult, op1=OP.add), reads=[ones32, bs], writes=[bo])
                k.emit("dve", lambda e, NB=NB: e.tensor_tensor(
                    out=rhs[:, 0:NB], in0=bo[:, 0:NB], in1=bs[:, 0:NB], op=OP.subtract),
                    reads=[bo, bs], writes=[rhs])
                k.emit("dve", lambda e, NB=NB, T=T: e.tensor_tensor(
                    out=PG.ap()[:, 0:T].rearrange("p (b t) -> p b t", t=128),
                    in0=P.ap()[:, 0:T].rearrange("p (b t) -> p b t", t=128),
                    in1=rhs.ap()[:, 0:NB].unsqueeze(2).broadcast_to([128, NB, 128]),
                    op=OP.add), reads=[P, rhs], writes=[PG])
                k.emit("dve", lambda e, slot=slot, T=T: e.scalar_tensor_tensor(
                    out=msel[:, 0:T], in0=PG[:, 0:T], scalar=quota[:, slot:slot + 1], in1=E[:, 0:T],
                    op0=OP.is_le, op1=OP.mult), reads=[PG, quota, E], writes=[msel])
                do_scores(j)
                k.emit("dve", lambda e, T=T: e.scalar_tensor_tensor(
                    out=scores[:, 0:T], in0=b2[:, 0:T], scalar=BIG, in1=scores[:, 0:T],
                    op0=OP.mult, op1=OP.add), reads=[b2, scores], writes=[scores])
                k.emit("dve", lambda e, T=T: e.scalar_tensor_tensor(
                    out=scores[:, 0:T], in0=msel[:, 0:T], scalar=BIG, in1=scores[:, 0:T],
                    op0=OP.mult, op1=OP.add), reads=[msel, scores], writes=[scores])
                finish_tile(j, scores, scores[:, 0:T])
        k.dma("sp", aoT, aoT[h * 128:(h + 1) * 128, :], outT, outT[:, :])
    k.merge_tokens(env.actT, [qhT, qrT, khT, krT, vsb, outT])
    for xb, subs in [(env.xpre[0], [sim[0], sim[1]]), (env.xpre[1], [sim[2], sim[3]]),
                     (env.xpre[2], [scores]), (env.xpre[3], [attn, attnT])]:
        k.merge_tokens(xb, subs)
    k.merge_tokens(env.arena, [E, msel])
    k.merge_tokens(env.wsb, [P, PG])
    k.merge_tokens(env.brow, [blkmask])
    k.merge_tokens(env.tstage, [b2])
    return [k.out_token(aoT)]


def build_A2():
    nc = bass.Bass("TRN2", target_bir_lowering=False)
    k = K(nc)
    io = dict(
        qhT=k.dram_in("qhT", [1024, S], BF16),
        qrT=k.dram_in("qrT", [1024, S], BF16),
        khT=k.dram_in("khT", [1024, S], BF16),
        krT=k.dram_in("krT", [256, S], BF16),
        v=k.dram_in("v", [2 * S, 128], BF16),
        aoT=k.dram_out("aoT", [1024, S], BF16),
    )
    with nc.Block() as blk:
        env = Env(k)
        fw = emit_stageA2(env, io)
        k.finalize(blk, final_waits=fw)
    k.close()
    return k



def emit_stageA1(env, io):
    k = env.k
    xA, WqkvT = io["xA"], io["WqkvT"]
    hp1, hp2, hb1 = io["hp1"], io["hp2"], io["hb1"]
    cosf, sinm = io["cosf"], io["sinm"]
    qrTo, qhTo, krTo, khTo, vo = io["qrT"], io["qhT"], io["krT"], io["khT"], io["v"]
    WhT, WlT = io["WhT"], io["WlT"]

    xhT = k.alias("p_xhT", env.actT, [128, NE * 512], BF16, 0)
    xlT = k.alias("p_xlT", env.actT, [128, NE * 512], BF16, NE * 1024)
    whsb = k.alias("p_whsb", env.wsb, [128, NE * 256], BF16, 0)
    wlsb = k.alias("p_wlsb", env.wsb, [128, NE * 256], BF16, NE * 512)
    proj = [k.alias(f"p_proj{t}", env.xpre[t], [128, 1536], F32, 0) for t in range(4)]
    xf = [k.alias(f"p_xf{t}", env.xpre[t], [128, 512], F32, 6144) for t in range(4)]
    cosg = k.alias("p_cosg", env.arena, [128, 512], F32, 0)
    sing = k.alias("p_sing", env.arena, [128, 512], F32, 2048)
    hp1s = k.alias("p_hp1", env.arena, [128, 8 * 128], F32, 4096)
    hp2s = k.alias("p_hp2", env.arena, [128, 8 * 128], F32, 8192)
    ropeA = k.alias("p_ropeA", env.arena, [128, 128], F32, 12288)
    ropeB = k.alias("p_ropeB", env.arena, [128, 128], F32, 12800)
    qrTf = k.alias("p_qrTf", env.arena, [128, 128], F32, 13312)
    hh1 = k.alias("p_hh1", env.arena, [128, 128], F32, 13824)
    hh2 = k.alias("p_hh2", env.arena, [128, 128], F32, 14336)
    qbf = k.alias("p_qbf", env.arena, [128, 128], BF16, 14848)
    vstg = k.alias("p_vstg", env.arena, [128, 256], BF16, 15104)
    xbf = k.alias("p_xbf", env.tstage, [128, D], BF16, 0)
    xlb = k.alias("p_xlb", env.brow, [128, D], BF16, 0)
    hbs = k.sbuf("p_hbs", [128, 32], F32)
    k.dma("sp", hp1s, hp1s.ap().rearrange("p (h e) -> p h e", e=128),
          hp1, hp1.ap().rearrange("(h p) e -> p h e", p=128))
    k.dma("sp", hp2s, hp2s.ap().rearrange("p (h e) -> p h e", e=128),
          hp2, hp2.ap().rearrange("(h p) e -> p h e", p=128))
    k.dma("sp", hbs, hbs[:, 0:16], hb1, hb1[:, :])

    wstg = [k.alias(f"p_wstg{i}", env.xpre[i], [128, 1536], F32, 6144 + 2048) if False else None
            for i in range(1)]
    wf = k.alias("p_wf", env.xpre[0], [128, 1536], F32, 8192)
    wh_t = k.alias("p_wht", env.tstage, [128, 1536], BF16, 0)
    wl_t = k.alias("p_wlt", env.brow, [128, 1536], BF16, 0)
    wres = k.alias("p_wres", env.xpre[1], [128, 1536], F32, 8192)
    for r in range(NE):
        k.dma("sp", wf, wf[:, :], WqkvT, WqkvT[r * 128:(r + 1) * 128, :])
        k.emit("dve", lambda e: e.tensor_copy(out=wh_t[:, :], in_=wf[:, :]), reads=[wf], writes=[wh_t])
        k.emit("dve", lambda e: e.tensor_tensor(out=wres[:, :], in0=wf[:, :], in1=wh_t[:, :], op=OP.subtract),
               reads=[wf, wh_t], writes=[wres])
        k.emit("dve", lambda e: e.tensor_copy(out=wl_t[:, :], in_=wres[:, :]), reads=[wres], writes=[wl_t])
        k.dma("sp", WhT, WhT[r * 128:(r + 1) * 128, :], wh_t, wh_t[:, :])
        k.dma("sp", WlT, WlT[r * 128:(r + 1) * 128, :], wl_t, wl_t[:, :])
    k.merge_tokens(env.tstage, [wh_t])
    k.merge_tokens(env.brow, [wl_t])
    k.merge_tokens(env.xpre[0], [wf])
    k.merge_tokens(env.xpre[1], [wres])

    def rope(src_ap, src_buf, ttok):
        co = cosg[:, 0:128] if False else cosg[:, ttok * 128:(ttok + 1) * 128]
        si = sing[:, ttok * 128:(ttok + 1) * 128]
        k.emit("dve", lambda e: e.tensor_tensor(out=ropeA[:, :], in0=src_ap, in1=co, op=OP.mult),
               reads=[src_buf, cosg], writes=[ropeA])
        k.emit("dve", lambda e: e.tensor_tensor(out=ropeB[:, 0:64], in0=src_ap[:, 64:128],
                                                in1=si[:, 0:64], op=OP.mult),
               reads=[src_buf, sing], writes=[ropeB])
        k.emit("dve", lambda e: e.tensor_tensor(out=ropeB[:, 64:128], in0=src_ap[:, 0:64],
                                                in1=si[:, 64:128], op=OP.mult),
               reads=[src_buf, sing], writes=[ropeB])
        k.emit("dve", lambda e: e.tensor_tensor(out=ropeA[:, :], in0=ropeA[:, :], in1=ropeB[:, :], op=OP.add),
               reads=[ropeA, ropeB], writes=[ropeA])
        k.emit("pe", lambda e: e.transpose(env.ptrf[:, 0:128], ropeA[:, :], env.idnf[:]),
               reads=[ropeA, env.idnf], writes=[env.ptrf])
        k.emit("dve", lambda e: e.tensor_copy(out=qrTf[:, :], in_=env.ptrf[:, 0:128]),
               reads=[env.ptrf], writes=[qrTf])
        k.emit("dve", lambda e: e.tensor_copy(out=qbf[:, :], in_=qrTf[:, :]), reads=[qrTf], writes=[qbf])

    def hash_chain(h, out_hbm, r0, c0):
        env.mm_group(env.pl[0][:, 0:128], env.pl[0],
                     [(hp1s[:, h * 128:(h + 1) * 128], qrTf[:, :], [hp1s, qrTf])])
        k.emit("act", lambda e, h=h: e.activation(out=hh1[:], in_=env.pl[0][:, 0:128], func=AF.Silu,
                                                  bias=hbs[:, h:h + 1], scale=1.0),
               reads=[env.pl[0], hbs], writes=[hh1])
        k.emit("dve", lambda e: e.tensor_tensor(out=hh1[:], in0=hh1[:], in1=qrTf[:, :], op=OP.add),
               reads=[hh1, qrTf], writes=[hh1])
        env.mm_group(env.pl[1][:, 0:128], env.pl[1],
                     [(hp2s[:, h * 128:(h + 1) * 128], hh1[:], [hp2s, hh1])])
        k.emit("act", lambda e, h=h: e.activation(out=hh2[:], in_=env.pl[1][:, 0:128], func=AF.Identity,
                                                  bias=hbs[:, 8 + h:9 + h], scale=1.0),
               reads=[env.pl[1], hbs], writes=[hh2])
        k.emit("dve", lambda e: e.tensor_tensor(out=hh2[:], in0=hh2[:], in1=hh1[:], op=OP.add),
               reads=[hh2, hh1], writes=[hh2])
        k.emit("act", lambda e: e.activation(out=qbf[:, :], in_=hh2[:], func=AF.Sign),
               reads=[hh2], writes=[qbf])
        k.dma("sp", out_hbm, out_hbm[r0:r0 + 128, c0:c0 + 128], qbf, qbf[:, :])

    for g in range(8):
        k.dma("sp", cosg, cosg.ap().rearrange("p (a d) -> p a d", d=128),
              cosf, cosf.ap().rearrange("(a p) d -> p a d", p=128)[:, g * 4:(g + 1) * 4, :])
        k.dma("sp", sing, sing.ap().rearrange("p (a d) -> p a d", d=128),
              sinm, sinm.ap().rearrange("(a p) d -> p a d", p=128)[:, g * 4:(g + 1) * 4, :])
        for t in range(4):
            gt = g * 4 + t
            k.dma("sp", xf[t], xf[t][:, 0:512], xA, xA[gt * 128:(gt + 1) * 128, 0:512])
        for t in range(4):
            gt = g * 4 + t
            for ch in range(8):
                k.dma("sp", xf[t], xf[t][:, 0:512], xA, xA[gt * 128:(gt + 1) * 128, ch * 512:(ch + 1) * 512])
                k.emit("dve", lambda e, t=t, ch=ch: e.tensor_copy(out=xbf[:, ch * 512:(ch + 1) * 512],
                                                                  in_=xf[t][:, 0:512]),
                       reads=[xf[t]], writes=[xbf])
                k.emit("dve", lambda e, t=t, ch=ch: e.tensor_tensor(
                    out=xf[t][:, 0:512], in0=xf[t][:, 0:512], in1=xbf[:, ch * 512:(ch + 1) * 512],
                    op=OP.subtract), reads=[xf[t], xbf], writes=[xf[t]])
                k.emit("dve", lambda e, t=t, ch=ch: e.tensor_copy(out=xlb[:, ch * 512:(ch + 1) * 512],
                                                                  in_=xf[t][:, 0:512]),
                       reads=[xf[t]], writes=[xlb])
            for src, dstT in ((xbf, xhT), (xlb, xlT)):
                for eg in range(4):
                    for j in range(8):
                        et = eg * 8 + j
                        k.emit("pe", lambda e, et=et, j=j, src=src: e.transpose(
                            env.ptrb[:, j * 128:(j + 1) * 128], src[:, et * 128:(et + 1) * 128], env.idn[:]),
                            reads=[src, env.idn], writes=[env.ptrb], inc=1 if j == 7 else 0)
                    k.emit("dve", lambda e, eg=eg, t=t, dstT=dstT: e.tensor_copy(
                        out=dstT.ap().rearrange("p (c t2) -> p c t2", t2=512)[:, eg * 8:(eg + 1) * 8,
                                                                             t * 128:(t + 1) * 128],
                        in_=env.ptrb.ap().rearrange("p (c t2) -> p c t2", t2=128)),
                        reads=[env.ptrb], writes=[dstT])
        for dc in range(6):
            for q in range(4):
                k.dma("sp", whsb, whsb.ap().rearrange("p (c o) -> p c o", o=256)[:, q * 8:(q + 1) * 8, :],
                      WhT, WhT.ap().rearrange("(c p) o -> p c o", p=128)[:, q * 8:(q + 1) * 8,
                                                                        dc * 256:(dc + 1) * 256])
            split = dc < 5
            if split:
                for q in range(4):
                    k.dma("sp", wlsb, wlsb.ap().rearrange("p (c o) -> p c o", o=256)[:, q * 8:(q + 1) * 8, :],
                          WlT, WlT.ap().rearrange("(c p) o -> p c o", p=128)[:, q * 8:(q + 1) * 8,
                                                                            dc * 256:(dc + 1) * 256])
            for t in range(4):
                pm = env.pmm[t % 2]
                terms = []
                for e_ in range(NE):
                    terms.append((xhT[:, e_ * 512 + t * 128: e_ * 512 + t * 128 + 128],
                                  whsb[:, e_ * 256:(e_ + 1) * 256], [xhT, whsb]))
                if split:
                    for e_ in range(NE):
                        terms.append((xhT[:, e_ * 512 + t * 128: e_ * 512 + t * 128 + 128],
                                      wlsb[:, e_ * 256:(e_ + 1) * 256], [xhT, wlsb]))
                    for e_ in range(NE):
                        terms.append((xlT[:, e_ * 512 + t * 128: e_ * 512 + t * 128 + 128],
                                      whsb[:, e_ * 256:(e_ + 1) * 256], [xlT, whsb]))
                env.mm_group(pm[:, 0:256], pm, terms)
                k.emit("dve", lambda e, t=t, dc=dc, pm=pm: e.tensor_copy(
                    out=proj[t][:, dc * 256:(dc + 1) * 256], in_=pm[:, 0:256]),
                    reads=[pm], writes=[proj[t]])
        for t in range(4):
            gt = g * 4 + t
            for h in range(8):
                rope(proj[t].ap()[:, h * 128:(h + 1) * 128], proj[t], t)
                k.dma("sp", qrTo, qrTo[h * 128:(h + 1) * 128, gt * 128:(gt + 1) * 128], qbf, qbf[:, :])
                hash_chain(h, qhTo, h * 128, gt * 128)
            for kvl in range(2):
                rope(proj[t].ap()[:, 1024 + kvl * 128: 1024 + (kvl + 1) * 128], proj[t], t)
                k.dma("sp", krTo, krTo[kvl * 128:(kvl + 1) * 128, gt * 128:(gt + 1) * 128], qbf, qbf[:, :])
                for hq in range(4):
                    hash_chain(kvl * 4 + hq, khTo, (kvl * 4 + hq) * 128, gt * 128)
            k.emit("dve", lambda e, t=t: e.tensor_copy(out=vstg[:, :], in_=proj[t][:, 1280:1536]),
                   reads=[proj[t]], writes=[vstg])
            for kvl in range(2):
                k.dma("sp", vo, vo[kvl * S + gt * 128: kvl * S + (gt + 1) * 128, :],
                      vstg, vstg[:, kvl * 128:(kvl + 1) * 128])
    k.merge_tokens(env.arena, [cosg, sing, hp1s, hp2s, ropeA, ropeB, qrTf, hh1, hh2, qbf, vstg])
    for i in range(4):
        k.merge_tokens(env.xpre[i], [proj[i], xf[i]])
    k.merge_tokens(env.actT, [xhT, xlT])
    k.merge_tokens(env.wsb, [whsb, wlsb])
    k.merge_tokens(env.tstage, [xbf])
    k.merge_tokens(env.brow, [xlb])
    return [k.out_token(qrTo), k.out_token(qhTo), k.out_token(krTo), k.out_token(khTo), k.out_token(vo)]


def build_A1():
    nc = bass.Bass("TRN2", target_bir_lowering=False)
    k = K(nc)
    io = dict(
        xA=k.dram_in("xA", [S, D], F32),
        WqkvT=k.dram_in("WqkvT", [D, 1536], F32),
        hp1=k.dram_in("hp1", [1024, 128], F32),
        hp2=k.dram_in("hp2", [1024, 128], F32),
        hb1=k.dram_in("hb1", [128, 16], F32),
        cosf=k.dram_in("cosf", [S, 128], F32),
        sinm=k.dram_in("sinm", [S, 128], F32),
        qrT=k.dram_out("qrT", [1024, S], BF16),
        qhT=k.dram_out("qhT", [1024, S], BF16),
        krT=k.dram_out("krT", [256, S], BF16),
        khT=k.dram_out("khT", [1024, S], BF16),
        v=k.dram_out("v", [2 * S, 128], BF16),
        WhT=k.dram_tmp("WhT", [D, 1536], BF16),
        WlT=k.dram_tmp("WlT", [D, 1536], BF16),
    )
    with nc.Block() as blk:
        env = Env(k)
        fw = emit_stageA1(env, io)
        k.finalize(blk, final_waits=fw)
    k.close()
    return k



def build_L1():
    nc = bass.Bass("TRN2", target_bir_lowering=False)
    k = K(nc)
    io = dict(
        xA=k.dram_in("xA", [S, D], F32),
        WqkvT=k.dram_in("WqkvT", [D, 1536], F32),
        hp1=k.dram_in("hp1", [1024, 128], F32),
        hp2=k.dram_in("hp2", [1024, 128], F32),
        hb1=k.dram_in("hb1", [128, 16], F32),
        cosf=k.dram_in("cosf", [S, 128], F32),
        sinm=k.dram_in("sinm", [S, 128], F32),
        qrT=k.dram_tmp("qrT", [1024, S], BF16),
        qhT=k.dram_tmp("qhT", [1024, S], BF16),
        krT=k.dram_tmp("krT", [256, S], BF16),
        khT=k.dram_tmp("khT", [1024, S], BF16),
        v=k.dram_tmp("v", [2 * S, 128], BF16),
        WhT=k.dram_tmp("WhT", [D, 1536], BF16),
        WlT=k.dram_tmp("WlT", [D, 1536], BF16),
        aoT=k.dram_out("aoT", [1024, S], BF16),
    )
    with nc.Block() as blk:
        env = Env(k)
        emit_stageA1(env, io)
        fw = emit_stageA2(env, io)
        k.finalize(blk, final_waits=fw)
    k.close()
    return k


def build_L2():
    nc = bass.Bass("TRN2", target_bir_lowering=False)
    k = K(nc)
    io = dict(
        aoTc=k.dram_in("aoTc", [D, NTOK], BF16),
        xrows=k.dram_in("xrows", [NTOK, D], F32),
        WoT=k.dram_in("WoT", [D, D], F32),
        kernW=k.dram_in("kernW", [D, D], F32),
        fbias=k.dram_in("fbias", [1, D], F32),
        LA=k.dram_in("LA", [512, D], F32),
        LBT=k.dram_in("LBT", [512, D], F32),
        SelAB=k.dram_in("SelAB", [512, 32], F32),
        AshT=k.dram_in("AshT", [D, 32], F32),
        BshT=k.dram_in("BshT", [32, D], F32),
        x2rows=k.dram_out("x2rows", [NTOK, D], F32),
        x2T=k.dram_out("x2T", [D, NTOK], BF16),
        x1sc=k.dram_tmp("x1sc", [NTOK, D], F32),
        x1Tsc=k.dram_tmp("x1Tsc", [D, NTOK], BF16),
        qkv_wT=k.dram_in("qkv_wT", [D, 3 * D], F32),
        ddtab=k.dram_in("ddtab", [32 * 256, 256], F32),
        qdtab=k.dram_in("qdtab", [32 * 128, 2], F32),
        kdtab=k.dram_in("kdtab", [32 * 128, 2], F32),
        attnp=k.dram_out("attnp", [NTOK, D], F32),
        kvloc=k.dram_out("kvloc", [D, 128], F32),
        cqTs=k.dram_out("cqTs", [D, NTOK], BF16),
    )
    import math
    HEADS = 32
    base = 1.0 / 2.0 ** (8.0 / HEADS)
    factor = 1.0 - 1 / (32 - 1 + 1e-05) + 1e-05
    bd = [math.exp(-(base ** (h + 1) * factor) * BS) for h in range(HEADS)]
    with nc.Block() as blk:
        env = Env(k)
        fwB = emit_stageB(env, io)
        fwC = emit_stageC1(env, io, dict(bd=bd))
        k.finalize(blk, final_waits=fwB + fwC)
    k.close()
    return k


def build_L3():
    nc = bass.Bass("TRN2", target_bir_lowering=False)
    k = K(nc)
    io = dict(
        kvall=k.dram_in("kvall", [8 * D, 128], F32),
        attnp=k.dram_in("attnp", [NTOK, D], F32),
        cqTs=k.dram_in("cqTs", [D, NTOK], BF16),
        qdbtab=k.dram_in("qdbtab", [32 * 128, 2 * NBLK], F32),
        Wkvb=k.dram_in("Wkvb", [32 * 128, 8], F32),
        attn_rows=k.dram_tmp("attn_rows", [NTOK, D], F32),
        x2rows=k.dram_in("x2rows", [NTOK, D], F32),
        x2T=k.dram_in("x2T", [D, NTOK], BF16),
        nwcol=k.dram_in("nwcol", [128, 32], F32),
        gate_wT=k.dram_in("gate_wT", [D, D], F32),
        out_wT=k.dram_in("out_wT", [D, D], F32),
        W_baseT=k.dram_in("W_baseT", [D, D], F32),
        b_base=k.dram_in("b_base", [1, D], F32),
        encT=k.dram_in("encT", [D, 40], F32),
        CW0=k.dram_in("CW0", [3 * D, 32], F32),
        MW0=k.dram_in("MW0", [3 * D, 32], U8),
        CW1=k.dram_in("CW1", [3 * D, 8], F32),
        MW1=k.dram_in("MW1", [3 * D, 8], U8),
        Kmat=k.dram_in("Kmat", [D, D], F32),
        outrows=k.dram_out("outrows", [NTOK, D], F32),
        x3sc=k.dram_tmp("x3sc", [NTOK, D], F32),
        x3Tsc=k.dram_tmp("x3Tsc", [D, NTOK], BF16),
        y0Tsc=k.dram_tmp("y0Tsc", [D, NTOK], BF16),
        Hsc=k.dram_tmp("Hsc", [40, D], BF16),
    )
    with nc.Block() as blk:
        env = Env(k)
        emit_stageC2(env, io, dict())
        fw = emit_stageD(env, io)
        k.finalize(blk, final_waits=fw)
    k.close()
    return k



def emit_stageW(env, io):
    k = env.k
    aoT, WoTloc, ypart = io["aoT"], io["WoTloc"], io["ypart"]
    whloc = k.alias("w_whloc", env.wsb, [128, 8 * 512], BF16, 0)
    wstg = env.xpre[0]
    for q in range(4):
        k.dma("sp", env.actT, env.actT.ap().rearrange("p (c t) -> p c t", t=S)[:, q * 2:(q + 1) * 2, :],
              aoT, aoT.ap().rearrange("(c p) t -> p c t", p=128)[:, q * 2:(q + 1) * 2, :])
    for oc in range(8):
        k.dma("pool", wstg, wstg.ap().rearrange("p (c o) -> p c o", o=512), WoTloc,
              WoTloc.ap().rearrange("(c p) o -> p c o", p=128)[:, :, oc * 512:(oc + 1) * 512])
        for e_ in range(8):
            k.emit("dve", lambda e, e_=e_: e.tensor_copy(out=whloc[:, e_ * 512:(e_ + 1) * 512],
                                                         in_=wstg[:, e_ * 512:(e_ + 1) * 512]),
                   reads=[wstg], writes=[whloc])
        for tt in range(32):
            pm = env.pmm[tt % 2]
            terms = [(env.actT[:, e_ * S + tt * 128: e_ * S + tt * 128 + 128],
                      whloc[:, e_ * 512:(e_ + 1) * 512], [env.actT, whloc]) for e_ in range(8)]
            env.mm_group(pm[:, :], pm, terms)
            ys = env.xres[tt % 4]
            k.emit("act", lambda e, pm=pm, ys=ys: e.copy(out=ys[:, :], in_=pm[:, :]),
                   reads=[pm], writes=[ys])
            k.dma("sp", ypart, ypart[tt * 128:(tt + 1) * 128, oc * 512:(oc + 1) * 512], ys, ys[:, :])
    k.merge_tokens(env.wsb, [whloc])
    return [k.out_token(ypart)]


def emit_lnrows(env, io):
    k = env.k
    xrows, ymy, x1sc, x1Tsc = io["xrows"], io["ymy"], io["x1sc"], io["x1Tsc"]
    for g in range(2):
        for t in range(HALF):
            gt = g * HALF + t
            xp = env.xpre[t]
            k.dma("sp", xp, xp[:, :], ymy, ymy[gt * 128:(gt + 1) * 128, :])
            for d in range(ND):
                xr = env.xres[t]
                k.dma("sp", xr, xr[:, :], xrows, xrows[gt * 128:(gt + 1) * 128, d * 512:(d + 1) * 512])
                k.emit("dve", lambda e, t=t, d=d, xp=xp, xr=xr: e.scalar_tensor_tensor(
                    out=xp[:, d * 512:(d + 1) * 512], in0=xp[:, d * 512:(d + 1) * 512], scalar=1.0,
                    in1=xr[:, :], op0=OP.mult, op1=OP.add, accum_out=env.sums[t][:, d:d + 1]),
                    reads=[xp, xr], writes=[xp, env.sums[t]], accum=[env.sums[t]])
                k.emit("act", lambda e, t=t, d=d, xp=xp: e.activation(
                    out=env.sqjunk[:], in_=xp[:, d * 512:(d + 1) * 512], func=AF.Square,
                    accum_out=env.sqs[t][:, d:d + 1]),
                    reads=[xp], writes=[env.sqjunk, env.sqs[t]], accum=[env.sqs[t]])
        for t in range(HALF):
            gt = g * HALF + t
            env.ln_stats(env.sums[t], env.sqs[t], 1.0 / D, 1e-5)
            k.emit("dve", lambda e, t=t: e.tensor_scalar(
                out=env.xpre[t][:, :], in0=env.xpre[t][:, :], scalar1=env.mean[:], scalar2=env.rstd[:],
                op0=OP.subtract, op1=OP.mult),
                reads=[env.xpre[t], env.mean, env.rstd], writes=[env.xpre[t]])
            k.dma("sp", x1sc, x1sc[gt * 128:(gt + 1) * 128, :], env.xpre[t], env.xpre[t][:, :])
            env.transpose_out(env.xpre[t], env.xpre[t].ap(), gt, x1Tsc)


def emit_stageB_fused(env, io):
    k = env.k
    kernW, fbias = io["kernW"], io["fbias"]
    LA, LBT, SelAB, AshT, BshT = io["LA"], io["LBT"], io["SelAB"], io["AshT"], io["BshT"]
    x2rows, x2T, x1sc, x1Tsc = io["x2rows"], io["x2T"], io["x1sc"], io["x1Tsc"]

    AeT = k.alias("AeT", env.arena, [128, NE * 32], BF16, 0)
    AshT_sb = k.alias("AshT_sb", env.arena, [128, NE * 32], BF16, 2 * 1024)
    BeT = k.alias("BeT", env.arena, [128, D], BF16, 4 * 1024)
    AxT = k.alias("AxT", env.arena, [128, NTOK], BF16, 12 * 1024)
    sel_sb = k.alias("sel_sb", env.arena, [128, 4 * 32], BF16, 14 * 1024)

    for i in range(4):
        k.dma("pool", env.wsb, env.wsb[:, i * D:(i + 1) * D], LA, LA[i * 128:(i + 1) * 128, :])
    k.dma("pool", sel_sb, sel_sb.ap().rearrange("p (c r) -> p c r", r=32),
          SelAB, SelAB.ap().rearrange("(c p) r -> p c r", p=128))
    k.dma("pool", AshT_sb, AshT_sb.ap().rearrange("p (c r) -> p c r", r=32),
          AshT, AshT.ap().rearrange("(c p) r -> p c r", p=128))
    k.dma("pool", BeT, BeT[32:64, :], BshT, BshT[:, :])
    for et in range(NE):
        terms = [(env.wsb[:, c * D + et * 128: c * D + (et + 1) * 128], sel_sb[:, c * 32:(c + 1) * 32],
                  [env.wsb, sel_sb]) for c in range(4)]
        env.mm_group(env.pl[0][:, 0:32], env.pl[0], terms)
        k.emit("dve", lambda e, et=et: e.tensor_copy(out=AeT[:, et * 32:(et + 1) * 32], in_=env.pl[0][:, 0:32]),
               reads=[env.pl[0]], writes=[AeT])
    for i in range(4):
        k.dma("pool", env.wsb, env.wsb[:, i * D:(i + 1) * D], LBT, LBT[i * 128:(i + 1) * 128, :])
    for oc in range(ND):
        terms = [(sel_sb[:, c * 32:(c + 1) * 32], env.wsb[:, c * D + oc * 512: c * D + (oc + 1) * 512],
                  [env.wsb, sel_sb]) for c in range(4)]
        env.mm_group(env.pl[1][0:32, :], env.pl[1], terms)
        k.emit("dve", lambda e, oc=oc: e.tensor_copy(out=BeT[0:32, oc * 512:(oc + 1) * 512], in_=env.pl[1][0:32, :]),
               reads=[env.pl[1]], writes=[BeT])

    env.load_actT(x1Tsc)
    for h in range(NTOK // 512):
        terms = [(AeT[:, e * 32:(e + 1) * 32], env.actT[:, e * NTOK + h * 512: e * NTOK + h * 512 + 512],
                  [AeT, env.actT]) for e in range(NE)]
        env.mm_group(env.pl[0][0:32, :], env.pl[0], terms)
        k.emit("dve", lambda e, h=h: e.tensor_copy(out=AxT[0:32, h * 512:(h + 1) * 512], in_=env.pl[0][0:32, :]),
               reads=[env.pl[0]], writes=[AxT])
        terms = [(AshT_sb[:, e * 32:(e + 1) * 32], env.actT[:, e * NTOK + h * 512: e * NTOK + h * 512 + 512],
                  [AshT_sb, env.actT]) for e in range(NE)]
        env.mm_group(env.pl[1][0:32, :], env.pl[1], terms)
        k.emit("dve", lambda e, h=h: e.tensor_copy(out=AxT[32:64, h * 512:(h + 1) * 512], in_=env.pl[1][0:32, :]),
               reads=[env.pl[1]], writes=[AxT])

    env.load_bias_row(fbias)
    def lora_terms(gt, d):
        return [(AxT[0:64, gt * 128:(gt + 1) * 128], BeT[0:64, d * 512:(d + 1) * 512], [AxT, BeT])]
    env.big_linear(kernW, x1sc, x2rows, x2T, add_bias=True, extra_terms=lora_terms, ln_eps=1e-5)
    k.merge_tokens(env.arena, [AeT, AshT_sb, BeT, AxT, sel_sb])
    return [k.out_token(x2rows), k.out_token(x2T)]


def build_FUSED():
    nc = bass.Bass("TRN2", target_bir_lowering=False)
    k = K(nc)
    io = dict(
        xA=k.dram_in("xA", [S, D], F32),
        WqkvT=k.dram_in("WqkvT", [D, 1536], F32),
        hp1=k.dram_in("hp1", [1024, 128], F32),
        hp2=k.dram_in("hp2", [1024, 128], F32),
        hb1=k.dram_in("hb1", [128, 16], F32),
        cosf=k.dram_in("cosf", [S, 128], F32),
        sinm=k.dram_in("sinm", [S, 128], F32),
        qrT=k.dram_tmp("qrT", [1024, S], BF16),
        qhT=k.dram_tmp("qhT", [1024, S], BF16),
        krT=k.dram_tmp("krT", [256, S], BF16),
        khT=k.dram_tmp("khT", [1024, S], BF16),
        v=k.dram_tmp("v", [2 * S, 128], BF16),
        WhT=k.dram_tmp("WhT", [D, 1536], BF16),
        WlT=k.dram_tmp("WlT", [D, 1536], BF16),
        aoT=k.dram_tmp("aoT", [1024, S], BF16),
        WoTloc=k.dram_in("WoTloc", [1024, D], F32),
        ypart=k.dram_tmp("ypart", [S, D], F32),
        ymy=k.dram_tmp("ymy", [NTOK, D], F32),
        xrows=k.dram_in("xrows", [NTOK, D], F32),
        kernW=k.dram_in("kernW", [D, D], F32),
        fbias=k.dram_in("fbias", [1, D], F32),
        LA=k.dram_in("LA", [512, D], F32),
        LBT=k.dram_in("LBT", [512, D], F32),
        SelAB=k.dram_in("SelAB", [512, 32], F32),
        AshT=k.dram_in("AshT", [D, 32], F32),
        BshT=k.dram_in("BshT", [32, D], F32),
        x1sc=k.dram_tmp("x1sc", [NTOK, D], F32),
        x1Tsc=k.dram_tmp("x1Tsc", [D, NTOK], BF16),
        x2rows=k.dram_tmp("x2rows", [NTOK, D], F32),
        x2T=k.dram_tmp("x2T", [D, NTOK], BF16),
        qkv_wT=k.dram_in("qkv_wT", [D, 3 * D], F32),
        ddtab=k.dram_in("ddtab", [32 * 256, 256], F32),
        qdtab=k.dram_in("qdtab", [32 * 128, 2], F32),
        kdtab=k.dram_in("kdtab", [32 * 128, 2], F32),
        attnp=k.dram_tmp("attnp", [NTOK, D], F32),
        kvloc=k.dram_tmp("kvloc", [D, 128], F32),
        cqTs=k.dram_tmp("cqTs", [D, NTOK], BF16),
        kvall=k.dram_tmp("kvall", [8 * D, 128], F32),
        qdbtab=k.dram_in("qdbtab", [32 * 128, 2 * NBLK], F32),
        Wkvb=k.dram_in("Wkvb", [32 * 128, 8], F32),
        attn_rows=k.dram_tmp("attn_rows", [NTOK, D], F32),
        nwcol=k.dram_in("nwcol", [128, 32], F32),
        gate_wT=k.dram_in("gate_wT", [D, D], F32),
        out_wT=k.dram_in("out_wT", [D, D], F32),
        W_baseT=k.dram_in("W_baseT", [D, D], F32),
        b_base=k.dram_in("b_base", [1, D], F32),
        encT=k.dram_in("encT", [D, 40], F32),
        CW0=k.dram_in("CW0", [3 * D, 32], F32),
        MW0=k.dram_in("MW0", [3 * D, 32], U8),
        CW1=k.dram_in("CW1", [3 * D, 8], F32),
        MW1=k.dram_in("MW1", [3 * D, 8], U8),
        Kmat=k.dram_in("Kmat", [D, D], F32),
        outrows=k.dram_out("outrows", [NTOK, D], F32),
        x3sc=k.dram_tmp("x3sc", [NTOK, D], F32),
        x3Tsc=k.dram_tmp("x3Tsc", [D, NTOK], BF16),
        y0Tsc=k.dram_tmp("y0Tsc", [D, NTOK], BF16),
        Hsc=k.dram_tmp("Hsc", [40, D], BF16),
    )
    import math
    HEADS = 32
    base = 1.0 / 2.0 ** (8.0 / HEADS)
    factor = 1.0 - 1 / (32 - 1 + 1e-05) + 1e-05
    bd = [math.exp(-(base ** (h + 1) * factor) * BS) for h in range(HEADS)]
    with nc.Block() as blk:
        env = Env(k)
        emit_stageA1(env, io)
        emit_stageA2(env, io)
        emit_stageW(env, io)
        k.emit("pool", lambda e: e.collective_compute(
            "ReduceScatter", OP.add, replica_groups=[[0, 1, 2, 3], [4, 5, 6, 7]],
            ins=[io["ypart"].ap().opt()], outs=[io["ymy"].ap().opt()]),
            reads=[io["ypart"]], writes=[io["ymy"]], dma=True)
        emit_lnrows(env, io)
        emit_stageB_fused(env, io)
        emit_stageC1(env, io, dict(bd=bd))
        k.emit("pool", lambda e: e.collective_compute(
            "AllGather", OP.bypass, replica_groups=[[0, 1, 2, 3, 4, 5, 6, 7]],
            ins=[io["kvloc"].ap().opt()], outs=[io["kvall"].ap().opt()]),
            reads=[io["kvloc"]], writes=[io["kvall"]], dma=True)
        emit_stageC2(env, io, dict())
        fw = emit_stageD(env, io)
        k.finalize(blk, final_waits=fw)
    k.close()
    return k



from concourse.bass_utils import run_bass_kernel_spmd
import math as _math


def _rope_tables():
    Dh = 128
    inv_freq = 1.0 / (10000.0 ** (np.arange(0, Dh, 2, dtype=np.float32) / Dh))
    t = np.arange(S, dtype=np.float32)
    freqs = np.outer(t, inv_freq)
    emb = np.concatenate([freqs, freqs], -1)
    cosf = np.cos(emb).astype(np.float32)
    sinf = np.sin(emb).astype(np.float32)
    sinm = sinf.copy()
    sinm[:, :64] *= -1.0
    return cosf, sinm


def _decay_tables():
    HEADS, BLK = 32, 256
    base = 1.0 / 2.0 ** (8.0 / HEADS)
    factor = 1.0 - 1 / (32 - 1 + 1e-05) + 1e-05
    slope = np.array([base ** (h + 1) * factor for h in range(HEADS)], np.float64)
    r = np.arange(BLK, dtype=np.float64) + 1.0
    qd = np.exp(-slope[:, None] * r[None, :]).astype(np.float32)
    kd = np.exp(-slope[:, None] * (BLK - r)[None, :]).astype(np.float32)
    diff = r[:, None] - r[None, :]
    dd = np.where(diff >= 0, np.exp(-slope[:, None, None] * diff[None, :, :]), 0.0).astype(np.float32)
    bd = np.exp(-slope * BLK)
    qdtab = np.zeros((32 * 128, 2), np.float32)
    kdtab = np.zeros((32 * 128, 2), np.float32)
    qdbtab = np.zeros((32 * 128, 8), np.float32)
    for h in range(32):
        for ss in range(2):
            qdtab[h * 128:(h + 1) * 128, ss] = qd[h, ss * 128:(ss + 1) * 128]
            kdtab[h * 128:(h + 1) * 128, ss] = kd[h, ss * 128:(ss + 1) * 128]
            for j in range(4):
                qdbtab[h * 128:(h + 1) * 128, j * 2 + ss] = qd[h, ss * 128:(ss + 1) * 128] * (bd[h] ** j)
    ddtab = np.zeros((32 * 256, 256), np.float32)
    for h in range(32):
        ddtab[h * 256:(h + 1) * 256, :] = dd[h].T
    return qdtab, kdtab, qdbtab, ddtab, bd


def _kmat():
    n = D
    Sm = np.zeros((n, n // 2 + 1), np.complex64)
    Sm[0, 0] = np.sqrt(n)
    ps = np.sqrt(n / 2.0)
    mf = (n - 1) // 2
    idx = np.arange(mf)
    Sm[2 * idx + 1, idx + 1] = ps
    Sm[2 * idx + 2, idx + 1] = -1j * ps
    Sm[n - 1, n // 2] = np.sqrt(n)
    return np.fft.irfft(Sm, n=n, axis=-1).astype(np.float32)


def kernel(**inp):
    C = np.ascontiguousarray
    x0 = np.asarray(inp["L0_attn_hidden_states"], np.float32)
    x0f = x0.reshape(2 * S, D)
    cosf, sinm = _rope_tables()
    qdtab, kdtab, qdbtab, ddtab, bd = _decay_tables()
    Kmat = _kmat()

    Wq, Wk, Wv, Wo = (np.asarray(inp[f"L0_attn_W{n}"], np.float32) for n in "qkvo")
    p1 = np.asarray(inp["L0_attn_hash_proj1"], np.float32)
    p2 = np.asarray(inp["L0_attn_hash_proj2"], np.float32)
    b1 = np.asarray(inp["L0_attn_hash_bias1"], np.float32)
    b2 = np.asarray(inp["L0_attn_hash_bias2"], np.float32)
    WqT, WkT, WvT = Wq.T, Wk.T, Wv.T
    WoT = C(Wo.T)

    gate = np.asarray(inp["L0_ffn_gate"], np.float32)
    lA = np.asarray(inp["L0_ffn_lora_A"], np.float32)
    lB = np.asarray(inp["L0_ffn_lora_B"], np.float32)
    lAs = np.asarray(inp["L0_ffn_lora_A_shared"], np.float32)
    lBs = np.asarray(inp["L0_ffn_lora_B_shared"], np.float32)
    kernW = C(np.asarray(inp["L0_ffn_kernel"], np.float32))
    fbias = np.asarray(inp["L0_ffn_bias"], np.float32).reshape(1, D)
    LA = C(lA.reshape(512, D))
    LBT = C(lB.transpose(0, 2, 1).reshape(512, D))
    AshT = C(lAs.T)
    BshT = C(lBs.T)

    qkv_wT = C(np.asarray(inp["L1_attn_qkv_w"], np.float32).T)
    nwcol = C(np.asarray(inp["L1_attn_norm_w"], np.float32).reshape(32, 128).T)
    gate_wT = C(np.asarray(inp["L1_attn_gate_w"], np.float32).T)
    out_wT = C(np.asarray(inp["L1_attn_out_w"], np.float32).T)
    W_baseT = C(np.asarray(inp["L1_ffn_W_base"], np.float32).T)
    b_base = np.asarray(inp["L1_ffn_b_base"], np.float32).reshape(1, D)
    encT = np.zeros((D, 40), np.float32)
    encT[:, 0:32] = np.asarray(inp["L1_ffn_enc0"], np.float32).T
    encT[:, 32:40] = np.asarray(inp["L1_ffn_enc1"], np.float32).T
    CW0 = C(np.asarray(inp["L1_ffn_coefs0"], np.float32).reshape(3 * D, 32))
    CW1 = C(np.asarray(inp["L1_ffn_coefs1"], np.float32).reshape(3 * D, 8))
    MW0 = C(np.asarray(inp["L1_ffn_mask0"]).reshape(3 * D, 32)).view(np.uint8)
    MW1 = C(np.asarray(inp["L1_ffn_mask1"]).reshape(3 * D, 8)).view(np.uint8)

    maps = []
    for c in range(8):
        b = c // 4
        pos = c % 4
        loc = pos * 1024
        hs = [8 * pos + i for i in range(8)]
        kvs = [2 * pos, 2 * pos + 1]
        WqkvT = C(np.concatenate(
            [WqT[:, h * 128:(h + 1) * 128] for h in hs]
            + [WkT[:, kv * 128:(kv + 1) * 128] for kv in kvs]
            + [WvT[:, kv * 128:(kv + 1) * 128] for kv in kvs], 1))
        hbc = np.zeros((128, 16), np.float32)
        for i, h in enumerate(hs):
            hbc[:, i] = b1[h, 0, :]
            hbc[:, 8 + i] = b2[h, 0, :]
        SelAB = np.zeros((512, 32), np.float32)
        for n_ in range(16):
            for r_ in range(32):
                SelAB[n_ * 32 + r_, r_] = gate[c, n_]
        Wkvb = np.zeros((32 * 128, 8), np.float32)
        for h in range(32):
            for kp in range(pos):
                Wkvb[h * 128:(h + 1) * 128, 4 * b + kp] = (bd[h] ** 4) ** (pos - 1 - kp)
        maps.append(dict(
            m1=dict(
                xA=C(x0[b]), WqkvT=WqkvT,
                hp1=C(np.concatenate([p1[h] for h in hs], 0)),
                hp2=C(np.concatenate([p2[h] for h in hs], 0)),
                hb1=hbc, cosf=cosf, sinm=sinm),
            m2=dict(
                xrows=C(x0f[c * 1024:(c + 1) * 1024]),
                WoT=WoT, kernW=kernW, fbias=fbias, LA=LA, LBT=LBT,
                SelAB=SelAB, AshT=AshT, BshT=BshT,
                qkv_wT=qkv_wT, ddtab=ddtab, qdtab=qdtab, kdtab=kdtab),
            m3=dict(
                qdbtab=qdbtab, Wkvb=Wkvb,
                nwcol=nwcol, gate_wT=gate_wT, out_wT=out_wT, W_baseT=W_baseT,
                b_base=b_base, encT=encT, CW0=CW0, MW0=MW0, CW1=CW1, MW1=MW1,
                Kmat=Kmat)))

    k1 = build_L1()
    r1 = run_bass_kernel_spmd(k1.nc, [m["m1"] for m in maps], core_ids=list(range(8))).results
    aoT_b = [np.concatenate([r1[4 * b + i]["aoT"] for i in range(4)], 0) for b in range(2)]

    maps2 = []
    for c in range(8):
        b, loc = c // 4, (c % 4) * 1024
        m = dict(maps[c]["m2"])
        m["aoTc"] = C(aoT_b[b][:, loc:loc + 1024])
        maps2.append(m)
    k2 = build_L2()
    r2 = run_bass_kernel_spmd(k2.nc, maps2, core_ids=list(range(8))).results

    kvall = np.concatenate([r2[c]["kvloc"] for c in range(8)], 0)
    maps3 = []
    for c in range(8):
        m = dict(maps[c]["m3"])
        m.update(kvall=kvall, attnp=r2[c]["attnp"], cqTs=r2[c]["cqTs"],
                 x2rows=r2[c]["x2rows"], x2T=r2[c]["x2T"])
        maps3.append(m)
    k3 = build_L3()
    r3 = run_bass_kernel_spmd(k3.nc, maps3, core_ids=list(range(8))).results

    out = np.concatenate([r3[c]["outrows"] for c in range(8)], 0)
    return out.reshape(2, S, D).astype(np.float32)
